# version 11
# baseline (speedup 1.0000x reference)
import numpy as np
import ml_dtypes
import concourse.bass as bass
import concourse.mybir as mybir
from concourse.bass_utils import run_bass_kernel_spmd
from contextlib import ExitStack

F32 = mybir.dt.float32
BF16 = mybir.dt.bfloat16
AF = mybir.ActivationFunctionType
ALU = mybir.AluOpType
AX = mybir.AxisListType
BF = ml_dtypes.bfloat16

B = 4
S = 4096
D = 1024
H = 16
DK = 64
HID = 2816
NEXP = 7
NCORES = 8
EPS = 1e-5


def shapes():
    NC = S // 128
    NOWN = NC // 2
    SOWN = NOWN * 128
    G = min(4, NOWN)
    NB = NOWN // G
    QB = G * 128
    NHC = HID // 128
    return NC, NOWN, SOWN, G, NB, QB, NHC


def gammas_np():
    lin = np.linspace(np.log(1.0 / 512.0), np.log(1.0 / 32.0), H)
    return 1.0 - np.exp(lin)


def packed_positions(parity):
    NC, NOWN, SOWN, G, NB, QB, NHC = shapes()
    own = np.concatenate([np.arange((parity + 2 * i) * 128, (parity + 2 * i) * 128 + 128)
                          for i in range(NOWN)])
    sib = np.concatenate([np.arange((1 - parity + 2 * i) * 128,
                                    (1 - parity + 2 * i) * 128 + 128)
                          for i in range(NOWN)])
    return np.concatenate([own, sib])


def host_constants(parity):
    NC, NOWN, SOWN, G, NB, QB, NHC = shapes()
    c = {}
    pos = np.arange(S, dtype=np.float64)
    inv = 10000.0 ** (-(np.arange(0, DK, 2, dtype=np.float64) / DK))
    ang = pos[:, None] * inv[None, :]
    cosr = np.repeat(np.cos(ang), 2, axis=-1).T
    sinr = np.repeat(np.sin(ang), 2, axis=-1).T
    pk = packed_positions(parity)
    c["cost"] = np.ascontiguousarray(np.tile(cosr, (2, 1))[:, pk]).astype(np.float32)
    c["sint"] = np.ascontiguousarray(np.tile(sinr, (2, 1))[:, pk]).astype(np.float32)
    P = np.zeros((128, 128), np.float32)
    for i in range(64):
        P[2 * i, 2 * i + 1] = -1.0
        P[2 * i + 1, 2 * i] = 1.0
    c["perm"] = np.ascontiguousarray(P.T)
    c["idn"] = np.eye(128, dtype=np.float32)
    c["ones"] = np.ones((128, 128), np.float32)
    dm = np.zeros((128, (2 * G) * QB), np.float32)
    ki = np.arange(128)[:, None]
    qi = np.arange(128)[None, :]
    for part in range(2):
        for r in range(G):
            for i in range(G):
                blk = dm[:, (part * G + r) * QB + i * 128:
                         (part * G + r) * QB + (i + 1) * 128]
                if part == 0:
                    if r < i:
                        blk[:] = 1.0
                    elif r == i:
                        blk[:] = (qi >= ki).astype(np.float32)
                else:
                    keep = (r < i) if parity == 0 else (r <= i)
                    if keep:
                        blk[:] = 1.0
    c["dmask"] = dm
    g = gammas_np()
    j = np.arange(QB)
    qoff = (256 * (j // 128) + (j % 128)).astype(np.float64)
    c["qfac"] = np.exp(np.log(g)[:, None] * qoff[None, :]).reshape(1, H * QB).astype(np.float32)
    KB = min(512, S)
    kio = np.arange(128, dtype=np.float64)
    own_e = np.tile(-kio, KB // 128)
    sib_e = np.tile(-kio + (128.0 * (2 * parity - 1)), KB // 128)
    c["kfct"] = np.exp(np.log(g)[:, None]
                       * np.concatenate([own_e, sib_e])[None, :]
                       ).reshape(1, H * 2 * KB).astype(np.float32)
    return c


def decay_imm(h, m, c, G):
    g = gammas_np()[h]
    v = np.exp(np.log(g) * 256.0 * (G * m - c))
    v32 = np.float32(v)
    if not np.isfinite(v32):
        v32 = np.float32(0.0)
    return float(v32)


class DQ:

    def __init__(self, E, name, n=4):
        self.E = E
        self.name = name
        self.n = n
        self.counts = [0] * n
        self.idx = 0
        for i in range(n):
            E.sem(f"{name}{i}")

    def issue(self, eng, out, in_):
        k = self.idx % self.n
        self.idx += 1
        sem = self.E.sem(f"{self.name}{k}")
        if self.counts[k] > 0:
            eng.wait_ge(sem, 16 * self.counts[k])
        eng.dma_start(out=out, in_=in_).then_inc(sem, 16)
        self.counts[k] += 1
        return (k, self.counts[k])

    def issue_indirect(self, eng, out, in_, in_offset):
        k = self.idx % self.n
        self.idx += 1
        sem = self.E.sem(f"{self.name}{k}")
        if self.counts[k] > 0:
            eng.wait_ge(sem, 16 * self.counts[k])
        eng.indirect_dma_start(out=out, out_offset=None, in_=in_,
                               in_offset=in_offset).then_inc(sem, 16)
        self.counts[k] += 1
        return (k, self.counts[k])

    def wait(self, eng, token):
        k, c = token
        eng.wait_ge(self.E.sem(f"{self.name}{k}"), 16 * c)

    def wait_all(self, eng):
        for k in range(self.n):
            if self.counts[k]:
                eng.wait_ge(self.E.sem(f"{self.name}{k}"), 16 * self.counts[k])


class Env:
    def __init__(self, nc, ctx, block):
        self.nc = nc
        self.ctx = ctx
        self.block = block
        self.sems = {}
        self.counts = {}
        self.barn = 0
        self.bufs = {}
        self._qS = None
        self._qG = None

    @property
    def qS(self):
        if self._qS is None:
            self._qS = DQ(self, "qS", 4)
        return self._qS

    @property
    def qG(self):
        if self._qG is None:
            self._qG = DQ(self, "qG", 4)
        return self._qG

    def sem(self, name):
        if name not in self.sems:
            self.sems[name] = self.ctx.enter_context(self.nc.semaphore(name))
            self.counts[name] = 0
        return self.sems[name]

    def inc(self, instr, name, k=1):
        instr.then_inc(self.sem(name), k)
        self.counts[name] += k
        return self.counts[name]

    def n(self, name):
        self.sem(name)
        return self.counts[name]

    def sb(self, name, shape, dtype, alias=None, side=None):
        if side is None:
            t = self.ctx.enter_context(self.nc.sbuf_tensor(name, shape, dtype))
        else:
            t = self.ctx.enter_context(self.nc.sbuf_tensor(name, shape, dtype, side=side))
        self.bufs[alias or name] = t
        return t

    def ps(self, name, shape, dtype=F32, alias=None):
        t = self.ctx.enter_context(self.nc.psum_tensor(name, shape, dtype))
        self.bufs[alias or name] = t
        return t

    def tinc(self, ins):
        return self.inc(ins, "eT", 1)

    def vinc(self, ins):
        return self.inc(ins, "eV", 1)

    def sinc(self, ins):
        return self.inc(ins, "eS", 1)

    def ginc(self, ins):
        return self.inc(ins, "eG", 1)

    def barrier(self):
        self.barn += 1
        bar = self.sem("bar")

        @self.block.sync
        def _(e):
            if self._qS is not None:
                self._qS.wait_all(e)
            e.nop().then_inc(bar, 1)

        @self.block.gpsimd
        def _(e):
            if self._qG is not None:
                self._qG.wait_all(e)
            e.nop().then_inc(bar, 1)

        for engf in (self.block.tensor, self.block.vector, self.block.scalar):
            @engf
            def _(e):
                e.nop().then_inc(bar, 1)
        tgt = 5 * self.barn
        self.counts["bar"] = tgt
        for engf in (self.block.tensor, self.block.vector, self.block.scalar,
                     self.block.gpsimd, self.block.sync):
            @engf
            def _(e):
                e.wait_ge(bar, tgt)


def r3(ap, c):
    return ap.rearrange("p (c w) -> p c w", c=c)


def emit_load_constants(E, want=("idn", "ones")):
    nc = E.nc
    g = None

    @E.block.gpsimd
    def _(gp):
        for nm in want:
            src = E.dram[nm]
            sh = list(src.shape)
            if nm == "idn":
                E.qG.issue(gp, E.sb("idnf", [128, 128], F32)[:], src)
                E.qG.issue(gp, E.sb("idnb", [128, 128], BF16)[:], src)
            elif nm == "ones":
                E.qG.issue(gp, E.sb("onesf", [128, 128], F32)[:], src)
                E.qG.issue(gp, E.sb("onesb", [128, 128], BF16)[:], src)
            elif nm == "perm":
                E.qG.issue(gp, E.sb("permb", [128, 128], BF16)[:], src)
            elif nm in ("cost", "sint"):
                E.qG.issue(gp, E.sb(nm + "b", sh, BF16)[:], src)
            elif nm == "dmask":
                E.qG.issue(gp, E.sb("dmaskb", sh, BF16)[:], src)
            elif nm in ("qfac", "kfct"):
                E.qG.issue(gp, E.sb(nm + "f", sh, F32)[:], src)


def alloc_ln_bufs(E, fp32_keep_n=0):
    E.sb("lnx0", [128, D], F32)
    E.sb("lnx1", [128, D], F32)
    E.sb("lnbns", [128, 24], F32)
    E.sb("lnmv", [128, 4], F32)
    E.sb("lnrstd", [128, 2], F32)
    E.sb("lnnbias", [128, 2], F32)
    E.sb("lnxn", [128, 2 * D], BF16)
    E.ps("lnptr0", [128, 1024], BF16)
    E.ps("lnptr1", [128, 1024], BF16)
    if fp32_keep_n:
        E.sb("xnf32", [128, fp32_keep_n * D], F32)


def emit_ln(E, chunks, hT_W, src=None, fp32_keep=False, pre_wait=()):
    assert not fp32_keep
    """LN chunks into feature-major bf16 hT (and optionally fp32 residual /
    fp32 normalized copies). chunks: list of (src_row|None, resid_slot|None,
    hT_col)."""
    nc = E.nc
    hT = E.bufs["hT"]
    resid = E.bufs.get("resid")
    xt = [E.bufs["lnx0"], E.bufs["lnx1"]]
    bns, mv = E.bufs["lnbns"], E.bufs["lnmv"]
    rstd, nbias, xn = E.bufs["lnrstd"], E.bufs["lnnbias"], E.bufs["lnxn"]
    ptr = [E.bufs["lnptr0"], E.bufs["lnptr1"]]
    ld, sv, ss, tt, ev = "lnld", "lnsv", "lnss", "lntt", "lnev"
    base = {k: E.n(k) for k in (ld, sv, ss, tt, ev)}

    indirect = isinstance(src, tuple) and src and src[0] == "indirect"

    def xsrc_of(i):
        row, slot, _hc = chunks[i]
        if slot is not None and not indirect:
            return resid[:, slot * D:(slot + 1) * D]
        if row is None and slot is not None:
            return resid[:, slot * D:(slot + 1) * D]
        return xt[i % 2][:]

    tokens = {}
    if src is not None and isinstance(src, tuple) and src[0] == "indirect":
        _, gsrc, gidx_sb = src

        @E.block.gpsimd
        def _(g):
            import bass_rust
            for i, (row, slot, _hc) in enumerate(chunks):
                if slot is None and i >= 2:
                    g.wait_ge(E.sem(sv), base[sv] + 5 * (i - 1))
                tokens[i] = E.qG.issue_indirect(
                    g, xsrc_of(i), gsrc,
                    bass_rust.IndirectOffsetOnAxis(gidx_sb[:, i:i + 1], 0))
        qwait = E.qG
    elif src is not None:
        @E.block.sync
        def _(sy):
            for i, (row, slot, _hc) in enumerate(chunks):
                if row is None:
                    continue
                if slot is None and i >= 2:
                    sy.wait_ge(E.sem(sv), base[sv] + 5 * (i - 1))
                tokens[i] = E.qS.issue(sy, xsrc_of(i), src[row:row + 128, :])
        qwait = E.qS
    else:
        qwait = E.qS

    n = len(chunks)

    @E.block.vector
    def _(v):
        for s_ in pre_wait:
            v.wait_ge(E.sem(s_), E.n(s_))

        def emit_evac(i):
            j = i % 2
            hc = chunks[i][2]
            v.wait_ge(E.sem(tt), base[tt] + i + 1)
            E.counts[ev] += 1
            v.tensor_copy(r3(hT[:], 8)[:, :, hc:hc + 128],
                          r3(ptr[j][:], 8)).then_inc(E.sem(ev), 1)

        for i, (row, slot, _hc) in enumerate(chunks):
            j = i % 2
            if i in tokens:
                qwait.wait(v, tokens[i])
            x = xsrc_of(i)
            v.bn_stats(bns[:, j * 12:j * 12 + 6], x[:, 0:512])
            E.counts[sv] += 1
            v.bn_stats(bns[:, j * 12 + 6:j * 12 + 12], x[:, 512:1024]).then_inc(E.sem(sv), 1)
            v.wait_ge(E.sem(sv), E.counts[sv])
            E.counts[sv] += 1
            v.bn_aggr(mv[:, j * 2:j * 2 + 2], bns[:, j * 12:j * 12 + 12]).then_inc(E.sem(sv), 1)
            v.wait_ge(E.sem(sv), E.counts[sv])
            E.counts[sv] += 1
            v.tensor_scalar_add(mv[:, j * 2 + 1:j * 2 + 2],
                                mv[:, j * 2 + 1:j * 2 + 2], EPS).then_inc(E.sem(sv), 1)
            v.wait_ge(E.sem(ss), base[ss] + 2 * i + 2)
            E.counts[sv] += 1
            v.scalar_tensor_tensor(nbias[:, j:j + 1], mv[:, j * 2:j * 2 + 1], -1.0,
                                   rstd[:, j:j + 1], ALU.mult, ALU.mult).then_inc(E.sem(sv), 1)
            v.wait_ge(E.sem(sv), E.counts[sv])
            if i >= 2:
                v.wait_ge(E.sem(tt), base[tt] + (i - 2) + 1)
            E.counts[sv] += 1
            v.scalar_tensor_tensor(xn[:, j * D:(j + 1) * D], x, rstd[:, j:j + 1],
                                   nbias[:, j:j + 1].broadcast_to((128, D)),
                                   ALU.mult, ALU.add).then_inc(E.sem(sv), 1)
            if i >= 1:
                emit_evac(i - 1)
        emit_evac(n - 1)

    @E.block.scalar
    def _(sc):
        for i, (row, slot, _hc) in enumerate(chunks):
            j = i % 2
            sc.wait_ge(E.sem(sv), base[sv] + 5 * i + 3)
            E.counts[ss] += 1
            sc.activation(rstd[:, j:j + 1], mv[:, j * 2 + 1:j * 2 + 2],
                          AF.Ln).then_inc(E.sem(ss), 1)
            sc.wait_ge(E.sem(ss), E.counts[ss])
            E.counts[ss] += 1
            sc.activation(rstd[:, j:j + 1], rstd[:, j:j + 1], AF.Exp,
                          scale=-0.5).then_inc(E.sem(ss), 1)

    @E.block.tensor
    def _(te):
        for i in range(n):
            j = i % 2
            te.wait_ge(E.sem(sv), base[sv] + 5 * i + 5)
            if i >= 2:
                te.wait_ge(E.sem(ev), base[ev] + (i - 2) + 1)
            E.counts[tt] += 1
            for k in range(8):
                ins = te.transpose(ptr[j][:, k * 128:(k + 1) * 128],
                                   xn[:, j * D + k * 128: j * D + (k + 1) * 128],
                                   E.bufs["idnb"][:])
                if k == 7:
                    ins.then_inc(E.sem(tt), 1)


def emit_zT_to_resid(E, zsrc_psum, zs_buf, oc, chunk_idxs, mcols_w, scale_ap=None,
                     ps_o=None, pfx="z", w_col=None):
    resid = E.bufs["resid"]
    n = len(chunk_idxs)
    smm, sev, str_, sad = pfx + "mm", pfx + "ev", pfx + "tr", pfx + "ad"
    it = E.n(sev)

    @E.block.vector
    def _(v):
        v.wait_ge(E.sem(smm), it + 1)
        if it >= 1:
            v.wait_ge(E.sem(str_), it)
        if scale_ap is not None:
            ins = v.tensor_scalar(zs_buf[:, 0:mcols_w], zsrc_psum[:, 0:mcols_w],
                                  scale_ap, None, ALU.mult)
        else:
            ins = v.tensor_copy(zs_buf[:, 0:mcols_w], zsrc_psum[:, 0:mcols_w])
        E.counts[sev] += 1
        ins.then_inc(E.sem(sev), 1)

    @E.block.tensor
    def _(te):
        te.wait_ge(E.sem(sev), it + 1)
        if it >= 1:
            te.wait_ge(E.sem(sad), it)
        for k in range(n):
            ins = te.transpose(ps_o[:, k * 128:(k + 1) * 128],
                               zs_buf[:, k * 128:(k + 1) * 128], E.bufs["idnf"][:])
        E.counts[str_] += 1
        ins.then_inc(E.sem(str_), 1)

    @E.block.vector
    def _(v):
        v.wait_ge(E.sem(str_), it + 1)
        for k, ci in enumerate(chunk_idxs):
            dst = resid[:, ci * D + oc * 128: ci * D + oc * 128 + 128]
            if w_col is None:
                ins = v.tensor_add(dst, dst, ps_o[:, k * 128:(k + 1) * 128])
            else:
                ins = v.scalar_tensor_tensor(dst, ps_o[:, k * 128:(k + 1) * 128],
                                             w_col[k], dst, ALU.mult, ALU.add)
        E.counts[sad] += 1
        ins.then_inc(E.sem(sad), 1)


def emit_stage2(E):
    nc = E.nc
    NC, NOWN, SOWN, G, NB, QB, NHC = shapes()
    wf = E.sb("s2wf", [128, 8 * D], F32)
    wqT = E.sb("s2wqT", [128, 8 * D], BF16)
    wqc = E.sb("s2wqc", [128, D], BF16)
    wtm = E.sb("s2wtm", [128, D], BF16)
    asum = E.sb("s2asum", [128, 8], F32)
    tot = E.sb("s2tot", [1, 2], F32)
    sca = E.sb("s2sca", [128, 3], F32)
    zs = E.sb("s2zs", [128, QB], F32)
    pst = E.ps("s2pst", [128, 1024], BF16)
    psz = [E.ps("s2psz0", [128, QB], F32), E.ps("s2psz1", [128, QB], F32)]
    pso = E.ps("s2pso", [128, QB], F32)
    pss = psz[0][0:1, 0:8]
    psr = pso[:, 0:1]
    sv, mm, ss = "s2sv", "s2mm", "s2ss"
    for k in (sv, mm, ss, "s2zmm", "s2zev", "s2ztr", "s2zad"):
        E.sem(k)

    tok_wf = [None]

    @E.block.sync
    def _(sy):
        tok_wf[0] = E.qS.issue(sy, r3(wf[:], 8),
                               E.dram["w0"].rearrange("(c p) n -> p c n", p=128))

    @E.block.vector
    def _(v):
        E.qS.wait(v, tok_wf[0])
        for c in range(8):
            ins = v.reduce_sum(asum[:, c:c + 1], wf[:, c * D:(c + 1) * D], AX.X,
                               apply_absolute_value=True)
        E.inc(ins, sv, 1)

    @E.block.tensor
    def _(te):
        te.wait_ge(E.sem(sv), E.n(sv))
        E.inc(te.matmul(pss[:], E.bufs["onesf"][:, 0:1], asum[:], start=True, stop=True),
              mm, 1)

    @E.block.vector
    def _(v):
        v.wait_ge(E.sem(mm), E.n(mm))
        E.inc(v.reduce_sum(tot[:, 0:1], pss[:], AX.X), sv, 1)
        v.wait_ge(E.sem(sv), E.n(sv))
        E.inc(v.tensor_scalar_mul(tot[:, 0:1], tot[:, 0:1], 1.0 / (D * D)), sv, 1)
        v.wait_ge(E.sem(sv), E.n(sv))
        E.inc(v.tensor_scalar_max(tot[:, 0:1], tot[:, 0:1], EPS), sv, 1)

    @E.block.tensor
    def _(te):
        te.wait_ge(E.sem(sv), E.n(sv))
        E.inc(te.matmul(psr[:], E.bufs["onesf"][0:1, :], tot[:, 0:1], start=True, stop=True),
              mm, 1)

    @E.block.vector
    def _(v):
        v.wait_ge(E.sem(mm), E.n(mm))
        E.inc(v.tensor_copy(sca[:, 0:1], psr[:]), sv, 1)
        v.wait_ge(E.sem(sv), E.n(sv))
        v.tensor_scalar_mul(sca[:, 1:2], sca[:, 0:1], 0.5)
        E.inc(v.tensor_scalar_mul(sca[:, 2:3], sca[:, 0:1], -0.5), sv, 1)

    for c in range(8):
        @E.block.vector
        def _(v, c=c):
            v.wait_ge(E.sem(sv), E.n(sv))
            if c >= 1:
                v.wait_ge(E.sem(mm), E.n(mm))
            v.tensor_scalar(wqc[:], wf[:, c * D:(c + 1) * D], sca[:, 1:2], None, ALU.is_gt)
            E.inc(v.tensor_scalar(wtm[:], wf[:, c * D:(c + 1) * D], sca[:, 2:3], None, ALU.is_lt), sv, 1)
            v.wait_ge(E.sem(sv), E.n(sv))
            E.inc(v.tensor_sub(wqc[:], wqc[:], wtm[:]), sv, 1)

        @E.block.tensor
        def _(te, c=c):
            te.wait_ge(E.sem(sv), E.n(sv))
            if c >= 1:
                te.wait_ge(E.sem(ss), E.n(ss))
            for k in range(8):
                ins = te.transpose(pst[:, k * 128:(k + 1) * 128],
                                   wqc[:, k * 128:(k + 1) * 128], E.bufs["idnb"][:])
            E.inc(ins, mm, 1)

        @E.block.vector
        def _(v, c=c):
            v.wait_ge(E.sem(mm), E.n(mm))
            E.inc(v.tensor_copy(r3(wqT[:], 8)[:, :, c * 128:(c + 1) * 128],
                                r3(pst[:], 8)), ss, 1)

    E.barrier()

    nlnev = E.n("lnev")
    for oc in range(8):
        for m in range(NB):
            it = oc * NB + m
            jj = it % 2
            @E.block.tensor
            def _(te):
                if it == 0 and nlnev:
                    te.wait_ge(E.sem("lnev"), nlnev)
                if it >= 2:
                    te.wait_ge(E.sem("s2zev"), it - 1)
                for ic in range(8):
                    ins = te.matmul(psz[jj][:],
                                    wqT[:, ic * D + oc * 128: ic * D + oc * 128 + 128],
                                    r3(E.bufs["hT"][:], 8)[:, ic, m * QB:(m + 1) * QB],
                                    start=(ic == 0), stop=(ic == 7))
                E.inc(ins, "s2zmm", 1)

            emit_zT_to_resid(E, psz[jj], zs, oc, [G * m + k for k in range(G)], QB,
                             scale_ap=sca[:, 0:1], ps_o=pso, pfx="s2z")
    E.barrier()


def emit_stage1(E):
    nc = E.nc
    NC, NOWN, SOWN, G, NB, QB, NHC = shapes()
    KB = min(512, S)
    NKB = S // KB
    hT = E.bufs["hT"]
    attnT = E.bufs["attnT"]
    kT = [E.bufs["s1kT0"], E.bufs["s1kT1"]]
    qT = [E.bufs["s1qT0"], E.bufs["s1qT1"]]
    vv = [E.bufs["s1vv0"], E.bufs["s1vv1"]]
    ws = E.bufs["s1ws"]
    qa = [E.bufs["s1qa0"], E.bufs["s1qa1"]]
    t1, t2 = E.bufs["s1t1"], E.bufs["s1t2"]
    sqb = E.bufs["s1sqb"]
    k2row = E.bufs["s1k2row"]
    mxb = E.bufs["s1mxb"]
    ptile = [E.bufs["s1pt0"], E.bufs["s1pt1"]]
    atmp = E.bufs["s1atmp"]
    sinv = E.bufs["s1sinv"]
    pp = [E.bufs["ps_a"], E.bufs["ps_b"]]
    pj = pp
    pr = [E.bufs["ps_c"], E.bufs["ps_d"]]
    ptv = E.bufs["ps_e"]
    po = E.bufs["ps_f"]
    prep = E.bufs["ps_c"]
    pq2 = E.bufs["ps_c"][0:1, 0:QB]
    pk2 = E.bufs["ps_d"][0:1, 0:KB]
    projfree = {}
    cosb, sinb = E.bufs["costb"], E.bufs["sintb"]
    cosqb, sinqb = cosb, sinb
    dmaskb = E.bufs["dmaskb"]
    SC = 0.35355339059327373

    @E.block.vector
    def _(v):
        v.memset(kT[0][64:65, :], -1.0)
        v.memset(kT[1][64:65, :], -1.0)
        v.memset(r3(vv[0][:], NC)[:, :, 64:65], 1.0)
        E.vinc(v.memset(r3(vv[1][:], NC)[:, :, 64:65], 1.0))

    wslot = 3 * 8 * 128
    wtok = {}
    projdone = {}
    mq2free = {}

    def w_ap(hp, kind):
        return r3(ws[:], 2 * 3 * 8)[:, (hp % 2) * 24 + kind * 8:(hp % 2) * 24 + kind * 8 + 8, :]

    for hp in range(8):
        @E.block.gpsimd
        def _(g):
            if hp >= 2:
                g.wait_ge(E.sem("eT"), projdone[hp - 2])
            for kind, nm in enumerate(("wq", "wk", "wv")):
                wtok[(hp, kind)] = E.qG.issue(
                    g, w_ap(hp, kind),
                    E.dram[nm].rearrange("(c p) n -> p c n", p=128)[:, :, hp * 128:(hp + 1) * 128])

        jobs = ([("k", gb) for gb in range(NKB)] + [("v", gb) for gb in range(NKB)]
                + [("q", m) for m in range(NB)])
        nJ = len(jobs)
        base_T = E.n("eT")
        base_V = E.n("eV")
        evac_cnt = [0] * nJ
        tail_cnt = [0] * nJ
        vc = base_V
        for bi, (kind, blk) in enumerate(jobs):
            vc += 1
            evac_cnt[bi] = vc
            if bi >= 1:
                vc += 2 if jobs[bi - 1][0] in ("k", "q") else 1
                tail_cnt[bi - 1] = vc
        vc += 2 if jobs[nJ - 1][0] in ("k", "q") else 1
        tail_cnt[nJ - 1] = vc
        proj_cnt = [0] * nJ
        rot_cnt = [0] * nJ
        tc = base_T
        for bi in range(nJ):
            tc += 1
            proj_cnt[bi] = tc
            if bi >= 1:
                tc += 1
                rot_cnt[bi - 1] = tc
        tc += 1
        rot_cnt[nJ - 1] = tc

        @E.block.tensor
        def _(te):
            def emit_tailT(bi):
                kind, blk = jobs[bi]
                j = bi % 2
                w = QB if kind == "q" else KB
                te.wait_ge(E.sem("eV"), evac_cnt[bi])
                if kind == "v":
                    if bi >= 1:
                        te.wait_ge(E.sem("eV"), tail_cnt[bi - 1])
                    for kq in range(w // 128):
                        ins = te.transpose(ptv[:, kq * 128:(kq + 1) * 128],
                                           qa[j][:, kq * 128:(kq + 1) * 128],
                                           E.bufs["idnb"][:])
                else:
                    if bi >= 2:
                        te.wait_ge(E.sem("eV"), tail_cnt[bi - 2])
                    ins = te.matmul(pr[j][:, 0:w], E.bufs["permb"][:], qa[j][:, 0:w],
                                    start=True, stop=True)
                E.tinc(ins)

            for bi, (kind, blk) in enumerate(jobs):
                j = bi % 2
                kindi = {"q": 0, "k": 1, "v": 2}[kind]
                w = QB if kind == "q" else KB
                if bi == 0:
                    if hp == 0:
                        te.wait_ge(E.sem("lnev"), E.n("lnev"))
                        te.wait_ge(E.sem("eV"), E.n("eV"))
                    for kk in range(3):
                        E.qG.wait(te, wtok[(hp, kk)])
                if bi >= 2:
                    te.wait_ge(E.sem("eV"), evac_cnt[bi - 2])
                elif hp > 0:
                    te.wait_ge(E.sem("eV"), base_V)
                for ic in range(8):
                    cols = blk * KB if kind != "q" else blk * QB
                    ins = te.matmul(pj[j][:, 0:w], w_ap(hp, kindi)[:, ic, :],
                                    r3(hT[:], 8)[:, ic, cols:cols + w],
                                    start=(ic == 0), stop=(ic == 7))
                E.tinc(ins)
                assert E.n("eT") == proj_cnt[bi]
                if bi >= 1:
                    emit_tailT(bi - 1)
            emit_tailT(nJ - 1)
            assert E.n("eT") == rot_cnt[nJ - 1]
        projdone[hp] = E.n("eT")

        base_G = E.n("eG")
        t2_cnt = {}
        gcv = base_G
        for bi, (kind, blk) in enumerate(jobs):
            if kind in ("q", "k"):
                gcv += 1
                t2_cnt[bi] = gcv

        @E.block.gpsimd
        def _(g):
            for bi, (kind, blk) in enumerate(jobs):
                if kind not in ("q", "k"):
                    continue
                j = bi % 2
                w = QB if kind == "q" else KB
                cs = cosqb if kind == "q" else cosb
                cols = blk * (QB if kind == "q" else KB)
                g.wait_ge(E.sem("eV"), evac_cnt[bi])
                if bi >= 2:
                    g.wait_ge(E.sem("eV"), tail_cnt[bi - 2])
                E.ginc(g.tensor_tensor(t2[:, j * KB:j * KB + w], qa[j][:, 0:w],
                                       cs[:, cols:cols + w], ALU.mult))

        @E.block.vector
        def _(v):
            def emit_tailV(bi):
                kind, blk = jobs[bi]
                j = bi % 2
                w = QB if kind == "q" else KB
                v.wait_ge(E.sem("eT"), rot_cnt[bi])
                if kind == "v":
                    g0 = blk * (w // 128)
                    nch = w // 128
                    v.tensor_copy(r3(vv[0][:], NC)[:, g0:g0 + nch, 0:64],
                                  r3(ptv[:, 0:w], nch)[:, :, 0:64])
                    ins = v.tensor_copy(r3(vv[1][:], NC)[:, g0:g0 + nch, 0:64],
                                        r3(ptv[:, 0:w], nch)[:, :, 64:128])
                    E.vinc(ins)
                else:
                    if kind == "q":
                        sn = sinqb
                        cols = blk * QB
                        dst0, dst1 = qT[0], qT[1]
                    else:
                        sn = sinb
                        cols = blk * KB
                        dst0, dst1 = kT[0], kT[1]
                    E.vinc(v.tensor_tensor(t1[:, 0:w], pr[j][:, 0:w],
                                           sn[:, cols:cols + w], ALU.mult))
                    v.wait_ge(E.sem("eV"), E.n("eV"))
                    v.wait_ge(E.sem("eG"), t2_cnt[bi])
                    v.tensor_tensor(dst0[0:64, cols:cols + w],
                                    t2[0:64, j * KB:j * KB + w],
                                    t1[0:64, 0:w], ALU.add)
                    E.vinc(v.tensor_tensor(dst1[0:64, cols:cols + w],
                                           t2[64:128, j * KB:j * KB + w],
                                           t1[64:128, 0:w], ALU.add))
                assert E.n("eV") == tail_cnt[bi]

            for bi, (kind, blk) in enumerate(jobs):
                j = bi % 2
                w = QB if kind == "q" else KB
                v.wait_ge(E.sem("eT"), proj_cnt[bi])
                if kind in ("q", "k"):
                    ins = v.tensor_scalar(qa[j][:, 0:w], pp[j][:, 0:w], SC, None,
                                          ALU.mult)
                else:
                    ins = v.tensor_copy(qa[j][:, 0:w], pp[j][:, 0:w])
                E.vinc(ins)
                assert E.n("eV") == evac_cnt[bi]
                if bi >= 1:
                    emit_tailV(bi - 1)
            emit_tailV(nJ - 1)
        projfree[hp] = E.n("eV")

        for x in range(2):
            kTx, qTx, vvx = kT[x], qT[x], vv[x]
            base_T = E.n("eT")
            base_V = E.n("eV")
            @E.block.vector
            def _(v):
                for gb in range(NKB):
                    if gb == 0:
                        v.wait_ge(E.sem("eV"), E.n("eV"))
                    else:
                        v.wait_ge(E.sem("eT"), base_T + gb)
                    E.vinc(v.tensor_mul(sqb[:, 0:KB], kTx[0:64, gb * KB:(gb + 1) * KB],
                                        kTx[0:64, gb * KB:(gb + 1) * KB]))
                    v.wait_ge(E.sem("eT"), base_T + gb + 1)
                    E.vinc(v.tensor_copy(k2row[0:1, gb * KB:(gb + 1) * KB], pk2))
                v.wait_ge(E.sem("eV"), E.n("eV"))
                E.vinc(v.reduce_max(mxb[:], k2row[:], AX.X))
                nmax = E.n("eV")
                for m in range(NB):
                    v.wait_ge(E.sem("eT"), base_T + NKB + m)
                    E.vinc(v.tensor_mul(sqb[:, 0:QB], qTx[0:64, m * QB:(m + 1) * QB],
                                        qTx[0:64, m * QB:(m + 1) * QB]))
                    v.wait_ge(E.sem("eT"), base_T + NKB + m + 1)
                    v.wait_ge(E.sem("eV"), nmax)
                    E.vinc(v.tensor_scalar(qTx[64:65, m * QB:(m + 1) * QB], pq2,
                                           mxb[0:1, 0:1], 0.5, ALU.add, ALU.mult))

            @E.block.tensor
            def _(te):
                for gb in range(NKB):
                    te.wait_ge(E.sem("eV"), base_V + 2 * gb + 1)
                    E.tinc(te.matmul(pk2, E.bufs["onesb"][0:64, 0:1], sqb[:, 0:KB],
                                     start=True, stop=True))
                for m in range(NB):
                    te.wait_ge(E.sem("eV"), base_V + 2 * NKB + 1 + 2 * m + 1)
                    E.tinc(te.matmul(pq2, E.bufs["onesb"][0:64, 0:1], sqb[:, 0:QB],
                                     start=True, stop=True))

            if x == 1:
                mq2free[hp] = E.n("eV")

            base_T = E.n("eT")
            base_S = E.n("eS")
            base_V = E.n("eV")
            seqs = []
            for m in range(NB):
                seq = [(part, c) for part in range(2) for c in range(G * m + G)]
                seqs.append(seq)
            score_cnt = {}
            acc_cnt = {}
            rep_cnt = {}
            exp_cnt = {}
            mask_cnt = {}
            sig_cnt = {}
            atmp_cnt = {}
            store_cnt = {}
            tc = base_T
            scv = base_S
            vcv = base_V
            for m in range(NB):
                for si, (part, c) in enumerate(seqs[m]):
                    tc += 1
                    score_cnt[(m, si)] = tc
                    tc += 1
                    acc_cnt[(m, si)] = tc
                    scv += 1
                    exp_cnt[(m, si)] = scv
                    if c >= G * m:
                        vcv += 1
                        mask_cnt[(m, si)] = vcv
                tc += 1
                rep_cnt[m] = tc
                vcv += 1
                sig_cnt[m] = vcv
                vcv += 1
                atmp_cnt[m] = vcv
                vcv += 1
                store_cnt[m] = vcv

            @E.block.tensor
            def _(te):
                for m in range(NB):
                    seq = seqs[m]
                    for si, (part, c) in enumerate(seq):
                        jp = part * NOWN + c
                        jj = si % 2
                        r = (part * G + (c - G * m)) if c >= G * m else -1
                        if si == 0:
                            te.wait_ge(E.sem("eV"),
                                       store_cnt[m - 1] if m > 0 else base_V)
                            if m == 0:
                                te.wait_ge(E.sem("eS"), base_S)
                        if si >= 2:
                            te.wait_ge(E.sem("eS"), exp_cnt[(m, si - 2)])
                        E.tinc(te.matmul(pp[jj][:, 0:QB], kTx[:, jp * 128:(jp + 1) * 128],
                                         qTx[:, m * QB:(m + 1) * QB], start=True, stop=True))
                        if r >= 0:
                            te.wait_ge(E.sem("eV"), mask_cnt[(m, si)])
                        else:
                            te.wait_ge(E.sem("eS"), exp_cnt[(m, si)])
                        E.tinc(te.matmul(po[:, 0:QB], r3(vvx[:], NC)[:, jp, :],
                                         ptile[jj][:, 0:QB], start=(si == 0),
                                         stop=(si == len(seq) - 1)))
                    te.wait_ge(E.sem("eV"), sig_cnt[m])
                    E.tinc(te.matmul(prep[:, 0:QB], E.bufs["onesf"][0:1, :],
                                     sinv[:, 0:QB], start=True, stop=True))
                    assert E.n("eT") == rep_cnt[m]

            @E.block.scalar
            def _(sc):
                for m in range(NB):
                    for si in range(len(seqs[m])):
                        sc.wait_ge(E.sem("eT"), score_cnt[(m, si)])
                        E.sinc(sc.activation(ptile[si % 2][:, 0:QB],
                                             pp[si % 2][:, 0:QB], AF.Exp))

            @E.block.vector
            def _(v):
                for m in range(NB):
                    seq = seqs[m]
                    for si, (part, c) in enumerate(seq):
                        r = (part * G + (c - G * m)) if c >= G * m else -1
                        if r >= 0:
                            v.wait_ge(E.sem("eS"), exp_cnt[(m, si)])
                            E.vinc(v.tensor_mul(ptile[si % 2][:, 0:QB],
                                                ptile[si % 2][:, 0:QB],
                                                dmaskb[:, r * QB:(r + 1) * QB]))
                    v.wait_ge(E.sem("eT"), acc_cnt[(m, len(seq) - 1)])
                    E.vinc(v.reciprocal(sinv[:, 0:QB], po[64:65, 0:QB]))
                    assert E.n("eV") == sig_cnt[m]
                    E.vinc(v.tensor_copy(atmp[:, 0:QB], po[0:64, 0:QB]))
                    v.wait_ge(E.sem("eV"), E.n("eV"))
                    v.wait_ge(E.sem("eT"), rep_cnt[m])
                    h = 2 * hp + x
                    dst = attnT[(h % 2) * 64:(h % 2) * 64 + 64,
                                (h // 2) * SOWN + m * QB:(h // 2) * SOWN + (m + 1) * QB]
                    E.vinc(v.tensor_tensor(dst, atmp[:, 0:QB], prep[0:64, 0:QB],
                                           ALU.mult))
                    assert E.n("eV") == store_cnt[m]


def emit_stage1_wo(E):
    nc = E.nc
    NC, NOWN, SOWN, G, NB, QB, NHC = shapes()
    attnT = E.bufs["attnT"]
    wo = E.sb("s1wo", [128, 8 * D], BF16)
    zs = E.sb("s1zs", [128, QB], F32)
    psz = [E.ps("s1psz0", [128, QB], F32), E.ps("s1psz1", [128, QB], F32)]
    pso = E.ps("s1pso", [128, QB], F32)
    for k in ("s1zmm", "s1zev", "s1ztr", "s1zad"):
        E.sem(k)

    @E.block.gpsimd
    def _(g):
        E.qG.issue(g, r3(wo[:], 8), E.dram["wo"].rearrange("(c p) n -> p c n", p=128))

    @E.block.sync
    def _(sy):
        for i in range(NOWN):
            E.qS.issue(sy, E.bufs["resid"][:, i * D:(i + 1) * D],
                       E.dram["x"][i * 128:(i + 1) * 128, :])
    E.barrier()

    for oc in range(8):
        for m in range(NB):
            it = oc * NB + m
            jj = it % 2

            @E.block.tensor
            def _(te):
                if it == 0:
                    te.wait_ge(E.sem("eV"), E.n("eV"))
                if it >= 2:
                    te.wait_ge(E.sem("s1zev"), it - 1)
                for ic in range(8):
                    ins = te.matmul(psz[jj][:],
                                    wo[:, ic * D + oc * 128: ic * D + oc * 128 + 128],
                                    r3(attnT[:], 8)[:, ic, m * QB:(m + 1) * QB],
                                    start=(ic == 0), stop=(ic == 7))
                E.inc(ins, "s1zmm", 1)

            emit_zT_to_resid(E, psz[jj], zs, oc, [G * m + k for k in range(G)], QB,
                             ps_o=pso, pfx="s1z")
    E.barrier()


def alloc_stage1_bufs(E):
    NC, NOWN, SOWN, G, NB, QB, NHC = shapes()
    KB = min(512, S)
    E.sb("s1kT0", [65, S], BF16)
    E.sb("s1kT1", [65, S], BF16)
    E.sb("s1qT0", [65, SOWN], BF16)
    E.sb("s1qT1", [65, SOWN], BF16)
    E.sb("s1vv0", [128, NC * 65], BF16)
    E.sb("s1vv1", [128, NC * 65], BF16)
    E.sb("s1ws", [128, 2 * 3 * 8 * 128], BF16)
    E.sb("s1qa0", [128, KB], BF16)
    E.sb("s1qa1", [128, KB], BF16)
    E.sb("s1t1", [128, KB], BF16)
    E.sb("s1t2", [128, 2 * KB], BF16)
    E.sb("s1sqb", [64, KB], BF16)
    E.sb("s1k2row", [1, S], F32)
    E.sb("s1mxb", [1, 1], F32)
    E.sb("s1q2s", [1, QB], F32)
    E.sb("s1pt0", [128, QB], BF16)
    E.sb("s1pt1", [128, QB], BF16)
    E.sb("s1atmp", [64, QB], BF16)
    E.sb("s1sinv", [1, QB], F32)
    E.ps("s1psa", [128, KB], F32, alias="ps_a")
    E.ps("s1psb", [128, KB], F32, alias="ps_b")
    E.ps("s1psc", [128, KB], F32, alias="ps_c")
    E.ps("s1psd", [128, KB], F32, alias="ps_d")
    E.ps("s1pse", [128, KB], BF16, alias="ps_e")
    E.ps("s1psf", [65, QB], F32, alias="ps_f")


def build_prog1():
    NC, NOWN, SOWN, G, NB, QB, NHC = shapes()
    specs = [("x", (S, D), F32, "ExternalInput"),
             ("wq", (D, D), F32, "ExternalInput"),
             ("wk", (D, D), F32, "ExternalInput"),
             ("wv", (D, D), F32, "ExternalInput"),
             ("wo", (D, D), F32, "ExternalInput"),
             ("x1", (SOWN, D), F32, "ExternalOutput")]
    specs += const_specs(("idn", "ones", "perm", "cost", "sint", "dmask"))
    nc, dram = new_prog(specs)
    with ExitStack() as ctx:
        block = ctx.enter_context(nc.Block())
        E = Env(nc, ctx, block)
        E.dram = dram
        E.sb("attnT", [128, 8 * SOWN], BF16)
        alloc_ln_bufs(E)
        emit_load_constants(E, want=("idn", "ones", "perm", "cost", "sint", "dmask"))
        with ExitStack() as s1ctx:
            Einner = E
            old_ctx = E.ctx
            E.ctx = s1ctx
            E.sb("hT", [128, 8 * S], BF16)
            alloc_stage1_bufs(E)
            E.ctx = old_ctx
            E.barrier()
            emit_ln(E, [(g * 128, None, g * 128) for g in range(NC)], S,
                    src=dram["x"])
            emit_stage1(E)
            E.barrier()
        E.sb("resid", [128, NOWN * D], F32)
        emit_stage1_wo(E)
        emit_writeback(E, "x1", waits=("s1zad",))
    return nc


def alloc_stage3_bufs(E):
    NC, NOWN, SOWN, G, NB, QB, NHC = shapes()
    KB = min(512, S)
    E.sb("s3rwq", [128, H * 64], BF16)
    E.sb("s3rwk", [128, H * 64], BF16)
    E.sb("s3rwv", [128, H * 64], BF16)
    E.sb("s3gns", [64, H], F32)
    E.sb("s3gnb", [64, H], F32)
    E.sb("s3kT", [64, S], BF16)
    E.sb("s3qT", [64, SOWN], BF16)
    E.sb("s3vv", [128, NC * 65], BF16)
    E.sb("s3repq", [64, QB], BF16)
    E.sb("s3repk0", [64, KB], BF16)
    E.sb("s3repk1", [64, KB], BF16)
    E.sb("s3qfac", [1, 2 * QB], BF16)
    E.sb("s3kfct", [1, 2 * 2 * KB], BF16)
    E.sb("s3pt0", [128, QB], BF16)
    E.sb("s3pt1", [128, QB], BF16)
    E.sb("s3rets", [64, QB], F32)
    E.sb("s3sq", [64, QB], F32)
    E.sb("s3mrow", [1, QB], F32)
    E.sb("s3vtmp", [1, QB], F32)
    E.sb("s3rstd", [1, QB], F32)
    E.ps("s3psf", [65, QB], F32, alias="ps_f")
    E.ps("s3ppa", [128, min(512, S)], F32, alias="ps_pa")
    E.ps("s3ppb", [128, min(512, S)], F32, alias="ps_pb")


def emit_stage3(E):
    nc = E.nc
    NC, NOWN, SOWN, G, NB, QB, NHC = shapes()
    KB = min(512, S)
    NKB = S // KB
    hT = E.bufs["hT"]
    kT, qT, vv = E.bufs["s3kT"], E.bufs["s3qT"], E.bufs["s3vv"]
    repq, repk = E.bufs["s3repq"], [E.bufs["s3repk0"], E.bufs["s3repk1"]]
    ptile = [E.bufs["s3pt0"], E.bufs["s3pt1"]]
    rets, sq = E.bufs["s3rets"], E.bufs["s3sq"]
    mrow, vtmp, rstd = E.bufs["s3mrow"], E.bufs["s3vtmp"], E.bufs["s3rstd"]
    normT = E.bufs["s3norm"]
    pA, pB, po = E.bufs["ps_a"], E.bufs["ps_b"], E.bufs["ps_f"]
    pPa, pPb = E.bufs["ps_pa"], E.bufs["ps_pb"]
    ps2 = pB[0:1, 0:QB]
    prm = pA[0:64, 0:QB]
    prs = pB[64:128, 0:QB]
    projfree = {}
    qfac, kfct = E.bufs["s3qfac"], E.bufs["s3kfct"]
    gns, gnb = E.bufs["s3gns"], E.bufs["s3gnb"]
    ftok = {}

    def hs(h, cols, w):
        return r3(hT[:], 8)[(h % 2) * 64:(h % 2) * 64 + 64, h // 2, cols:cols + w]

    for h in range(H):
        sl = h % 2
        if h == 0:
            @E.block.gpsimd
            def _(g):
                ftok[(0, "q")] = E.qG.issue(g, qfac[0:1, 0:QB],
                                            E.dram["qfac"][0:1, 0:QB])
                ftok[(0, "k")] = E.qG.issue(g, kfct[0:1, 0:2 * KB],
                                            E.dram["kfct"][0:1, 0:2 * KB])

        jobs = ([("r", i) for i in range(3)] + [("k", gb) for gb in range(NKB)]
                + [("q", m) for m in range(NB)] + [("v", g) for g in range(NC)])
        nJ = len(jobs)
        base_T = E.n("eT")
        base_V = E.n("eV")
        mm_cnt = [base_T + 1 + i for i in range(nJ)]
        ev_cnt = [0] * nJ
        vc = base_V
        for bi, (kind, blk) in enumerate(jobs):
            vc += 2 if kind == "v" else 1
            ev_cnt[bi] = vc

        if h + 1 < H:
            @E.block.gpsimd
            def _(g):
                if h >= 1:
                    g.wait_ge(E.sem("eT"), mm_cnt[2])
                nsl = (h + 1) % 2
                ftok[(h + 1, "q")] = E.qG.issue(
                    g, qfac[0:1, nsl * QB:(nsl + 1) * QB],
                    E.dram["qfac"][0:1, (h + 1) * QB:(h + 2) * QB])
                ftok[(h + 1, "k")] = E.qG.issue(
                    g, kfct[0:1, nsl * 2 * KB:(nsl + 1) * 2 * KB],
                    E.dram["kfct"][0:1, (h + 1) * 2 * KB:(h + 2) * 2 * KB])

        @E.block.tensor
        def _(te):
            for bi, (kind, blk) in enumerate(jobs):
                jj = bi % 2
                pX = pPa if jj == 0 else pPb
                if bi == 0:
                    E.qG.wait(te, ftok[(h, "q")])
                    E.qG.wait(te, ftok[(h, "k")])
                    if h == 0:
                        te.wait_ge(E.sem("lnev"), E.n("lnev"))
                        E.qG.wait_all(te)
                if bi <= 1 and h > 0:
                    te.wait_ge(E.sem("eV"), projfree[h - 1])
                if bi >= 2:
                    te.wait_ge(E.sem("eV"), ev_cnt[bi - 2])
                b0 = (h % 2) * 64
                if kind == "r":
                    srcap = (qfac[0:1, sl * QB:sl * QB + QB] if blk == 0 else
                             kfct[0:1, sl * 2 * KB + (blk - 1) * KB:
                                  sl * 2 * KB + blk * KB])
                    w = QB if blk == 0 else KB
                    ins = te.matmul(pX[0:64, 0:w], E.bufs["onesb"][0:1, 0:64], srcap,
                                    start=True, stop=True)
                elif kind == "k":
                    ins = te.matmul(pX[0:64, 0:KB],
                                    E.bufs["s3rwk"][b0:b0 + 64, h * 64:(h + 1) * 64],
                                    hs(h, blk * KB, KB), start=True, stop=True)
                elif kind == "q":
                    ins = te.matmul(pX[0:64, 0:QB],
                                    E.bufs["s3rwq"][b0:b0 + 64, h * 64:(h + 1) * 64],
                                    hs(h, blk * QB, QB), start=True, stop=True)
                else:
                    ins = te.matmul(pX[0:128, 0:64], hs(h, blk * 128, 128),
                                    E.bufs["s3rwv"][b0:b0 + 64, h * 64:(h + 1) * 64],
                                    start=True, stop=True)
                E.tinc(ins)
                assert E.n("eT") == mm_cnt[bi]

        @E.block.vector
        def _(v):
            for bi, (kind, blk) in enumerate(jobs):
                jj = bi % 2
                pX = pPa if jj == 0 else pPb
                v.wait_ge(E.sem("eT"), mm_cnt[bi])
                if kind == "r":
                    dst = repq if blk == 0 else repk[blk - 1]
                    w = QB if blk == 0 else KB
                    ins = v.tensor_copy(dst[:, 0:w], pX[0:64, 0:w])
                elif kind == "k":
                    part = 0 if blk * KB < SOWN else 1
                    ins = v.tensor_tensor(kT[:, blk * KB:(blk + 1) * KB], pX[0:64, 0:KB],
                                          repk[part][:, 0:KB], ALU.mult)
                elif kind == "q":
                    ins = v.tensor_tensor(qT[:, blk * QB:(blk + 1) * QB], pX[0:64, 0:QB],
                                          repq[:, 0:QB], ALU.mult)
                else:
                    E.vinc(v.tensor_copy(r3(vv[:], NC)[:, blk, 0:64], pX[0:128, 0:64]))
                    with nc.allow_low_precision(reason="v_aug rowsum feeds bf16 matmul"):
                        ins = v.reduce_sum(r3(vv[:], NC)[:, blk, 64:65], pX[0:128, 0:64],
                                           AX.X)
                E.vinc(ins)
                assert E.n("eV") == ev_cnt[bi]
        projfree[h] = E.n("eV")

        base_T = E.n("eT")
        base_S = E.n("eS")
        base_V = E.n("eV")
        seqs = [[(part, c) for part in range(2) for c in range(G * m + G)]
                for m in range(NB)]
        sT_cnt = {}
        acc_cnt = {}
        sums_cnt = {}
        repm_cnt = {}
        reps_cnt = {}
        evp_cnt = {}
        mask_cnt = {}
        retc_cnt = {}
        sqm_cnt = {}
        mrow_cnt = {}
        var1_cnt = {}
        var2_cnt = {}
        eps_cnt = {}
        sub_cnt = {}
        mul_cnt = {}
        gn_cnt = {}
        ln_cnt = {}
        exp_cnt = {}
        tc, scv, vcv = base_T, base_S, base_V
        for m in range(NB):
            for si, (part, c) in enumerate(seqs[m]):
                tc += 1
                sT_cnt[(m, si)] = tc
                tc += 1
                acc_cnt[(m, si)] = tc
                vcv += 1
                evp_cnt[(m, si)] = vcv
                if c >= G * m:
                    vcv += 1
                    mask_cnt[(m, si)] = vcv
            vcv += 1
            retc_cnt[m] = vcv
            vcv += 1
            sqm_cnt[m] = vcv
            vcv += 1
            mrow_cnt[m] = vcv
            tc += 1
            sums_cnt[m] = tc
            vcv += 1
            var1_cnt[m] = vcv
            vcv += 1
            var2_cnt[m] = vcv
            vcv += 1
            eps_cnt[m] = vcv
            scv += 1
            ln_cnt[m] = scv
            scv += 1
            exp_cnt[m] = scv
            tc += 1
            repm_cnt[m] = tc
            tc += 1
            reps_cnt[m] = tc
            vcv += 1
            sub_cnt[m] = vcv
            vcv += 1
            mul_cnt[m] = vcv
            vcv += 1
            gn_cnt[m] = vcv

        @E.block.tensor
        def _(te):
            for m in range(NB):
                seq = seqs[m]
                for si, (part, c) in enumerate(seq):
                    jp = part * NOWN + c
                    jj = si % 2
                    r = (part * G + (c - G * m)) if c >= G * m else -1
                    pS = pA if jj == 0 else pB
                    if si <= 1:
                        te.wait_ge(E.sem("eV"), gn_cnt[m - 1] if m > 0 else base_V)
                    if si >= 2:
                        te.wait_ge(E.sem("eV"), evp_cnt[(m, si - 2)])
                    E.tinc(te.matmul(pS[:, 0:QB], kT[:, jp * 128:(jp + 1) * 128],
                                     qT[:, m * QB:(m + 1) * QB], start=True, stop=True))
                    te.wait_ge(E.sem("eV"),
                               mask_cnt[(m, si)] if r >= 0 else evp_cnt[(m, si)])
                    E.tinc(te.matmul(po[:, 0:QB], r3(vv[:], NC)[:, jp, :],
                                     ptile[jj][:, 0:QB], start=(si == 0),
                                     stop=(si == len(seq) - 1)))
                te.wait_ge(E.sem("eV"), sqm_cnt[m])
                E.tinc(te.matmul(ps2, E.bufs["onesf"][0:64, 0:1], sq[:, 0:QB],
                                 start=True, stop=True))
                te.wait_ge(E.sem("eV"), mrow_cnt[m])
                E.tinc(te.matmul(prm, E.bufs["onesf"][0:1, 0:64], mrow[:, 0:QB],
                                 start=True, stop=True))
                te.wait_ge(E.sem("eS"), exp_cnt[m])
                E.tinc(te.matmul(prs, E.bufs["onesf"][0:1, 0:64], rstd[:, 0:QB],
                                 start=True, stop=True))
                assert E.n("eT") == reps_cnt[m]

        @E.block.scalar
        def _(sc):
            for m in range(NB):
                sc.wait_ge(E.sem("eV"), eps_cnt[m])
                E.sinc(sc.activation(rstd[:, 0:QB], vtmp[:, 0:QB], AF.Ln))
                sc.wait_ge(E.sem("eS"), E.n("eS"))
                E.sinc(sc.activation(rstd[:, 0:QB], rstd[:, 0:QB], AF.Exp, scale=-0.5))

        @E.block.vector
        def _(v):
            for m in range(NB):
                seq = seqs[m]
                for si, (part, c) in enumerate(seq):
                    jj = si % 2
                    r = (part * G + (c - G * m)) if c >= G * m else -1
                    pS = pA if jj == 0 else pB
                    v.wait_ge(E.sem("eT"), sT_cnt[(m, si)])
                    if si >= 2:
                        pass
                    if si >= 2:
                        v.wait_ge(E.sem("eT"), acc_cnt[(m, si - 2)])
                    E.vinc(v.tensor_scalar(ptile[jj][:, 0:QB], pS[:, 0:QB],
                                           decay_imm(h, m, c, G), None, ALU.mult))
                    if r >= 0:
                        v.wait_ge(E.sem("eV"), E.n("eV"))
                        E.vinc(v.tensor_mul(ptile[jj][:, 0:QB], ptile[jj][:, 0:QB],
                                            E.bufs["dmaskb"][:, r * QB:(r + 1) * QB]))
                v.wait_ge(E.sem("eT"), acc_cnt[(m, len(seq) - 1)])
                E.vinc(v.tensor_copy(rets[:, 0:QB], po[0:64, 0:QB]))
                v.wait_ge(E.sem("eV"), E.n("eV"))
                E.vinc(v.tensor_mul(sq[:, 0:QB], rets[:, 0:QB], rets[:, 0:QB]))
                E.vinc(v.tensor_scalar(mrow[:, 0:QB], po[64:65, 0:QB], 1.0 / 64,
                                       None, ALU.mult))
                v.wait_ge(E.sem("eV"), E.n("eV"))
                E.vinc(v.scalar_tensor_tensor(vtmp[:, 0:QB], mrow[:, 0:QB], -1.0,
                                              mrow[:, 0:QB], ALU.mult, ALU.mult))
                v.wait_ge(E.sem("eT"), sums_cnt[m])
                v.wait_ge(E.sem("eV"), E.n("eV"))
                E.vinc(v.scalar_tensor_tensor(vtmp[:, 0:QB], ps2, 1.0 / 64,
                                              vtmp[:, 0:QB], ALU.mult, ALU.add))
                v.wait_ge(E.sem("eV"), E.n("eV"))
                E.vinc(v.tensor_scalar_add(vtmp[:, 0:QB], vtmp[:, 0:QB], EPS))
                assert E.n("eV") == eps_cnt[m]
                v.wait_ge(E.sem("eT"), repm_cnt[m])
                E.vinc(v.tensor_sub(rets[:, 0:QB], rets[:, 0:QB], prm))
                v.wait_ge(E.sem("eT"), reps_cnt[m])
                v.wait_ge(E.sem("eV"), E.n("eV"))
                E.vinc(v.tensor_mul(rets[:, 0:QB], rets[:, 0:QB], prs))
                v.wait_ge(E.sem("eV"), E.n("eV"))
                dst = normT[(h % 2) * 64:(h % 2) * 64 + 64,
                            (h // 2) * SOWN + m * QB:(h // 2) * SOWN + (m + 1) * QB]
                E.vinc(v.scalar_tensor_tensor(
                    dst, rets[:, 0:QB], gns[:, h:h + 1],
                    gnb[:, h:h + 1].broadcast_to((64, QB)),
                    ALU.mult, ALU.add))
                assert E.n("eV") == gn_cnt[m]


def emit_stage3_out(E):
    nc = E.nc
    NC, NOWN, SOWN, G, NB, QB, NHC = shapes()
    normT = E.bufs["s3norm"]
    w1 = E.sb("s3w12", [128, 8 * D], BF16)
    w2 = w1
    gsil = [E.sb("s3gsil", [128, QB], BF16), E.sb("s3gsil1", [128, QB], BF16)]
    psg = [E.bufs["ps_a"], E.bufs["ps_b"]]
    zs = E.sb("s3zs", [128, QB], F32)
    pso = E.ps("s3pso", [128, QB], F32)
    for k in ("s3zmm", "s3zev", "s3ztr", "s3zad"):
        E.sem(k)

    @E.block.gpsimd
    def _(g):
        E.qG.issue(g, r3(w1[:], 8), E.dram["w1"].rearrange("(c p) n -> p c n", p=128))
    E.barrier()

    nIT = 8 * NB
    base_T = E.n("eT")
    base_S = E.n("eS")
    base_V = E.n("eV")

    @E.block.tensor
    def _(te):
        for oc in range(8):
            for m in range(NB):
                it = oc * NB + m
                jj = it % 2
                if it >= 2:
                    te.wait_ge(E.sem("eV"), base_V + it - 1)
                for ic in range(8):
                    ins = te.matmul(psg[jj][:, 0:QB],
                                    w1[:, ic * D + oc * 128: ic * D + oc * 128 + 128],
                                    r3(E.bufs["hT"][:], 8)[:, ic, m * QB:(m + 1) * QB],
                                    start=(ic == 0), stop=(ic == 7))
                E.tinc(ins)

    @E.block.scalar
    def _(sc):
        for it in range(nIT):
            jj = it % 2
            sc.wait_ge(E.sem("eT"), base_T + it + 1)
            if it >= 2:
                sc.wait_ge(E.sem("eV"), base_V + it - 1)
            E.sinc(sc.activation(gsil[jj][:, 0:QB], psg[jj][:, 0:QB], AF.Silu))

    @E.block.vector
    def _(v):
        for oc in range(8):
            for m in range(NB):
                it = oc * NB + m
                jj = it % 2
                v.wait_ge(E.sem("eS"), base_S + it + 1)
                dst = normT[:, oc * SOWN + m * QB: oc * SOWN + (m + 1) * QB]
                E.vinc(v.tensor_add(dst, dst, gsil[jj][:, 0:QB]))
    E.barrier()

    @E.block.gpsimd
    def _(g):
        E.qG.issue(g, r3(w2[:], 8), E.dram["w2"].rearrange("(c p) n -> p c n", p=128))
    E.barrier()

    for oc in range(8):
        for m in range(NB):
            it = oc * NB + m
            jj = it % 2

            @E.block.tensor
            def _(te):
                if it == 0:
                    te.wait_ge(E.sem("eV"), E.n("eV"))
                if it >= 2:
                    te.wait_ge(E.sem("s3zev"), it - 1)
                for ic in range(8):
                    ins = te.matmul(psg[jj][:, 0:QB],
                                    w2[:, ic * D + oc * 128: ic * D + oc * 128 + 128],
                                    r3(normT[:], 8)[:, ic, m * QB:(m + 1) * QB],
                                    start=(ic == 0), stop=(ic == 7))
                E.inc(ins, "s3zmm", 1)

            emit_zT_to_resid(E, psg[jj], zs, oc, [G * m + k for k in range(G)], QB,
                             ps_o=pso, pfx="s3z")
    E.barrier()


def build_prog3():
    NC, NOWN, SOWN, G, NB, QB, NHC = shapes()
    specs = [("x2f", (S, D), F32, "ExternalInput"),
             ("rwq", (128, H * 64), F32, "ExternalInput"),
             ("rwk", (128, H * 64), F32, "ExternalInput"),
             ("rwv", (128, H * 64), F32, "ExternalInput"),
             ("w1", (D, D), F32, "ExternalInput"),
             ("w2", (D, D), F32, "ExternalInput"),
             ("gns", (64, H), F32, "ExternalInput"),
             ("gnb", (64, H), F32, "ExternalInput"),
             ("x3", (SOWN, D), F32, "ExternalOutput")]
    specs += const_specs(("idn", "ones", "dmask", "qfac", "kfct"))
    nc, dram = new_prog(specs)
    with ExitStack() as ctx:
        block = ctx.enter_context(nc.Block())
        E = Env(nc, ctx, block)
        E.dram = dram
        emit_load_constants(E, want=("idn", "ones", "dmask"))
        E.sb("hT", [128, 8 * S], BF16)
        E.sb("s3norm", [128, 8 * (S // 2)], BF16)
        E.ps("s3psa", [128, min(512, S)], F32, alias="ps_a")
        E.ps("s3psb", [128, min(512, S)], F32, alias="ps_b")
        with ExitStack() as s3ctx:
            octx = E.ctx
            E.ctx = s3ctx
            with ExitStack() as lnctx:
                E.ctx = lnctx
                alloc_ln_bufs(E)
                E.ctx = s3ctx
                E.barrier()
                emit_ln(E, [(g * 128, None, g * 128) for g in range(NC)],
                        S, src=dram["x2f"])
                E.barrier()
            alloc_stage3_bufs(E)
            E.ctx = octx

            @E.block.gpsimd
            def _(g):
                for nm in ("rwq", "rwk", "rwv"):
                    E.qG.issue(g, E.bufs["s3" + nm][:], E.dram[nm])
                E.qG.issue(g, E.bufs["s3gns"][:], E.dram["gns"])
                E.qG.issue(g, E.bufs["s3gnb"][:], E.dram["gnb"])
            emit_stage3(E)
            E.barrier()
        E.sb("resid", [128, NOWN * D], F32)

        @E.block.sync
        def _(sy):
            sy.wait_ge(E.sem("eS"), E.n("eS"))
            sy.wait_ge(E.sem("eV"), E.n("eV"))
            sy.wait_ge(E.sem("eT"), E.n("eT"))
            for i in range(NOWN):
                E.qS.issue(sy, E.bufs["resid"][:, i * D:(i + 1) * D],
                           E.dram["x2f"][i * 128:(i + 1) * 128, :])
        E.barrier()
        emit_stage3_out(E)
        emit_writeback(E, "x3", waits=("s3zad",))
    return nc


def alloc_stage4_bufs(E):
    NC, NOWN, SOWN, G, NB, QB, NHC = shapes()
    E.sb("s4xf", [128, D], F32)
    E.sb("s4xfT", [128, 8 * 128], F32)
    E.sb("s4wr", [128, 8 * 8], F32)
    E.sb("s4rb", [128, 8], F32)
    E.sb("s4probs", [128, 8], F32)
    E.sb("s4t8", [128, 8], F32)
    E.sb("s4rec", [128, 1], F32)
    E.sb("s4msk", [128, 8], F32)
    E.sb("s4wbuf", [128, NOWN * 8], F32)
    E.sb("s4bns", [128, 12], F32)
    E.sb("s4mv", [128, 2], F32)
    E.sb("s4rstd", [128, 1], F32)
    E.sb("s4nb", [128, 1], F32)
    E.sb("s4wgu", [128, 2 * 2 * 8 * 128], BF16)
    E.sb("s4wd", [128, 2 * NHC * 128], BF16)
    E.sb("s4gs0", [128, QB], BF16)
    E.sb("s4gs1", [128, QB], BF16)
    E.sb("s4zsb", [128, QB], BF16)
    E.sb("s4zsb1", [128, QB], BF16)
    E.sb("s4gu", [128, NHC * (SOWN if NB == 1 else SOWN // 2)], BF16)
    E.ps("s4psa", [128, QB], F32, alias="ps_a")
    E.ps("s4psb", [128, QB], F32, alias="ps_b")
    E.ps("s4pz0", [128, max(QB, 512)], F32)
    E.ps("s4pz1", [128, QB], F32)
    E.ps("s4ptrp", [128, QB], BF16)


def emit_stage4_router(E):
    nc = E.nc
    NC, NOWN, SOWN, G, NB, QB, NHC = shapes()
    resid = E.bufs["resid"]
    xf, xfT = E.bufs["s4xf"], E.bufs["s4xfT"]
    wr, rb = E.bufs["s4wr"], E.bufs["s4rb"]
    probs, t8, rec, msk = E.bufs["s4probs"], E.bufs["s4t8"], E.bufs["s4rec"], E.bufs["s4msk"]
    wbuf = E.bufs["s4wbuf"]
    bns, mv, rstd, nb = E.bufs["s4bns"], E.bufs["s4mv"], E.bufs["s4rstd"], E.bufs["s4nb"]
    plog = E.bufs["s4pz1"][:, 0:8]
    pxf = E.bufs["s4pz0"][:, 0:512]

    rtok = {}

    @E.block.gpsimd
    def _(g):
        rtok[0] = E.qG.issue(g, r3(wr[:], 8),
                             E.dram["wr"].rearrange("(c p) n -> p c n", p=128))

    @E.block.sync
    def _(sy):
        E.qS.issue(sy, rb[0:1, 0:7], E.dram["rbias"])

    @E.block.tensor
    def _(te):
        E.qS.wait_all(te)
        E.tinc(te.matmul(plog[:, 0:7], E.bufs["onesf"][0:1, :], rb[0:1, 0:7],
                         start=True, stop=True))

    @E.block.vector
    def _(v):
        v.wait_ge(E.sem("eT"), E.n("eT"))
        E.vinc(v.tensor_copy(rb[:, 0:7], plog[:, 0:7]))
        E.vinc(v.memset(probs[:, 7:8], -1e30))

    for i in range(NOWN):
        x = resid[:, i * D:(i + 1) * D]

        @E.block.vector
        def _(v):
            if i >= 1:
                v.wait_ge(E.sem("eS"), E.n("eS"))
                v.wait_ge(E.sem("eT"), E.n("eT"))
            v.bn_stats(bns[:, 0:6], x[:, 0:512])
            E.vinc(v.bn_stats(bns[:, 6:12], x[:, 512:1024]))
            v.wait_ge(E.sem("eV"), E.n("eV"))
            E.vinc(v.bn_aggr(mv[:], bns[:]))
            v.wait_ge(E.sem("eV"), E.n("eV"))
            E.vinc(v.tensor_scalar_add(mv[:, 1:2], mv[:, 1:2], EPS))

        @E.block.scalar
        def _(sc):
            sc.wait_ge(E.sem("eV"), E.n("eV"))
            E.sinc(sc.activation(rstd[:], mv[:, 1:2], AF.Ln))
            sc.wait_ge(E.sem("eS"), E.n("eS"))
            E.sinc(sc.activation(rstd[:], rstd[:], AF.Exp, scale=-0.5))

        @E.block.vector
        def _(v):
            v.wait_ge(E.sem("eS"), E.n("eS"))
            E.vinc(v.scalar_tensor_tensor(nb[:], mv[:, 0:1], -1.0, rstd[:],
                                          ALU.mult, ALU.mult))

        @E.block.vector
        def _(v):
            v.wait_ge(E.sem("eV"), E.n("eV"))
            E.vinc(v.scalar_tensor_tensor(xf[:], x, rstd[:],
                                          nb[:, 0:1].broadcast_to((128, D)),
                                          ALU.mult, ALU.add))

        for half in range(2):
            @E.block.tensor
            def _(te):
                te.wait_ge(E.sem("eV"), E.n("eV"))
                for k in range(4):
                    ins = te.transpose(pxf[:, k * 128:(k + 1) * 128],
                                       xf[:, half * 512 + k * 128: half * 512 + (k + 1) * 128],
                                       E.bufs["idnf"][:])
                E.tinc(ins)

            @E.block.vector
            def _(v):
                v.wait_ge(E.sem("eT"), E.n("eT"))
                E.vinc(v.tensor_copy(xfT[:, half * 512:(half + 1) * 512], pxf[:]))

        @E.block.tensor
        def _(te):
            te.wait_ge(E.sem("eV"), E.n("eV"))
            if i == 0:
                E.qG.wait(te, rtok[0])
            for ic in range(8):
                ins = te.matmul(plog[:, 0:7], xfT[:, ic * 128:(ic + 1) * 128],
                                r3(wr[:], 8)[:, ic, 0:7],
                                start=(ic == 0), stop=(ic == 7))
            E.tinc(ins)

        @E.block.vector
        def _(v):
            v.wait_ge(E.sem("eT"), E.n("eT"))
            E.vinc(v.tensor_mul(probs[:, 0:7], plog[:, 0:7], rb[:, 0:7]))

        @E.block.scalar
        def _(sc):
            sc.wait_ge(E.sem("eV"), E.n("eV"))
            E.sinc(sc.activation(probs[:, 0:7], probs[:, 0:7], AF.Exp, scale=-1.0))

        @E.block.vector
        def _(v):
            v.wait_ge(E.sem("eS"), E.n("eS"))
            E.vinc(v.tensor_scalar_add(probs[:, 0:7], probs[:, 0:7], 1.0))
            v.wait_ge(E.sem("eV"), E.n("eV"))
            E.vinc(v.reciprocal(probs[:, 0:7], probs[:, 0:7]))
            v.wait_ge(E.sem("eV"), E.n("eV"))
            E.vinc(v.max(t8[:], probs[:]))
            v.wait_ge(E.sem("eV"), E.n("eV"))
            E.vinc(v.tensor_add(rec[:], t8[:, 0:1], t8[:, 1:2]))
            v.wait_ge(E.sem("eV"), E.n("eV"))
            E.vinc(v.reciprocal(rec[:], rec[:]))
            v.wait_ge(E.sem("eV"), E.n("eV"))
            E.vinc(v.tensor_scalar(msk[:, 0:7], probs[:, 0:7], t8[:, 1:2], None, ALU.is_ge))
            v.wait_ge(E.sem("eV"), E.n("eV"))
            E.vinc(v.tensor_mul(msk[:, 0:7], msk[:, 0:7], probs[:, 0:7]))
            v.wait_ge(E.sem("eV"), E.n("eV"))
            E.vinc(v.tensor_scalar(wbuf[:, i * 8:i * 8 + 7], msk[:, 0:7], rec[:],
                                   None, ALU.mult))


def emit_stage4_experts(E):
    nc = E.nc
    NC, NOWN, SOWN, G, NB, QB, NHC = shapes()
    hT = E.bufs["hT"]
    resid = E.bufs["resid"]
    gu = E.bufs["s4gu"]
    wgu, wd = E.bufs["s4wgu"], E.bufs["s4wd"]
    gs = [E.bufs["s4gs0"], E.bufs["s4gs1"]]
    zsb = [E.bufs["s4zsb"], E.bufs["s4zsb1"]]
    pA = [E.bufs["ps_a"], E.bufs["s4pz0"][:, 0:QB]]
    pB = [E.bufs["ps_b"], E.bufs["s4pz1"]]
    pz = [E.bufs["ps_a"], E.bufs["ps_b"]]
    ptr = [E.bufs["s4ptrp"][:, 0:QB], E.bufs["lnptr0"][:, 0:QB]]
    wg3 = E.dram["wg"].rearrange("e (c p) n -> e p c n", p=128)
    wu3 = E.dram["wu"].rearrange("e (c p) n -> e p c n", p=128)
    wd3 = E.dram["wd"].rearrange("e (c p) n -> e p c n", p=128)
    wbuf = E.bufs["s4wbuf"]
    halves = [list(range(NB))] if NB == 1 else [list(range(NB // 2)),
                                               list(range(NB // 2, NB))]
    gtok = {}
    up_mm_hist = []
    dn_mm_hist = []
    first = True

    for e in range(NEXP + 1):
      for hf, tts in enumerate(halves):
        NT = len(tts)
        base_T = E.n("eT")
        base_S = E.n("eS")
        base_V = E.n("eV")
        nK = NHC * NT
        silu_cnt = lambda k: base_S + k + 1
        gu_cnt = lambda k: base_V + k + 1
        tmm = lambda k: base_T + k + 1

        @E.block.gpsimd
        def _(g):
            for hc in range(NHC):
                sl = hc % 2
                if hc < 2 and up_mm_hist:
                    g.wait_ge(E.sem("eT"), up_mm_hist[-1])
                if hc >= 2:
                    g.wait_ge(E.sem("eT"), base_T + (hc - 2) * NT + NT)
                gtok[(e, hf, hc, 0)] = E.qG.issue(
                    g, r3(wgu[:], 4 * 8)[:, sl * 16:sl * 16 + 8, :],
                    wg3[e][:, :, hc * 128:(hc + 1) * 128])
                gtok[(e, hf, hc, 1)] = E.qG.issue(
                    g, r3(wgu[:], 4 * 8)[:, sl * 16 + 8:sl * 16 + 16, :],
                    wu3[e][:, :, hc * 128:(hc + 1) * 128])

        @E.block.tensor
        def _(te):
            if first:
                te.wait_ge(E.sem("lnev"), E.n("lnev"))
            for hc in range(NHC):
                sl = hc % 2
                for lt, tt in enumerate(tts):
                    k = hc * NT + lt
                    jj = k % 2
                    if lt == 0:
                        E.qG.wait(te, gtok[(e, hf, hc, 0)])
                        E.qG.wait(te, gtok[(e, hf, hc, 1)])
                    if k >= 2:
                        te.wait_ge(E.sem("eS"), silu_cnt(k - 2))
                        te.wait_ge(E.sem("eV"), gu_cnt(k - 2))
                    elif e + hf > 0:
                        te.wait_ge(E.sem("eV"), base_V)
                    for ic in range(8):
                        te.matmul(pA[jj][:, 0:QB],
                                  r3(wgu[:], 4 * 8)[:, sl * 16 + ic, :],
                                  r3(hT[:], 8)[:, ic, tt * QB:(tt + 1) * QB],
                                  start=(ic == 0), stop=(ic == 7))
                    for ic in range(8):
                        ins = te.matmul(pB[jj][:, 0:QB],
                                        r3(wgu[:], 4 * 8)[:, sl * 16 + 8 + ic, :],
                                        r3(hT[:], 8)[:, ic, tt * QB:(tt + 1) * QB],
                                        start=(ic == 0), stop=(ic == 7))
                    E.tinc(ins)

        @E.block.scalar
        def _(sc):
            for hc in range(NHC):
                for lt in range(NT):
                    k = hc * NT + lt
                    jj = k % 2
                    sc.wait_ge(E.sem("eT"), tmm(k))
                    if k >= 2:
                        sc.wait_ge(E.sem("eV"), gu_cnt(k - 2))
                    E.sinc(sc.activation(gs[jj][:, 0:QB], pA[jj][:, 0:QB], AF.Silu))

        @E.block.vector
        def _(v):
            for hc in range(NHC):
                for lt in range(NT):
                    k = hc * NT + lt
                    jj = k % 2
                    v.wait_ge(E.sem("eS"), silu_cnt(k))
                    E.vinc(v.tensor_tensor(
                        r3(gu[:], NHC)[:, hc, lt * QB:(lt + 1) * QB],
                        gs[jj][:, 0:QB], pB[jj][:, 0:QB], ALU.mult))
        up_mm_hist.append(E.n("eT"))
        first = False

        base_T = E.n("eT")
        base_V = E.n("eV")
        gu_final = base_V
        nDK = 8 * NT
        dn_mm = lambda k: base_T + 2 * k + 1 - (1 if k == 0 else 0)
        t_cnt = [0] * (nDK + 1)
        cnt = base_T
        mm_cnt = [0] * nDK
        tr_cnt = [0] * nDK
        for k in range(nDK):
            cnt += 1
            mm_cnt[k] = cnt
            if k >= 1:
                cnt += 1
                tr_cnt[k - 1] = cnt
        cnt += 1
        tr_cnt[nDK - 1] = cnt
        evac_cnt = [0] * nDK
        vcnt = base_V
        for k in range(nDK):
            vcnt += 1
            evac_cnt[k] = vcnt
            if k >= 1:
                vcnt += 1
        add_last = vcnt + 1

        @E.block.gpsimd
        def _(g):
            for oc in range(8):
                slo = oc % 2
                if oc < 2 and dn_mm_hist:
                    g.wait_ge(E.sem("eT"), dn_mm_hist[-1])
                if oc >= 2:
                    g.wait_ge(E.sem("eT"), mm_cnt[(oc - 2) * NT + NT - 1])
                gtok[("d", e, hf, oc)] = E.qG.issue(
                    g, r3(wd[:], 2 * NHC)[:, slo * NHC:slo * NHC + NHC, :],
                    wd3[e][:, :, oc * 128:(oc + 1) * 128])

        @E.block.tensor
        def _(te):
            for oc in range(8):
                slo = oc % 2
                for lt, tt in enumerate(tts):
                    k = oc * NT + lt
                    jj = k % 2
                    if lt == 0:
                        E.qG.wait(te, gtok[("d", e, hf, oc)])
                        if oc == 0:
                            te.wait_ge(E.sem("eV"), gu_final)
                    if k >= 2:
                        te.wait_ge(E.sem("eV"), evac_cnt[k - 2])
                    for hc in range(NHC):
                        ins = te.matmul(pz[jj][:, 0:QB],
                                        r3(wd[:], 2 * NHC)[:, slo * NHC + hc, :],
                                        r3(gu[:], NHC)[:, hc, lt * QB:(lt + 1) * QB],
                                        start=(hc == 0), stop=(hc == NHC - 1))
                    E.tinc(ins)
                    assert E.n("eT") == mm_cnt[k]
                    if k >= 1:
                        j2 = (k - 1) % 2
                        te.wait_ge(E.sem("eV"), evac_cnt[k - 1])
                        for q in range(G):
                            ins = te.transpose(ptr[j2][:, q * 128:(q + 1) * 128],
                                               zsb[j2][:, q * 128:(q + 1) * 128],
                                               E.bufs["idnb"][:])
                        E.tinc(ins)
            j2 = (nDK - 1) % 2
            te.wait_ge(E.sem("eV"), evac_cnt[nDK - 1])
            for q in range(G):
                ins = te.transpose(ptr[j2][:, q * 128:(q + 1) * 128],
                                   zsb[j2][:, q * 128:(q + 1) * 128],
                                   E.bufs["idnb"][:])
            E.tinc(ins)
            assert E.n("eT") == tr_cnt[nDK - 1]

        @E.block.vector
        def _(v):
            def adds(k):
                j2 = k % 2
                oc = k // NT
                tt = tts[k % NT]
                v.wait_ge(E.sem("eT"), tr_cnt[k])
                for q in range(G):
                    ci = G * tt + q
                    dst = resid[:, ci * D + oc * 128: ci * D + oc * 128 + 128]
                    if e == NEXP:
                        ins = v.tensor_add(dst, dst, ptr[j2][:, q * 128:(q + 1) * 128])
                    else:
                        ins = v.scalar_tensor_tensor(
                            dst, ptr[j2][:, q * 128:(q + 1) * 128],
                            wbuf[:, ci * 8 + e:ci * 8 + e + 1],
                            dst, ALU.mult, ALU.add)
                E.vinc(ins)

            for k in range(nDK):
                jj = k % 2
                v.wait_ge(E.sem("eT"), mm_cnt[k])
                E.vinc(v.tensor_copy(zsb[jj][:, 0:QB], pz[jj][:, 0:QB]))
                assert E.n("eV") == evac_cnt[k]
                if k >= 1:
                    adds(k - 1)
            adds(nDK - 1)
            assert E.n("eV") == add_last
        dn_mm_hist.append(E.n("eT"))


def build_prog4():
    NC, NOWN, SOWN, G, NB, QB, NHC = shapes()
    specs = [("xin", (SOWN, D), F32, "ExternalInput"),
             ("wg", (NEXP + 1, D, HID), F32, "ExternalInput"),
             ("wu", (NEXP + 1, D, HID), F32, "ExternalInput"),
             ("wd", (NEXP + 1, HID, D), F32, "ExternalInput"),
             ("wr", (D, 8), F32, "ExternalInput"),
             ("rbias", (1, 7), F32, "ExternalInput"),
             ("x4", (SOWN, D), F32, "ExternalOutput")]
    specs += const_specs(("idn", "ones"))
    nc, dram = new_prog(specs)
    with ExitStack() as ctx:
        block = ctx.enter_context(nc.Block())
        E = Env(nc, ctx, block)
        E.dram = dram
        E.sb("resid", [128, NOWN * D], F32)
        E.sb("hT", [128, 8 * SOWN], BF16)
        alloc_ln_bufs(E)
        alloc_stage4_bufs(E)
        emit_load_constants(E, want=("idn", "ones"))
        E.barrier()
        emit_ln(E, [(i * 128, i, i * 128) for i in range(NOWN)], SOWN,
                src=dram["xin"])
        emit_stage4_router(E)
        E.barrier()
        emit_stage4_experts(E)
        E.barrier()
        emit_writeback(E, "x4", waits=("eV",))
    return nc


def emit_writeback(E, name, waits=()):
    NC, NOWN, SOWN, G, NB, QB, NHC = shapes()
    resid = E.bufs["resid"]

    @E.block.sync
    def _(sy):
        for s in waits:
            sy.wait_ge(E.sem(s), E.n(s))
        for i in range(NOWN):
            E.qS.issue(sy, E.dram[name][i * 128:(i + 1) * 128, :],
                       resid[:, i * D:(i + 1) * D])
        E.qS.wait_all(sy)


def new_prog(dram_specs):
    nc = bass.Bass("TRN2", target_bir_lowering=False)
    dram = {}
    for nm, shp, dt, kind in dram_specs:
        dram[nm] = nc.dram_tensor(nm, list(shp), dt, kind=kind).ap()
    return nc, dram


def const_specs(names):
    cs = host_constants(0)
    return [(k, cs[k].shape, F32, "ExternalInput") for k in names]


def build_prog2():
    NC, NOWN, SOWN, G, NB, QB, NHC = shapes()
    specs = [("xin", (SOWN, D), F32, "ExternalInput"),
             ("w0", (D, D), F32, "ExternalInput"),
             ("x2", (SOWN, D), F32, "ExternalOutput")]
    specs += const_specs(("idn", "ones"))
    nc, dram = new_prog(specs)
    with ExitStack() as ctx:
        block = ctx.enter_context(nc.Block())
        E = Env(nc, ctx, block)
        E.dram = dram
        E.sb("resid", [128, NOWN * D], F32)
        E.sb("hT", [128, 8 * SOWN], BF16)
        alloc_ln_bufs(E)
        emit_load_constants(E, want=("idn", "ones"))
        E.barrier()
        emit_ln(E, [(i * 128, i, i * 128) for i in range(NOWN)], SOWN,
                src=dram["xin"])
        emit_stage2(E)
        emit_writeback(E, "x2", waits=("s2zad",))
    return nc


U32 = mybir.dt.uint32


def gather_index(parity):
    NC, NOWN, SOWN, G, NB, QB, NHC = shapes()
    gi = np.zeros((128, NC), np.uint32)
    qi = np.arange(128, dtype=np.uint32)
    for g in range(NC):
        if g < NOWN:
            gi[:, g] = parity * SOWN + g * 128 + qi
        else:
            gi[:, g] = (1 - parity) * SOWN + (g - NOWN) * 128 + qi
    return gi


def build_prog_full():
    NC, NOWN, SOWN, G, NB, QB, NHC = shapes()
    specs = [("x", (S, D), F32, "ExternalInput"),
             ("wq", (D, D), F32, "ExternalInput"),
             ("wk", (D, D), F32, "ExternalInput"),
             ("wv", (D, D), F32, "ExternalInput"),
             ("wo", (D, D), F32, "ExternalInput"),
             ("w0", (D, D), F32, "ExternalInput"),
             ("rwq", (128, H * 64), F32, "ExternalInput"),
             ("rwk", (128, H * 64), F32, "ExternalInput"),
             ("rwv", (128, H * 64), F32, "ExternalInput"),
             ("w1", (D, D), F32, "ExternalInput"),
             ("w2", (D, D), F32, "ExternalInput"),
             ("gns", (64, H), F32, "ExternalInput"),
             ("gnb", (64, H), F32, "ExternalInput"),
             ("wg", (NEXP + 1, D, HID), F32, "ExternalInput"),
             ("wu", (NEXP + 1, D, HID), F32, "ExternalInput"),
             ("wd", (NEXP + 1, HID, D), F32, "ExternalInput"),
             ("wr", (D, 8), F32, "ExternalInput"),
             ("rbias", (1, 7), F32, "ExternalInput"),
             ("gidx", (128, NC), U32, "ExternalInput"),
             ("y", (SOWN, D), F32, "ExternalOutput")]
    specs += const_specs(("idn", "ones", "perm", "cost", "sint", "dmask",
                          "qfac", "kfct"))
    nc, dram = new_prog(specs)
    cx2 = nc.dram_tensor("cx2", [SOWN, D], F32, kind="Internal").ap()
    gx2 = nc.dram_tensor("gx2", [S, D], F32, kind="Internal").ap()
    with ExitStack() as ctx:
        block = ctx.enter_context(nc.Block())
        E = Env(nc, ctx, block)
        E.dram = dram
        alloc_ln_bufs(E)
        emit_load_constants(E, want=("idn", "ones", "dmask"))
        E.sb("gidxb", [128, NC], U32)

        @E.block.gpsimd
        def _(g):
            E.qG.issue(g, E.bufs["gidxb"][:], dram["gidx"])

        with ExitStack() as epochA:
            with ExitStack() as s1o:
                E.ctx = s1o
                E.sb("attnT", [128, 8 * SOWN], BF16)
                emit_load_constants(E, want=("perm", "cost", "sint"))
                with ExitStack() as s1i:
                    E.ctx = s1i
                    E.sb("hT1", [128, 8 * S], BF16, alias="hT")
                    alloc_stage1_bufs(E)
                    E.ctx = s1o
                    E.barrier()
                    with nc.named_scope("s1ln"):
                        emit_ln(E, [(g * 128, None, g * 128) for g in range(NC)], S,
                                src=dram["x"])
                    with nc.named_scope("s1attn"):
                        emit_stage1(E)
                    E.barrier()
                E.ctx = epochA
                E.sb("residA", [128, NOWN * D], F32, alias="resid", side="right")
                E.ctx = s1o
                with nc.named_scope("s1wo"):
                    emit_stage1_wo(E)
                E.barrier()
                E.ctx = ctx

            with ExitStack() as s2o:
                E.ctx = s2o
                E.sb("hT2", [128, 8 * SOWN], BF16, alias="hT")
                with nc.named_scope("s2"):
                    emit_ln(E, [(None, i, i * 128) for i in range(NOWN)], SOWN,
                            pre_wait=("s1zad",))
                    emit_stage2(E)
                E.ctx = ctx

            @E.block.sync
            def _(sy):
                sy.wait_ge(E.sem("eV"), E.n("eV"))
                sy.wait_ge(E.sem("eS"), E.n("eS"))
                sy.wait_ge(E.sem("eT"), E.n("eT"))
                sy.wait_ge(E.sem("s2zad"), E.n("s2zad"))
                for i in range(NOWN):
                    E.qS.issue(sy, cx2[i * 128:(i + 1) * 128, :],
                               E.bufs["resid"][:, i * D:(i + 1) * D])

            @E.block.gpsimd
            def _(g):
                E.qS.wait_all(g)
                nc.enter_named_scope("xchg") if False else None
                g.collective_compute(
                    "AllGather", ALU.bypass,
                    replica_groups=[[0, 1], [2, 3], [4, 5], [6, 7]],
                    ins=[cx2.opt()], outs=[gx2.opt()]).then_inc(E.sem("cc"))
                E.counts["cc"] += 1
                g.wait_ge(E.sem("cc"), E.counts["cc"])
            E.barrier()

        with ExitStack() as epochB:
            with ExitStack() as s3o:
                E.ctx = s3o
                E.sb("hT3", [128, 8 * S], BF16, alias="hT")
                E.sb("s3norm", [128, 8 * SOWN], BF16)
                E.ps("s3psa", [128, min(512, S)], F32, alias="ps_a")
                E.ps("s3psb", [128, min(512, S)], F32, alias="ps_b")
                with ExitStack() as s3i:
                    E.ctx = s3i
                    alloc_stage3_bufs(E)
                    E.ctx = s3o

                    @E.block.gpsimd
                    def _(g):
                        for nm in ("rwq", "rwk", "rwv"):
                            E.qG.issue(g, E.bufs["s3" + nm][:], E.dram[nm])
                        E.qG.issue(g, E.bufs["s3gns"][:], E.dram["gns"])
                        E.qG.issue(g, E.bufs["s3gnb"][:], E.dram["gnb"])
                    with nc.named_scope("s3ln"):
                        emit_ln(E, [(g * 128, None, g * 128) for g in range(NC)], S,
                                src=("indirect", gx2, E.bufs["gidxb"]))
                    with nc.named_scope("s3ret"):
                        emit_stage3(E)
                    E.barrier()
                E.ctx = epochB
                E.sb("residB", [128, NOWN * D], F32, alias="resid", side="right")
                E.ctx = s3o

                @E.block.sync
                def _(sy):
                    sy.wait_ge(E.sem("eS"), E.n("eS"))
                    sy.wait_ge(E.sem("eV"), E.n("eV"))
                    sy.wait_ge(E.sem("eT"), E.n("eT"))
                    for i in range(NOWN):
                        E.qS.issue(sy, E.bufs["resid"][:, i * D:(i + 1) * D],
                                   cx2[i * 128:(i + 1) * 128, :])
                E.barrier()
                with nc.named_scope("s3out"):
                    emit_stage3_out(E)
                E.barrier()
                E.ctx = ctx

            with ExitStack() as s4o:
                E.ctx = s4o
                E.sb("hT4", [128, 8 * SOWN], BF16, alias="hT")
                alloc_stage4_bufs(E)
                with nc.named_scope("s4rt"):
                    emit_ln(E, [(None, i, i * 128) for i in range(NOWN)], SOWN,
                            pre_wait=("s3zad",))
                    emit_stage4_router(E)
                E.barrier()
                with nc.named_scope("s4moe"):
                    emit_stage4_experts(E)
                E.barrier()
                E.ctx = ctx
            emit_writeback(E, "y", waits=("eV",))
    return nc


def kernel_fused(**inputs):
    NC, NOWN, SOWN, G, NB, QB, NHC = shapes()
    w = _prep_host(inputs)
    nc = build_prog_full()
    maps = []
    for c in range(NCORES):
        p = c & 1
        m = dict(host_constants(p))
        m["gidx"] = gather_index(p)
        m["x"] = np.ascontiguousarray(w["x"][c // 2][packed_positions(p)])
        for k in ("wq", "wk", "wv", "wo", "w0", "rwq", "rwk", "rwv", "w1", "w2",
                  "gns", "gnb", "wg", "wu", "wd", "wr", "rbias"):
            m[k] = w[k]
        maps.append(m)
    res = run_bass_kernel_spmd(nc, maps, core_ids=list(range(NCORES)))
    out = np.empty((B, S, D), np.float32)
    for c in range(NCORES):
        out[c // 2][packed_positions(c & 1)[:SOWN]] = res.results[c]["y"]
    return out


def _prep_host(inputs):
    w = {}
    w["x"] = np.asarray(inputs["L0_attn_x"])
    w["wq"] = np.ascontiguousarray(np.asarray(inputs["L0_attn_Wq"]).transpose(1, 0, 2).reshape(D, D))
    w["wk"] = np.ascontiguousarray(np.asarray(inputs["L0_attn_Wk"]).transpose(1, 0, 2).reshape(D, D))
    w["wv"] = np.ascontiguousarray(np.asarray(inputs["L0_attn_Wv"]).transpose(1, 0, 2).reshape(D, D))
    w["wo"] = np.ascontiguousarray(np.asarray(inputs["L0_attn_Wo"]))
    w["w0"] = np.ascontiguousarray(np.asarray(inputs["L0_ffn_weight"]))
    for nm, key in (("rwq", "L1_attn_wq"), ("rwk", "L1_attn_wk"), ("rwv", "L1_attn_wv")):
        a = np.asarray(inputs[key]).transpose(1, 0, 2).reshape(64, H * 64)
        w[nm] = np.ascontiguousarray(np.tile(a, (2, 1)))
    w["w1"] = np.ascontiguousarray(np.asarray(inputs["L1_attn_w1"]))
    w["w2"] = np.ascontiguousarray(np.asarray(inputs["L1_attn_w2"]))
    w["gns"] = np.ascontiguousarray(np.asarray(inputs["L1_attn_gn_scale"]).reshape(H, 64).T)
    w["gnb"] = np.ascontiguousarray(np.asarray(inputs["L1_attn_gn_bias"]).reshape(H, 64).T)
    w["wg"] = np.ascontiguousarray(np.concatenate(
        [np.asarray(inputs["L1_ffn_routed_gate"]),
         np.asarray(inputs["L1_ffn_shared_gate"])[None]], 0))
    w["wu"] = np.ascontiguousarray(np.concatenate(
        [np.asarray(inputs["L1_ffn_routed_up"]),
         np.asarray(inputs["L1_ffn_shared_up"])[None]], 0))
    w["wd"] = np.ascontiguousarray(np.concatenate(
        [np.asarray(inputs["L1_ffn_routed_down"]),
         np.asarray(inputs["L1_ffn_shared_down"])[None]], 0))
    wr = np.asarray(inputs["L1_ffn_Wr"])
    wrp = np.zeros((D, 8), np.float32)
    wrp[:, 0:7] = wr
    w["wr"] = wrp
    w["rbias"] = np.ascontiguousarray(np.asarray(inputs["L1_ffn_routing_bias"]).reshape(1, 7))
    return w


def kernel(**inputs):
    return kernel_fused(**inputs)


def kernel_multilaunch(**inputs):
    NC, NOWN, SOWN, G, NB, QB, NHC = shapes()
    w = _prep_host(inputs)
    consts = [host_constants(c & 1) for c in range(NCORES)]
    packs = [packed_positions(c & 1) for c in range(NCORES)]

    def cmaps(names, extra):
        maps = []
        for c in range(NCORES):
            m = {k: consts[c][k] for k in names}
            m.update(extra(c))
            maps.append(m)
        return maps

    nc1 = build_prog1()
    maps1 = cmaps(("idn", "ones", "perm", "cost", "sint", "dmask"), lambda c: {
        "x": np.ascontiguousarray(w["x"][c // 2][packs[c]]),
        "wq": w["wq"], "wk": w["wk"], "wv": w["wv"], "wo": w["wo"]})
    res1 = run_bass_kernel_spmd(nc1, maps1, core_ids=list(range(NCORES)))
    x1own = [res1.results[c]["x1"] for c in range(NCORES)]

    nc2 = build_prog2()
    maps2 = cmaps(("idn", "ones"), lambda c: {"xin": x1own[c], "w0": w["w0"]})
    res2 = run_bass_kernel_spmd(nc2, maps2, core_ids=list(range(NCORES)))
    x2own = [res2.results[c]["x2"] for c in range(NCORES)]

    nc3 = build_prog3()
    maps3 = cmaps(("idn", "ones", "dmask", "qfac", "kfct"), lambda c: {
        "x2f": np.ascontiguousarray(np.concatenate([x2own[c], x2own[c ^ 1]], 0)),
        "rwq": w["rwq"], "rwk": w["rwk"], "rwv": w["rwv"],
        "w1": w["w1"], "w2": w["w2"], "gns": w["gns"], "gnb": w["gnb"]})
    res3 = run_bass_kernel_spmd(nc3, maps3, core_ids=list(range(NCORES)))
    x3own = [res3.results[c]["x3"] for c in range(NCORES)]

    nc4 = build_prog4()
    maps4 = cmaps(("idn", "ones"), lambda c: {
        "xin": x3own[c], "wg": w["wg"], "wu": w["wu"], "wd": w["wd"],
        "wr": w["wr"], "rbias": w["rbias"]})
    res4 = run_bass_kernel_spmd(nc4, maps4, core_ids=list(range(NCORES)))

    out = np.empty((B, S, D), np.float32)
    for c in range(NCORES):
        out[c // 2][packs[c][:SOWN]] = res4.results[c]["x4"]
    return out


# version 12
# speedup vs baseline: 1.4457x; 1.4457x over previous
import numpy as np
import ml_dtypes
import concourse.bass as bass
import concourse.mybir as mybir
from concourse.bass_utils import run_bass_kernel_spmd
from contextlib import ExitStack

F32 = mybir.dt.float32
BF16 = mybir.dt.bfloat16
AF = mybir.ActivationFunctionType
ALU = mybir.AluOpType
AX = mybir.AxisListType
BF = ml_dtypes.bfloat16

B = 4
S = 4096
D = 1024
H = 16
DK = 64
HID = 2816
NEXP = 7
NCORES = 8
EPS = 1e-5


def shapes():
    NC = S // 128
    NOWN = NC // 2
    SOWN = NOWN * 128
    G = min(4, NOWN)
    NB = NOWN // G
    QB = G * 128
    NHC = HID // 128
    return NC, NOWN, SOWN, G, NB, QB, NHC


def gammas_np():
    lin = np.linspace(np.log(1.0 / 512.0), np.log(1.0 / 32.0), H)
    return 1.0 - np.exp(lin)


def packed_positions(parity):
    NC, NOWN, SOWN, G, NB, QB, NHC = shapes()
    own = np.concatenate([np.arange((parity + 2 * i) * 128, (parity + 2 * i) * 128 + 128)
                          for i in range(NOWN)])
    sib = np.concatenate([np.arange((1 - parity + 2 * i) * 128,
                                    (1 - parity + 2 * i) * 128 + 128)
                          for i in range(NOWN)])
    return np.concatenate([own, sib])


def host_constants(parity):
    NC, NOWN, SOWN, G, NB, QB, NHC = shapes()
    c = {}
    pos = np.arange(S, dtype=np.float64)
    inv = 10000.0 ** (-(np.arange(0, DK, 2, dtype=np.float64) / DK))
    ang = pos[:, None] * inv[None, :]
    cosr = np.repeat(np.cos(ang), 2, axis=-1).T
    sinr = np.repeat(np.sin(ang), 2, axis=-1).T
    pk = packed_positions(parity)
    c["cost"] = np.ascontiguousarray(np.tile(cosr, (2, 1))[:, pk]).astype(np.float32)
    c["sint"] = np.ascontiguousarray(np.tile(sinr, (2, 1))[:, pk]).astype(np.float32)
    P = np.zeros((128, 128), np.float32)
    for i in range(64):
        P[2 * i, 2 * i + 1] = -1.0
        P[2 * i + 1, 2 * i] = 1.0
    c["perm"] = np.ascontiguousarray(P.T)
    c["idn"] = np.eye(128, dtype=np.float32)
    c["ones"] = np.ones((128, 128), np.float32)
    dm = np.zeros((128, (2 * G) * QB), np.float32)
    ki = np.arange(128)[:, None]
    qi = np.arange(128)[None, :]
    for part in range(2):
        for r in range(G):
            for i in range(G):
                blk = dm[:, (part * G + r) * QB + i * 128:
                         (part * G + r) * QB + (i + 1) * 128]
                if part == 0:
                    if r < i:
                        blk[:] = 1.0
                    elif r == i:
                        blk[:] = (qi >= ki).astype(np.float32)
                else:
                    keep = (r < i) if parity == 0 else (r <= i)
                    if keep:
                        blk[:] = 1.0
    c["dmask"] = dm
    g = gammas_np()
    j = np.arange(QB)
    qoff = (256 * (j // 128) + (j % 128)).astype(np.float64)
    c["qfac"] = np.exp(np.log(g)[:, None] * qoff[None, :]).reshape(1, H * QB).astype(np.float32)
    KB = min(512, S)
    kio = np.arange(128, dtype=np.float64)
    own_e = np.tile(-kio, KB // 128)
    sib_e = np.tile(-kio + (128.0 * (2 * parity - 1)), KB // 128)
    c["kfct"] = np.exp(np.log(g)[:, None]
                       * np.concatenate([own_e, sib_e])[None, :]
                       ).reshape(1, H * 2 * KB).astype(np.float32)
    return c


def decay_imm(h, m, c, G):
    g = gammas_np()[h]
    v = np.exp(np.log(g) * 256.0 * (G * m - c))
    v32 = np.float32(v)
    if not np.isfinite(v32):
        v32 = np.float32(0.0)
    return float(v32)


class DQ:

    def __init__(self, E, name, n=4):
        self.E = E
        self.name = name
        self.n = n
        self.counts = [0] * n
        self.idx = 0
        for i in range(n):
            E.sem(f"{name}{i}")

    def issue(self, eng, out, in_):
        k = self.idx % self.n
        self.idx += 1
        sem = self.E.sem(f"{self.name}{k}")
        if self.counts[k] > 0:
            eng.wait_ge(sem, 16 * self.counts[k])
        eng.dma_start(out=out, in_=in_).then_inc(sem, 16)
        self.counts[k] += 1
        return (k, self.counts[k])

    def issue_indirect(self, eng, out, in_, in_offset):
        k = self.idx % self.n
        self.idx += 1
        sem = self.E.sem(f"{self.name}{k}")
        if self.counts[k] > 0:
            eng.wait_ge(sem, 16 * self.counts[k])
        eng.indirect_dma_start(out=out, out_offset=None, in_=in_,
                               in_offset=in_offset).then_inc(sem, 16)
        self.counts[k] += 1
        return (k, self.counts[k])

    def wait(self, eng, token):
        k, c = token
        eng.wait_ge(self.E.sem(f"{self.name}{k}"), 16 * c)

    def wait_all(self, eng):
        for k in range(self.n):
            if self.counts[k]:
                eng.wait_ge(self.E.sem(f"{self.name}{k}"), 16 * self.counts[k])


class Env:
    def __init__(self, nc, ctx, block):
        self.nc = nc
        self.ctx = ctx
        self.block = block
        self.sems = {}
        self.counts = {}
        self.barn = 0
        self.bufs = {}
        self._qS = None
        self._qG = None

    @property
    def qS(self):
        if self._qS is None:
            self._qS = DQ(self, "qS", 4)
        return self._qS

    @property
    def qG(self):
        if self._qG is None:
            self._qG = DQ(self, "qG", 4)
        return self._qG

    def sem(self, name):
        if name not in self.sems:
            self.sems[name] = self.ctx.enter_context(self.nc.semaphore(name))
            self.counts[name] = 0
        return self.sems[name]

    def inc(self, instr, name, k=1):
        instr.then_inc(self.sem(name), k)
        self.counts[name] += k
        return self.counts[name]

    def n(self, name):
        self.sem(name)
        return self.counts[name]

    def sb(self, name, shape, dtype, alias=None, side=None):
        if side is None:
            t = self.ctx.enter_context(self.nc.sbuf_tensor(name, shape, dtype))
        else:
            t = self.ctx.enter_context(self.nc.sbuf_tensor(name, shape, dtype, side=side))
        self.bufs[alias or name] = t
        return t

    def ps(self, name, shape, dtype=F32, alias=None):
        t = self.ctx.enter_context(self.nc.psum_tensor(name, shape, dtype))
        self.bufs[alias or name] = t
        return t

    def tinc(self, ins):
        return self.inc(ins, "eT", 1)

    def vinc(self, ins):
        return self.inc(ins, "eV", 1)

    def sinc(self, ins):
        return self.inc(ins, "eS", 1)

    def ginc(self, ins):
        return self.inc(ins, "eG", 1)

    def barrier(self):
        self.barn += 1
        bar = self.sem("bar")

        @self.block.sync
        def _(e):
            if self._qS is not None:
                self._qS.wait_all(e)
            e.nop().then_inc(bar, 1)

        @self.block.gpsimd
        def _(e):
            if self._qG is not None:
                self._qG.wait_all(e)
            e.nop().then_inc(bar, 1)

        for engf in (self.block.tensor, self.block.vector, self.block.scalar):
            @engf
            def _(e):
                e.nop().then_inc(bar, 1)
        tgt = 5 * self.barn
        self.counts["bar"] = tgt
        for engf in (self.block.tensor, self.block.vector, self.block.scalar,
                     self.block.gpsimd, self.block.sync):
            @engf
            def _(e):
                e.wait_ge(bar, tgt)


def r3(ap, c):
    return ap.rearrange("p (c w) -> p c w", c=c)


def emit_load_constants(E, want=("idn", "ones")):
    nc = E.nc
    g = None

    @E.block.gpsimd
    def _(gp):
        for nm in want:
            src = E.dram[nm]
            sh = list(src.shape)
            if nm == "idn":
                E.qG.issue(gp, E.sb("idnf", [128, 128], F32)[:], src)
                E.qG.issue(gp, E.sb("idnb", [128, 128], BF16)[:], src)
            elif nm == "ones":
                E.qG.issue(gp, E.sb("onesf", [128, 128], F32)[:], src)
                E.qG.issue(gp, E.sb("onesb", [128, 128], BF16)[:], src)
            elif nm == "perm":
                E.qG.issue(gp, E.sb("permb", [128, 128], BF16)[:], src)
            elif nm in ("cost", "sint"):
                E.qG.issue(gp, E.sb(nm + "b", sh, BF16)[:], src)
            elif nm == "dmask":
                E.qG.issue(gp, E.sb("dmaskb", sh, BF16)[:], src)
            elif nm in ("qfac", "kfct"):
                E.qG.issue(gp, E.sb(nm + "f", sh, F32)[:], src)


def alloc_ln_bufs(E, fp32_keep_n=0):
    E.sb("lnx0", [128, D], F32)
    E.sb("lnx1", [128, D], F32)
    E.sb("lnbns", [128, 24], F32)
    E.sb("lnmv", [128, 4], F32)
    E.sb("lnrstd", [128, 2], F32)
    E.sb("lnnbias", [128, 2], F32)
    E.sb("lnxn", [128, 2 * D], BF16)
    E.ps("lnptr0", [128, 1024], BF16)
    E.ps("lnptr1", [128, 1024], BF16)
    if fp32_keep_n:
        E.sb("xnf32", [128, fp32_keep_n * D], F32)


def emit_ln(E, chunks, hT_W, src=None, fp32_keep=False, pre_wait=()):
    assert not fp32_keep
    """LN chunks into feature-major bf16 hT (and optionally fp32 residual /
    fp32 normalized copies). chunks: list of (src_row|None, resid_slot|None,
    hT_col)."""
    nc = E.nc
    hT = E.bufs["hT"]
    resid = E.bufs.get("resid")
    xt = [E.bufs["lnx0"], E.bufs["lnx1"]]
    bns, mv = E.bufs["lnbns"], E.bufs["lnmv"]
    rstd, nbias, xn = E.bufs["lnrstd"], E.bufs["lnnbias"], E.bufs["lnxn"]
    ptr = [E.bufs["lnptr0"], E.bufs["lnptr1"]]
    ld, sv, ss, tt, ev = "lnld", "lnsv", "lnss", "lntt", "lnev"
    base = {k: E.n(k) for k in (ld, sv, ss, tt, ev)}

    indirect = isinstance(src, tuple) and src and src[0] == "indirect"

    def xsrc_of(i):
        row, slot, _hc = chunks[i]
        if slot is not None and not indirect:
            return resid[:, slot * D:(slot + 1) * D]
        if row is None and slot is not None:
            return resid[:, slot * D:(slot + 1) * D]
        return xt[i % 2][:]

    tokens = {}
    if src is not None and isinstance(src, tuple) and src[0] == "indirect":
        _, gsrc, gidx_sb = src

        @E.block.gpsimd
        def _(g):
            import bass_rust
            for i, (row, slot, _hc) in enumerate(chunks):
                if slot is None and i >= 2:
                    g.wait_ge(E.sem(sv), base[sv] + 5 * (i - 1))
                tokens[i] = E.qG.issue_indirect(
                    g, xsrc_of(i), gsrc,
                    bass_rust.IndirectOffsetOnAxis(gidx_sb[:, i:i + 1], 0))
        qwait = E.qG
    elif src is not None:
        @E.block.sync
        def _(sy):
            for i, (row, slot, _hc) in enumerate(chunks):
                if row is None:
                    continue
                if slot is None and i >= 2:
                    sy.wait_ge(E.sem(sv), base[sv] + 5 * (i - 1))
                tokens[i] = E.qS.issue(sy, xsrc_of(i), src[row:row + 128, :])
        qwait = E.qS
    else:
        qwait = E.qS

    n = len(chunks)

    @E.block.vector
    def _(v):
        for s_ in pre_wait:
            v.wait_ge(E.sem(s_), E.n(s_))

        def emit_evac(i):
            j = i % 2
            hc = chunks[i][2]
            v.wait_ge(E.sem(tt), base[tt] + i + 1)
            E.counts[ev] += 1
            v.tensor_copy(r3(hT[:], 8)[:, :, hc:hc + 128],
                          r3(ptr[j][:], 8)).then_inc(E.sem(ev), 1)

        for i, (row, slot, _hc) in enumerate(chunks):
            j = i % 2
            if i in tokens:
                qwait.wait(v, tokens[i])
            x = xsrc_of(i)
            v.bn_stats(bns[:, j * 12:j * 12 + 6], x[:, 0:512])
            E.counts[sv] += 1
            v.bn_stats(bns[:, j * 12 + 6:j * 12 + 12], x[:, 512:1024]).then_inc(E.sem(sv), 1)
            v.wait_ge(E.sem(sv), E.counts[sv])
            E.counts[sv] += 1
            v.bn_aggr(mv[:, j * 2:j * 2 + 2], bns[:, j * 12:j * 12 + 12]).then_inc(E.sem(sv), 1)
            v.wait_ge(E.sem(sv), E.counts[sv])
            E.counts[sv] += 1
            v.tensor_scalar_add(mv[:, j * 2 + 1:j * 2 + 2],
                                mv[:, j * 2 + 1:j * 2 + 2], EPS).then_inc(E.sem(sv), 1)
            v.wait_ge(E.sem(ss), base[ss] + 2 * i + 2)
            E.counts[sv] += 1
            v.scalar_tensor_tensor(nbias[:, j:j + 1], mv[:, j * 2:j * 2 + 1], -1.0,
                                   rstd[:, j:j + 1], ALU.mult, ALU.mult).then_inc(E.sem(sv), 1)
            v.wait_ge(E.sem(sv), E.counts[sv])
            if i >= 2:
                v.wait_ge(E.sem(tt), base[tt] + (i - 2) + 1)
            E.counts[sv] += 1
            v.scalar_tensor_tensor(xn[:, j * D:(j + 1) * D], x, rstd[:, j:j + 1],
                                   nbias[:, j:j + 1].broadcast_to((128, D)),
                                   ALU.mult, ALU.add).then_inc(E.sem(sv), 1)
            if i >= 1:
                emit_evac(i - 1)
        emit_evac(n - 1)

    @E.block.scalar
    def _(sc):
        for i, (row, slot, _hc) in enumerate(chunks):
            j = i % 2
            sc.wait_ge(E.sem(sv), base[sv] + 5 * i + 3)
            E.counts[ss] += 1
            sc.activation(rstd[:, j:j + 1], mv[:, j * 2 + 1:j * 2 + 2],
                          AF.Ln).then_inc(E.sem(ss), 1)
            sc.wait_ge(E.sem(ss), E.counts[ss])
            E.counts[ss] += 1
            sc.activation(rstd[:, j:j + 1], rstd[:, j:j + 1], AF.Exp,
                          scale=-0.5).then_inc(E.sem(ss), 1)

    @E.block.tensor
    def _(te):
        for i in range(n):
            j = i % 2
            te.wait_ge(E.sem(sv), base[sv] + 5 * i + 5)
            if i >= 2:
                te.wait_ge(E.sem(ev), base[ev] + (i - 2) + 1)
            E.counts[tt] += 1
            for k in range(8):
                ins = te.transpose(ptr[j][:, k * 128:(k + 1) * 128],
                                   xn[:, j * D + k * 128: j * D + (k + 1) * 128],
                                   E.bufs["idnb"][:])
                if k == 7:
                    ins.then_inc(E.sem(tt), 1)


def emit_zT_to_resid(E, zsrc_psum, zs_buf, oc, chunk_idxs, mcols_w, scale_ap=None,
                     ps_o=None, pfx="z", w_col=None):
    resid = E.bufs["resid"]
    n = len(chunk_idxs)
    smm, sev, str_, sad = pfx + "mm", pfx + "ev", pfx + "tr", pfx + "ad"
    it = E.n(sev)

    @E.block.vector
    def _(v):
        v.wait_ge(E.sem(smm), it + 1)
        if it >= 1:
            v.wait_ge(E.sem(str_), it)
        if scale_ap is not None:
            ins = v.tensor_scalar(zs_buf[:, 0:mcols_w], zsrc_psum[:, 0:mcols_w],
                                  scale_ap, None, ALU.mult)
        else:
            ins = v.tensor_copy(zs_buf[:, 0:mcols_w], zsrc_psum[:, 0:mcols_w])
        E.counts[sev] += 1
        ins.then_inc(E.sem(sev), 1)

    @E.block.tensor
    def _(te):
        te.wait_ge(E.sem(sev), it + 1)
        if it >= 1:
            te.wait_ge(E.sem(sad), it)
        for k in range(n):
            ins = te.transpose(ps_o[:, k * 128:(k + 1) * 128],
                               zs_buf[:, k * 128:(k + 1) * 128], E.bufs["idnf"][:])
        E.counts[str_] += 1
        ins.then_inc(E.sem(str_), 1)

    @E.block.vector
    def _(v):
        v.wait_ge(E.sem(str_), it + 1)
        for k, ci in enumerate(chunk_idxs):
            dst = resid[:, ci * D + oc * 128: ci * D + oc * 128 + 128]
            if w_col is None:
                ins = v.tensor_add(dst, dst, ps_o[:, k * 128:(k + 1) * 128])
            else:
                ins = v.scalar_tensor_tensor(dst, ps_o[:, k * 128:(k + 1) * 128],
                                             w_col[k], dst, ALU.mult, ALU.add)
        E.counts[sad] += 1
        ins.then_inc(E.sem(sad), 1)


def emit_stage2(E):
    nc = E.nc
    NC, NOWN, SOWN, G, NB, QB, NHC = shapes()
    wf = E.sb("s2wf", [128, 8 * D], F32)
    wqT = E.sb("s2wqT", [128, 8 * D], BF16)
    wqc = E.sb("s2wqc", [128, D], BF16)
    wtm = E.sb("s2wtm", [128, D], BF16)
    asum = E.sb("s2asum", [128, 8], F32)
    tot = E.sb("s2tot", [1, 2], F32)
    sca = E.sb("s2sca", [128, 3], F32)
    zs = E.sb("s2zs", [128, QB], F32)
    pst = E.ps("s2pst", [128, 1024], BF16)
    psz = [E.ps("s2psz0", [128, QB], F32), E.ps("s2psz1", [128, QB], F32)]
    pso = E.ps("s2pso", [128, QB], F32)
    pss = psz[0][0:1, 0:8]
    psr = pso[:, 0:1]
    sv, mm, ss = "s2sv", "s2mm", "s2ss"
    for k in (sv, mm, ss, "s2zmm", "s2zev", "s2ztr", "s2zad"):
        E.sem(k)

    tok_wf = [None]

    @E.block.sync
    def _(sy):
        tok_wf[0] = E.qS.issue(sy, r3(wf[:], 8),
                               E.dram["w0"].rearrange("(c p) n -> p c n", p=128))

    @E.block.vector
    def _(v):
        E.qS.wait(v, tok_wf[0])
        for c in range(8):
            ins = v.reduce_sum(asum[:, c:c + 1], wf[:, c * D:(c + 1) * D], AX.X,
                               apply_absolute_value=True)
        E.inc(ins, sv, 1)

    @E.block.tensor
    def _(te):
        te.wait_ge(E.sem(sv), E.n(sv))
        E.inc(te.matmul(pss[:], E.bufs["onesf"][:, 0:1], asum[:], start=True, stop=True),
              mm, 1)

    @E.block.vector
    def _(v):
        v.wait_ge(E.sem(mm), E.n(mm))
        E.inc(v.reduce_sum(tot[:, 0:1], pss[:], AX.X), sv, 1)
        v.wait_ge(E.sem(sv), E.n(sv))
        E.inc(v.tensor_scalar_mul(tot[:, 0:1], tot[:, 0:1], 1.0 / (D * D)), sv, 1)
        v.wait_ge(E.sem(sv), E.n(sv))
        E.inc(v.tensor_scalar_max(tot[:, 0:1], tot[:, 0:1], EPS), sv, 1)

    @E.block.tensor
    def _(te):
        te.wait_ge(E.sem(sv), E.n(sv))
        E.inc(te.matmul(psr[:], E.bufs["onesf"][0:1, :], tot[:, 0:1], start=True, stop=True),
              mm, 1)

    @E.block.vector
    def _(v):
        v.wait_ge(E.sem(mm), E.n(mm))
        E.inc(v.tensor_copy(sca[:, 0:1], psr[:]), sv, 1)
        v.wait_ge(E.sem(sv), E.n(sv))
        v.tensor_scalar_mul(sca[:, 1:2], sca[:, 0:1], 0.5)
        E.inc(v.tensor_scalar_mul(sca[:, 2:3], sca[:, 0:1], -0.5), sv, 1)

    for c in range(8):
        @E.block.vector
        def _(v, c=c):
            v.wait_ge(E.sem(sv), E.n(sv))
            if c >= 1:
                v.wait_ge(E.sem(mm), E.n(mm))
            v.tensor_scalar(wqc[:], wf[:, c * D:(c + 1) * D], sca[:, 1:2], None, ALU.is_gt)
            E.inc(v.tensor_scalar(wtm[:], wf[:, c * D:(c + 1) * D], sca[:, 2:3], None, ALU.is_lt), sv, 1)
            v.wait_ge(E.sem(sv), E.n(sv))
            E.inc(v.tensor_sub(wqc[:], wqc[:], wtm[:]), sv, 1)

        @E.block.tensor
        def _(te, c=c):
            te.wait_ge(E.sem(sv), E.n(sv))
            if c >= 1:
                te.wait_ge(E.sem(ss), E.n(ss))
            for k in range(8):
                ins = te.transpose(pst[:, k * 128:(k + 1) * 128],
                                   wqc[:, k * 128:(k + 1) * 128], E.bufs["idnb"][:])
            E.inc(ins, mm, 1)

        @E.block.vector
        def _(v, c=c):
            v.wait_ge(E.sem(mm), E.n(mm))
            E.inc(v.tensor_copy(r3(wqT[:], 8)[:, :, c * 128:(c + 1) * 128],
                                r3(pst[:], 8)), ss, 1)

    E.barrier()

    nlnev = E.n("lnev")
    for oc in range(8):
        for m in range(NB):
            it = oc * NB + m
            jj = it % 2
            @E.block.tensor
            def _(te):
                if it == 0 and nlnev:
                    te.wait_ge(E.sem("lnev"), nlnev)
                if it >= 2:
                    te.wait_ge(E.sem("s2zev"), it - 1)
                for ic in range(8):
                    ins = te.matmul(psz[jj][:],
                                    wqT[:, ic * D + oc * 128: ic * D + oc * 128 + 128],
                                    r3(E.bufs["hT"][:], 8)[:, ic, m * QB:(m + 1) * QB],
                                    start=(ic == 0), stop=(ic == 7))
                E.inc(ins, "s2zmm", 1)

            emit_zT_to_resid(E, psz[jj], zs, oc, [G * m + k for k in range(G)], QB,
                             scale_ap=sca[:, 0:1], ps_o=pso, pfx="s2z")
    E.barrier()


def emit_stage1(E):
    nc = E.nc
    NC, NOWN, SOWN, G, NB, QB, NHC = shapes()
    KB = min(512, S)
    NKB = S // KB
    hT = E.bufs["hT"]
    attnT = E.bufs["attnT"]
    kT = [E.bufs["s1kT0"], E.bufs["s1kT1"]]
    qT = [E.bufs["s1qT0"], E.bufs["s1qT1"]]
    vv = [E.bufs["s1vv0"], E.bufs["s1vv1"]]
    ws = E.bufs["s1ws"]
    qa = [E.bufs["s1qa0"], E.bufs["s1qa1"]]
    t1, t2 = E.bufs["s1t1"], E.bufs["s1t2"]
    sqb = E.bufs["s1sqb"]
    k2row = E.bufs["s1k2row"]
    mxb = E.bufs["s1mxb"]
    ptile = [E.bufs["s1pt0"], E.bufs["s1pt1"]]
    atmp = E.bufs["s1atmp"]
    sinv = E.bufs["s1sinv"]
    pp = [E.bufs["ps_a"], E.bufs["ps_b"]]
    pj = pp
    pr = [E.bufs["ps_c"], E.bufs["ps_d"]]
    ptv = E.bufs["ps_e"]
    po = E.bufs["ps_f"]
    prep = E.bufs["ps_c"]
    pq2 = E.bufs["ps_c"][0:1, 0:QB]
    pk2 = E.bufs["ps_d"][0:1, 0:KB]
    projfree = {}
    cosb, sinb = E.bufs["costb"], E.bufs["sintb"]
    cosqb, sinqb = cosb, sinb
    dmaskb = E.bufs["dmaskb"]
    SC = 0.35355339059327373

    @E.block.vector
    def _(v):
        v.memset(kT[0][64:65, :], -1.0)
        v.memset(kT[1][64:65, :], -1.0)
        v.memset(r3(vv[0][:], NC)[:, :, 64:65], 1.0)
        E.vinc(v.memset(r3(vv[1][:], NC)[:, :, 64:65], 1.0))

    wslot = 3 * 8 * 128
    wtok = {}
    projdone = {}
    mq2free = {}

    def w_ap(hp, kind):
        return r3(ws[:], 2 * 3 * 8)[:, (hp % 2) * 24 + kind * 8:(hp % 2) * 24 + kind * 8 + 8, :]

    for hp in range(8):
        @E.block.gpsimd
        def _(g):
            if hp >= 2:
                g.wait_ge(E.sem("eT"), projdone[hp - 2])
            for kind, nm in enumerate(("wq", "wk", "wv")):
                wtok[(hp, kind)] = E.qG.issue(
                    g, w_ap(hp, kind),
                    E.dram[nm].rearrange("(c p) n -> p c n", p=128)[:, :, hp * 128:(hp + 1) * 128])

        jobs = ([("k", gb) for gb in range(NKB)] + [("v", gb) for gb in range(NKB)]
                + [("q", m) for m in range(NB)])
        nJ = len(jobs)
        base_T = E.n("eT")
        base_V = E.n("eV")
        evac_cnt = [0] * nJ
        tail_cnt = [0] * nJ
        vc = base_V
        for bi, (kind, blk) in enumerate(jobs):
            vc += 1
            evac_cnt[bi] = vc
            if bi >= 1:
                vc += 2 if jobs[bi - 1][0] in ("k", "q") else 1
                tail_cnt[bi - 1] = vc
        vc += 2 if jobs[nJ - 1][0] in ("k", "q") else 1
        tail_cnt[nJ - 1] = vc
        proj_cnt = [0] * nJ
        rot_cnt = [0] * nJ
        tc = base_T
        for bi in range(nJ):
            tc += 1
            proj_cnt[bi] = tc
            if bi >= 1:
                tc += 1
                rot_cnt[bi - 1] = tc
        tc += 1
        rot_cnt[nJ - 1] = tc

        @E.block.tensor
        def _(te):
            def emit_tailT(bi):
                kind, blk = jobs[bi]
                j = bi % 2
                w = QB if kind == "q" else KB
                te.wait_ge(E.sem("eV"), evac_cnt[bi])
                if kind == "v":
                    if bi >= 1:
                        te.wait_ge(E.sem("eV"), tail_cnt[bi - 1])
                    for kq in range(w // 128):
                        ins = te.transpose(ptv[:, kq * 128:(kq + 1) * 128],
                                           qa[j][:, kq * 128:(kq + 1) * 128],
                                           E.bufs["idnb"][:])
                else:
                    if bi >= 2:
                        te.wait_ge(E.sem("eV"), tail_cnt[bi - 2])
                    ins = te.matmul(pr[j][:, 0:w], E.bufs["permb"][:], qa[j][:, 0:w],
                                    start=True, stop=True)
                E.tinc(ins)

            for bi, (kind, blk) in enumerate(jobs):
                j = bi % 2
                kindi = {"q": 0, "k": 1, "v": 2}[kind]
                w = QB if kind == "q" else KB
                if bi == 0:
                    if hp == 0:
                        te.wait_ge(E.sem("lnev"), E.n("lnev"))
                        te.wait_ge(E.sem("eV"), E.n("eV"))
                    for kk in range(3):
                        E.qG.wait(te, wtok[(hp, kk)])
                if bi >= 2:
                    te.wait_ge(E.sem("eV"), evac_cnt[bi - 2])
                elif hp > 0:
                    te.wait_ge(E.sem("eV"), base_V)
                for ic in range(8):
                    cols = blk * KB if kind != "q" else blk * QB
                    ins = te.matmul(pj[j][:, 0:w], w_ap(hp, kindi)[:, ic, :],
                                    r3(hT[:], 8)[:, ic, cols:cols + w],
                                    start=(ic == 0), stop=(ic == 7))
                E.tinc(ins)
                assert E.n("eT") == proj_cnt[bi]
                if bi >= 1:
                    emit_tailT(bi - 1)
            emit_tailT(nJ - 1)
            assert E.n("eT") == rot_cnt[nJ - 1]
        projdone[hp] = E.n("eT")

        base_G = E.n("eG")
        t2_cnt = {}
        gcv = base_G
        for bi, (kind, blk) in enumerate(jobs):
            if kind in ("q", "k"):
                gcv += 1
                t2_cnt[bi] = gcv

        @E.block.gpsimd
        def _(g):
            for bi, (kind, blk) in enumerate(jobs):
                if kind not in ("q", "k"):
                    continue
                j = bi % 2
                w = QB if kind == "q" else KB
                cs = cosqb if kind == "q" else cosb
                cols = blk * (QB if kind == "q" else KB)
                g.wait_ge(E.sem("eV"), evac_cnt[bi])
                if bi >= 2:
                    g.wait_ge(E.sem("eV"), tail_cnt[bi - 2])
                E.ginc(g.tensor_tensor(t2[:, j * KB:j * KB + w], qa[j][:, 0:w],
                                       cs[:, cols:cols + w], ALU.mult))

        @E.block.vector
        def _(v):
            def emit_tailV(bi):
                kind, blk = jobs[bi]
                j = bi % 2
                w = QB if kind == "q" else KB
                v.wait_ge(E.sem("eT"), rot_cnt[bi])
                if kind == "v":
                    g0 = blk * (w // 128)
                    nch = w // 128
                    v.tensor_copy(r3(vv[0][:], NC)[:, g0:g0 + nch, 0:64],
                                  r3(ptv[:, 0:w], nch)[:, :, 0:64])
                    ins = v.tensor_copy(r3(vv[1][:], NC)[:, g0:g0 + nch, 0:64],
                                        r3(ptv[:, 0:w], nch)[:, :, 64:128])
                    E.vinc(ins)
                else:
                    if kind == "q":
                        sn = sinqb
                        cols = blk * QB
                        dst0, dst1 = qT[0], qT[1]
                    else:
                        sn = sinb
                        cols = blk * KB
                        dst0, dst1 = kT[0], kT[1]
                    E.vinc(v.tensor_tensor(t1[:, 0:w], pr[j][:, 0:w],
                                           sn[:, cols:cols + w], ALU.mult))
                    v.wait_ge(E.sem("eV"), E.n("eV"))
                    v.wait_ge(E.sem("eG"), t2_cnt[bi])
                    v.tensor_tensor(dst0[0:64, cols:cols + w],
                                    t2[0:64, j * KB:j * KB + w],
                                    t1[0:64, 0:w], ALU.add)
                    E.vinc(v.tensor_tensor(dst1[0:64, cols:cols + w],
                                           t2[64:128, j * KB:j * KB + w],
                                           t1[64:128, 0:w], ALU.add))
                assert E.n("eV") == tail_cnt[bi]

            for bi, (kind, blk) in enumerate(jobs):
                j = bi % 2
                w = QB if kind == "q" else KB
                v.wait_ge(E.sem("eT"), proj_cnt[bi])
                if kind in ("q", "k"):
                    ins = v.tensor_scalar(qa[j][:, 0:w], pp[j][:, 0:w], SC, None,
                                          ALU.mult)
                else:
                    ins = v.tensor_copy(qa[j][:, 0:w], pp[j][:, 0:w])
                E.vinc(ins)
                assert E.n("eV") == evac_cnt[bi]
                if bi >= 1:
                    emit_tailV(bi - 1)
            emit_tailV(nJ - 1)
        projfree[hp] = E.n("eV")

        for x in range(2):
            kTx, qTx, vvx = kT[x], qT[x], vv[x]
            base_T = E.n("eT")
            base_V = E.n("eV")
            @E.block.vector
            def _(v):
                for gb in range(NKB):
                    if gb == 0:
                        v.wait_ge(E.sem("eV"), E.n("eV"))
                    else:
                        v.wait_ge(E.sem("eT"), base_T + gb)
                    E.vinc(v.tensor_mul(sqb[:, 0:KB], kTx[0:64, gb * KB:(gb + 1) * KB],
                                        kTx[0:64, gb * KB:(gb + 1) * KB]))
                    v.wait_ge(E.sem("eT"), base_T + gb + 1)
                    E.vinc(v.tensor_copy(k2row[0:1, gb * KB:(gb + 1) * KB], pk2))
                v.wait_ge(E.sem("eV"), E.n("eV"))
                E.vinc(v.reduce_max(mxb[:], k2row[:], AX.X))
                nmax = E.n("eV")
                for m in range(NB):
                    v.wait_ge(E.sem("eT"), base_T + NKB + m)
                    E.vinc(v.tensor_mul(sqb[:, 0:QB], qTx[0:64, m * QB:(m + 1) * QB],
                                        qTx[0:64, m * QB:(m + 1) * QB]))
                    v.wait_ge(E.sem("eT"), base_T + NKB + m + 1)
                    v.wait_ge(E.sem("eV"), nmax)
                    E.vinc(v.tensor_scalar(qTx[64:65, m * QB:(m + 1) * QB], pq2,
                                           mxb[0:1, 0:1], 0.5, ALU.add, ALU.mult))

            @E.block.tensor
            def _(te):
                for gb in range(NKB):
                    te.wait_ge(E.sem("eV"), base_V + 2 * gb + 1)
                    E.tinc(te.matmul(pk2, E.bufs["onesb"][0:64, 0:1], sqb[:, 0:KB],
                                     start=True, stop=True))
                for m in range(NB):
                    te.wait_ge(E.sem("eV"), base_V + 2 * NKB + 1 + 2 * m + 1)
                    E.tinc(te.matmul(pq2, E.bufs["onesb"][0:64, 0:1], sqb[:, 0:QB],
                                     start=True, stop=True))

            if x == 1:
                mq2free[hp] = E.n("eV")

            base_T = E.n("eT")
            base_S = E.n("eS")
            base_V = E.n("eV")
            seqs = []
            for m in range(NB):
                seq = [(part, c) for part in range(2) for c in range(G * m + G)]
                seqs.append(seq)
            t_ops = []
            for m in range(NB):
                n = len(seqs[m])
                for si in range(n):
                    t_ops.append(("score", m, si))
                    if si >= 1:
                        t_ops.append(("acc", m, si - 1))
                t_ops.append(("acc", m, n - 1))
                t_ops.append(("rep", m, 0))
            score_cnt = {}
            acc_cnt = {}
            rep_cnt = {}
            tc = base_T
            for op, m, si in t_ops:
                tc += 1
                if op == "score":
                    score_cnt[(m, si)] = tc
                elif op == "acc":
                    acc_cnt[(m, si)] = tc
                else:
                    rep_cnt[m] = tc
            exp_cnt = {}
            mask_cnt = {}
            sig_cnt = {}
            atmp_cnt = {}
            store_cnt = {}
            scv = base_S
            vcv = base_V
            for m in range(NB):
                for si, (part, c) in enumerate(seqs[m]):
                    scv += 1
                    exp_cnt[(m, si)] = scv
                    if c >= G * m:
                        vcv += 1
                        mask_cnt[(m, si)] = vcv
                vcv += 1
                sig_cnt[m] = vcv
                vcv += 1
                atmp_cnt[m] = vcv
                vcv += 1
                store_cnt[m] = vcv

            @E.block.tensor
            def _(te):
                for op, m, si in t_ops:
                    seq = seqs[m]
                    if op == "score":
                        part, c = seq[si]
                        jp = part * NOWN + c
                        jj = si % 2
                        if si == 0:
                            te.wait_ge(E.sem("eV"),
                                       store_cnt[m - 1] if m > 0 else base_V)
                            if m == 0:
                                te.wait_ge(E.sem("eS"), base_S)
                        if si >= 2:
                            te.wait_ge(E.sem("eS"), exp_cnt[(m, si - 2)])
                        E.tinc(te.matmul(pp[jj][:, 0:QB],
                                         kTx[:, jp * 128:(jp + 1) * 128],
                                         qTx[:, m * QB:(m + 1) * QB],
                                         start=True, stop=True))
                        assert E.n("eT") == score_cnt[(m, si)]
                    elif op == "acc":
                        part, c = seq[si]
                        jp = part * NOWN + c
                        jj = si % 2
                        r = (part * G + (c - G * m)) if c >= G * m else -1
                        if r >= 0:
                            te.wait_ge(E.sem("eV"), mask_cnt[(m, si)])
                        else:
                            te.wait_ge(E.sem("eS"), exp_cnt[(m, si)])
                        E.tinc(te.matmul(po[:, 0:QB], r3(vvx[:], NC)[:, jp, :],
                                         ptile[jj][:, 0:QB], start=(si == 0),
                                         stop=(si == len(seq) - 1)))
                        assert E.n("eT") == acc_cnt[(m, si)]
                    else:
                        te.wait_ge(E.sem("eV"), sig_cnt[m])
                        E.tinc(te.matmul(prep[:, 0:QB], E.bufs["onesf"][0:1, :],
                                         sinv[:, 0:QB], start=True, stop=True))
                        assert E.n("eT") == rep_cnt[m]

            @E.block.scalar
            def _(sc):
                for m in range(NB):
                    for si in range(len(seqs[m])):
                        sc.wait_ge(E.sem("eT"), score_cnt[(m, si)])
                        if si >= 2:
                            sc.wait_ge(E.sem("eT"), acc_cnt[(m, si - 2)])
                        E.sinc(sc.activation(ptile[si % 2][:, 0:QB],
                                             pp[si % 2][:, 0:QB], AF.Exp))

            @E.block.vector
            def _(v):
                for m in range(NB):
                    seq = seqs[m]
                    for si, (part, c) in enumerate(seq):
                        r = (part * G + (c - G * m)) if c >= G * m else -1
                        if r >= 0:
                            v.wait_ge(E.sem("eS"), exp_cnt[(m, si)])
                            E.vinc(v.tensor_mul(ptile[si % 2][:, 0:QB],
                                                ptile[si % 2][:, 0:QB],
                                                dmaskb[:, r * QB:(r + 1) * QB]))
                    v.wait_ge(E.sem("eT"), acc_cnt[(m, len(seq) - 1)])
                    E.vinc(v.reciprocal(sinv[:, 0:QB], po[64:65, 0:QB]))
                    assert E.n("eV") == sig_cnt[m]
                    E.vinc(v.tensor_copy(atmp[:, 0:QB], po[0:64, 0:QB]))
                    v.wait_ge(E.sem("eV"), E.n("eV"))
                    v.wait_ge(E.sem("eT"), rep_cnt[m])
                    h = 2 * hp + x
                    dst = attnT[(h % 2) * 64:(h % 2) * 64 + 64,
                                (h // 2) * SOWN + m * QB:(h // 2) * SOWN + (m + 1) * QB]
                    E.vinc(v.tensor_tensor(dst, atmp[:, 0:QB], prep[0:64, 0:QB],
                                           ALU.mult))
                    assert E.n("eV") == store_cnt[m]


def emit_stage1_wo(E):
    nc = E.nc
    NC, NOWN, SOWN, G, NB, QB, NHC = shapes()
    attnT = E.bufs["attnT"]
    wo = E.sb("s1wo", [128, 8 * D], BF16)
    zs = E.sb("s1zs", [128, QB], F32)
    psz = [E.ps("s1psz0", [128, QB], F32), E.ps("s1psz1", [128, QB], F32)]
    pso = E.ps("s1pso", [128, QB], F32)
    for k in ("s1zmm", "s1zev", "s1ztr", "s1zad"):
        E.sem(k)

    @E.block.gpsimd
    def _(g):
        E.qG.issue(g, r3(wo[:], 8), E.dram["wo"].rearrange("(c p) n -> p c n", p=128))

    @E.block.sync
    def _(sy):
        for i in range(NOWN):
            E.qS.issue(sy, E.bufs["resid"][:, i * D:(i + 1) * D],
                       E.dram["x"][i * 128:(i + 1) * 128, :])
    E.barrier()

    for oc in range(8):
        for m in range(NB):
            it = oc * NB + m
            jj = it % 2

            @E.block.tensor
            def _(te):
                if it == 0:
                    te.wait_ge(E.sem("eV"), E.n("eV"))
                if it >= 2:
                    te.wait_ge(E.sem("s1zev"), it - 1)
                for ic in range(8):
                    ins = te.matmul(psz[jj][:],
                                    wo[:, ic * D + oc * 128: ic * D + oc * 128 + 128],
                                    r3(attnT[:], 8)[:, ic, m * QB:(m + 1) * QB],
                                    start=(ic == 0), stop=(ic == 7))
                E.inc(ins, "s1zmm", 1)

            emit_zT_to_resid(E, psz[jj], zs, oc, [G * m + k for k in range(G)], QB,
                             ps_o=pso, pfx="s1z")
    E.barrier()


def alloc_stage1_bufs(E):
    NC, NOWN, SOWN, G, NB, QB, NHC = shapes()
    KB = min(512, S)
    E.sb("s1kT0", [65, S], BF16)
    E.sb("s1kT1", [65, S], BF16)
    E.sb("s1qT0", [65, SOWN], BF16)
    E.sb("s1qT1", [65, SOWN], BF16)
    E.sb("s1vv0", [128, NC * 65], BF16)
    E.sb("s1vv1", [128, NC * 65], BF16)
    E.sb("s1ws", [128, 2 * 3 * 8 * 128], BF16)
    E.sb("s1qa0", [128, KB], BF16)
    E.sb("s1qa1", [128, KB], BF16)
    E.sb("s1t1", [128, KB], BF16)
    E.sb("s1t2", [128, 2 * KB], BF16)
    E.sb("s1sqb", [64, KB], BF16)
    E.sb("s1k2row", [1, S], F32)
    E.sb("s1mxb", [1, 1], F32)
    E.sb("s1q2s", [1, QB], F32)
    E.sb("s1pt0", [128, QB], BF16)
    E.sb("s1pt1", [128, QB], BF16)
    E.sb("s1atmp", [64, QB], BF16)
    E.sb("s1sinv", [1, QB], F32)
    E.ps("s1psa", [128, KB], F32, alias="ps_a")
    E.ps("s1psb", [128, KB], F32, alias="ps_b")
    E.ps("s1psc", [128, KB], F32, alias="ps_c")
    E.ps("s1psd", [128, KB], F32, alias="ps_d")
    E.ps("s1pse", [128, KB], BF16, alias="ps_e")
    E.ps("s1psf", [65, QB], F32, alias="ps_f")


def build_prog1():
    NC, NOWN, SOWN, G, NB, QB, NHC = shapes()
    specs = [("x", (S, D), F32, "ExternalInput"),
             ("wq", (D, D), F32, "ExternalInput"),
             ("wk", (D, D), F32, "ExternalInput"),
             ("wv", (D, D), F32, "ExternalInput"),
             ("wo", (D, D), F32, "ExternalInput"),
             ("x1", (SOWN, D), F32, "ExternalOutput")]
    specs += const_specs(("idn", "ones", "perm", "cost", "sint", "dmask"))
    nc, dram = new_prog(specs)
    with ExitStack() as ctx:
        block = ctx.enter_context(nc.Block())
        E = Env(nc, ctx, block)
        E.dram = dram
        E.sb("attnT", [128, 8 * SOWN], BF16)
        alloc_ln_bufs(E)
        emit_load_constants(E, want=("idn", "ones", "perm", "cost", "sint", "dmask"))
        with ExitStack() as s1ctx:
            Einner = E
            old_ctx = E.ctx
            E.ctx = s1ctx
            E.sb("hT", [128, 8 * S], BF16)
            alloc_stage1_bufs(E)
            E.ctx = old_ctx
            E.barrier()
            emit_ln(E, [(g * 128, None, g * 128) for g in range(NC)], S,
                    src=dram["x"])
            emit_stage1(E)
            E.barrier()
        E.sb("resid", [128, NOWN * D], F32)
        emit_stage1_wo(E)
        emit_writeback(E, "x1", waits=("s1zad",))
    return nc


def alloc_stage3_bufs(E):
    NC, NOWN, SOWN, G, NB, QB, NHC = shapes()
    KB = min(512, S)
    E.sb("s3rwq", [128, H * 64], BF16)
    E.sb("s3rwk", [128, H * 64], BF16)
    E.sb("s3rwv", [128, H * 64], BF16)
    E.sb("s3gns", [64, H], F32)
    E.sb("s3gnb", [64, H], F32)
    E.sb("s3kT", [64, S], BF16)
    E.sb("s3qT", [64, SOWN], BF16)
    E.sb("s3vv", [128, NC * 65], BF16)
    E.sb("s3repq", [64, QB], BF16)
    E.sb("s3repk0", [64, KB], BF16)
    E.sb("s3repk1", [64, KB], BF16)
    E.sb("s3qfac", [1, 2 * QB], BF16)
    E.sb("s3kfct", [1, 2 * 2 * KB], BF16)
    E.sb("s3pt0", [128, QB], BF16)
    E.sb("s3pt1", [128, QB], BF16)
    E.sb("s3rets", [64, QB], F32)
    E.sb("s3sq", [64, QB], F32)
    E.sb("s3mrow", [1, QB], F32)
    E.sb("s3vtmp", [1, QB], F32)
    E.sb("s3rstd", [1, QB], F32)
    E.ps("s3psf", [65, QB], F32, alias="ps_f")
    E.ps("s3ppa", [128, min(512, S)], F32, alias="ps_pa")
    E.ps("s3ppb", [128, min(512, S)], F32, alias="ps_pb")


def emit_stage3(E):
    nc = E.nc
    NC, NOWN, SOWN, G, NB, QB, NHC = shapes()
    KB = min(512, S)
    NKB = S // KB
    hT = E.bufs["hT"]
    kT, qT, vv = E.bufs["s3kT"], E.bufs["s3qT"], E.bufs["s3vv"]
    repq, repk = E.bufs["s3repq"], [E.bufs["s3repk0"], E.bufs["s3repk1"]]
    ptile = [E.bufs["s3pt0"], E.bufs["s3pt1"]]
    rets, sq = E.bufs["s3rets"], E.bufs["s3sq"]
    mrow, vtmp, rstd = E.bufs["s3mrow"], E.bufs["s3vtmp"], E.bufs["s3rstd"]
    normT = E.bufs["s3norm"]
    pA, pB, po = E.bufs["ps_a"], E.bufs["ps_b"], E.bufs["ps_f"]
    pPa, pPb = E.bufs["ps_pa"], E.bufs["ps_pb"]
    ps2 = pB[0:1, 0:QB]
    prm = pA[0:64, 0:QB]
    prs = pB[64:128, 0:QB]
    projfree = {}
    qfac, kfct = E.bufs["s3qfac"], E.bufs["s3kfct"]
    gns, gnb = E.bufs["s3gns"], E.bufs["s3gnb"]
    ftok = {}

    def hs(h, cols, w):
        return r3(hT[:], 8)[(h % 2) * 64:(h % 2) * 64 + 64, h // 2, cols:cols + w]

    for h in range(H):
        sl = h % 2
        if h == 0:
            @E.block.gpsimd
            def _(g):
                ftok[(0, "q")] = E.qG.issue(g, qfac[0:1, 0:QB],
                                            E.dram["qfac"][0:1, 0:QB])
                ftok[(0, "k")] = E.qG.issue(g, kfct[0:1, 0:2 * KB],
                                            E.dram["kfct"][0:1, 0:2 * KB])

        jobs = ([("r", i) for i in range(3)] + [("k", gb) for gb in range(NKB)]
                + [("q", m) for m in range(NB)] + [("v", g) for g in range(NC)])
        nJ = len(jobs)
        base_T = E.n("eT")
        base_V = E.n("eV")
        mm_cnt = [base_T + 1 + i for i in range(nJ)]
        ev_cnt = [0] * nJ
        vc = base_V
        for bi, (kind, blk) in enumerate(jobs):
            vc += 2 if kind == "v" else 1
            ev_cnt[bi] = vc

        if h + 1 < H:
            @E.block.gpsimd
            def _(g):
                if h >= 1:
                    g.wait_ge(E.sem("eT"), mm_cnt[2])
                nsl = (h + 1) % 2
                ftok[(h + 1, "q")] = E.qG.issue(
                    g, qfac[0:1, nsl * QB:(nsl + 1) * QB],
                    E.dram["qfac"][0:1, (h + 1) * QB:(h + 2) * QB])
                ftok[(h + 1, "k")] = E.qG.issue(
                    g, kfct[0:1, nsl * 2 * KB:(nsl + 1) * 2 * KB],
                    E.dram["kfct"][0:1, (h + 1) * 2 * KB:(h + 2) * 2 * KB])

        @E.block.tensor
        def _(te):
            for bi, (kind, blk) in enumerate(jobs):
                jj = bi % 2
                pX = pPa if jj == 0 else pPb
                if bi == 0:
                    E.qG.wait(te, ftok[(h, "q")])
                    E.qG.wait(te, ftok[(h, "k")])
                    if h == 0:
                        te.wait_ge(E.sem("lnev"), E.n("lnev"))
                        E.qG.wait_all(te)
                if bi <= 1 and h > 0:
                    te.wait_ge(E.sem("eV"), projfree[h - 1])
                if bi >= 2:
                    te.wait_ge(E.sem("eV"), ev_cnt[bi - 2])
                b0 = (h % 2) * 64
                if kind == "r":
                    srcap = (qfac[0:1, sl * QB:sl * QB + QB] if blk == 0 else
                             kfct[0:1, sl * 2 * KB + (blk - 1) * KB:
                                  sl * 2 * KB + blk * KB])
                    w = QB if blk == 0 else KB
                    ins = te.matmul(pX[0:64, 0:w], E.bufs["onesb"][0:1, 0:64], srcap,
                                    start=True, stop=True)
                elif kind == "k":
                    ins = te.matmul(pX[0:64, 0:KB],
                                    E.bufs["s3rwk"][b0:b0 + 64, h * 64:(h + 1) * 64],
                                    hs(h, blk * KB, KB), start=True, stop=True)
                elif kind == "q":
                    ins = te.matmul(pX[0:64, 0:QB],
                                    E.bufs["s3rwq"][b0:b0 + 64, h * 64:(h + 1) * 64],
                                    hs(h, blk * QB, QB), start=True, stop=True)
                else:
                    ins = te.matmul(pX[0:128, 0:64], hs(h, blk * 128, 128),
                                    E.bufs["s3rwv"][b0:b0 + 64, h * 64:(h + 1) * 64],
                                    start=True, stop=True)
                E.tinc(ins)
                assert E.n("eT") == mm_cnt[bi]

        @E.block.vector
        def _(v):
            for bi, (kind, blk) in enumerate(jobs):
                jj = bi % 2
                pX = pPa if jj == 0 else pPb
                v.wait_ge(E.sem("eT"), mm_cnt[bi])
                if kind == "r":
                    dst = repq if blk == 0 else repk[blk - 1]
                    w = QB if blk == 0 else KB
                    ins = v.tensor_copy(dst[:, 0:w], pX[0:64, 0:w])
                elif kind == "k":
                    part = 0 if blk * KB < SOWN else 1
                    ins = v.tensor_tensor(kT[:, blk * KB:(blk + 1) * KB], pX[0:64, 0:KB],
                                          repk[part][:, 0:KB], ALU.mult)
                elif kind == "q":
                    ins = v.tensor_tensor(qT[:, blk * QB:(blk + 1) * QB], pX[0:64, 0:QB],
                                          repq[:, 0:QB], ALU.mult)
                else:
                    E.vinc(v.tensor_copy(r3(vv[:], NC)[:, blk, 0:64], pX[0:128, 0:64]))
                    with nc.allow_low_precision(reason="v_aug rowsum feeds bf16 matmul"):
                        ins = v.reduce_sum(r3(vv[:], NC)[:, blk, 64:65], pX[0:128, 0:64],
                                           AX.X)
                E.vinc(ins)
                assert E.n("eV") == ev_cnt[bi]
        projfree[h] = E.n("eV")

        base_T = E.n("eT")
        base_S = E.n("eS")
        base_V = E.n("eV")
        seqs = [[(part, c) for part in range(2) for c in range(G * m + G)]
                for m in range(NB)]
        sT_cnt = {}
        acc_cnt = {}
        sums_cnt = {}
        repm_cnt = {}
        reps_cnt = {}
        evp_cnt = {}
        mask_cnt = {}
        retc_cnt = {}
        sqm_cnt = {}
        mrow_cnt = {}
        var1_cnt = {}
        var2_cnt = {}
        eps_cnt = {}
        sub_cnt = {}
        mul_cnt = {}
        gn_cnt = {}
        ln_cnt = {}
        exp_cnt = {}
        tc, scv, vcv = base_T, base_S, base_V
        for m in range(NB):
            n = len(seqs[m])
            for si in range(n):
                tc += 1
                sT_cnt[(m, si)] = tc
                if si >= 1:
                    tc += 1
                    acc_cnt[(m, si - 1)] = tc
            tc += 1
            acc_cnt[(m, n - 1)] = tc
            tc += 1
            sums_cnt[m] = tc
            tc += 1
            repm_cnt[m] = tc
            tc += 1
            reps_cnt[m] = tc
            for si, (part, c) in enumerate(seqs[m]):
                vcv += 1
                evp_cnt[(m, si)] = vcv
                if c >= G * m:
                    vcv += 1
                    mask_cnt[(m, si)] = vcv
            vcv += 1
            retc_cnt[m] = vcv
            vcv += 1
            sqm_cnt[m] = vcv
            vcv += 1
            mrow_cnt[m] = vcv
            vcv += 1
            var1_cnt[m] = vcv
            vcv += 1
            var2_cnt[m] = vcv
            vcv += 1
            eps_cnt[m] = vcv
            scv += 1
            ln_cnt[m] = scv
            scv += 1
            exp_cnt[m] = scv
            vcv += 1
            sub_cnt[m] = vcv
            vcv += 1
            mul_cnt[m] = vcv
            vcv += 1
            gn_cnt[m] = vcv

        t_ops = []
        for m in range(NB):
            n = len(seqs[m])
            for si in range(n):
                t_ops.append(("sT", m, si))
                if si >= 1:
                    t_ops.append(("acc", m, si - 1))
            t_ops.append(("acc", m, n - 1))
            t_ops.append(("sum", m, 0))
            t_ops.append(("repm", m, 0))
            t_ops.append(("reps", m, 0))

        @E.block.tensor
        def _(te):
            for op, m, si in t_ops:
                seq = seqs[m]
                if op == "sT":
                    part, c = seq[si]
                    jp = part * NOWN + c
                    jj = si % 2
                    pS = pA if jj == 0 else pB
                    if si <= 1:
                        te.wait_ge(E.sem("eV"), gn_cnt[m - 1] if m > 0 else base_V)
                    if si >= 2:
                        te.wait_ge(E.sem("eV"), evp_cnt[(m, si - 2)])
                    E.tinc(te.matmul(pS[:, 0:QB], kT[:, jp * 128:(jp + 1) * 128],
                                     qT[:, m * QB:(m + 1) * QB], start=True, stop=True))
                    assert E.n("eT") == sT_cnt[(m, si)]
                elif op == "acc":
                    part, c = seq[si]
                    jp = part * NOWN + c
                    jj = si % 2
                    r = (part * G + (c - G * m)) if c >= G * m else -1
                    te.wait_ge(E.sem("eV"),
                               mask_cnt[(m, si)] if r >= 0 else evp_cnt[(m, si)])
                    E.tinc(te.matmul(po[:, 0:QB], r3(vv[:], NC)[:, jp, :],
                                     ptile[jj][:, 0:QB], start=(si == 0),
                                     stop=(si == len(seq) - 1)))
                    assert E.n("eT") == acc_cnt[(m, si)]
                elif op == "sum":
                    te.wait_ge(E.sem("eV"), sqm_cnt[m])
                    E.tinc(te.matmul(ps2, E.bufs["onesf"][0:64, 0:1], sq[:, 0:QB],
                                     start=True, stop=True))
                elif op == "repm":
                    te.wait_ge(E.sem("eV"), mrow_cnt[m])
                    E.tinc(te.matmul(prm, E.bufs["onesf"][0:1, 0:64], mrow[:, 0:QB],
                                     start=True, stop=True))
                else:
                    te.wait_ge(E.sem("eS"), exp_cnt[m])
                    E.tinc(te.matmul(prs, E.bufs["onesf"][0:1, 0:64], rstd[:, 0:QB],
                                     start=True, stop=True))
                    assert E.n("eT") == reps_cnt[m]

        @E.block.scalar
        def _(sc):
            for m in range(NB):
                sc.wait_ge(E.sem("eV"), eps_cnt[m])
                E.sinc(sc.activation(rstd[:, 0:QB], vtmp[:, 0:QB], AF.Ln))
                sc.wait_ge(E.sem("eS"), E.n("eS"))
                E.sinc(sc.activation(rstd[:, 0:QB], rstd[:, 0:QB], AF.Exp, scale=-0.5))

        @E.block.vector
        def _(v):
            for m in range(NB):
                seq = seqs[m]
                for si, (part, c) in enumerate(seq):
                    jj = si % 2
                    r = (part * G + (c - G * m)) if c >= G * m else -1
                    pS = pA if jj == 0 else pB
                    v.wait_ge(E.sem("eT"), sT_cnt[(m, si)])
                    if si >= 2:
                        pass
                    if si >= 2:
                        v.wait_ge(E.sem("eT"), acc_cnt[(m, si - 2)])
                    E.vinc(v.tensor_scalar(ptile[jj][:, 0:QB], pS[:, 0:QB],
                                           decay_imm(h, m, c, G), None, ALU.mult))
                    if r >= 0:
                        v.wait_ge(E.sem("eV"), E.n("eV"))
                        E.vinc(v.tensor_mul(ptile[jj][:, 0:QB], ptile[jj][:, 0:QB],
                                            E.bufs["dmaskb"][:, r * QB:(r + 1) * QB]))
                v.wait_ge(E.sem("eT"), acc_cnt[(m, len(seq) - 1)])
                E.vinc(v.tensor_copy(rets[:, 0:QB], po[0:64, 0:QB]))
                v.wait_ge(E.sem("eV"), E.n("eV"))
                E.vinc(v.tensor_mul(sq[:, 0:QB], rets[:, 0:QB], rets[:, 0:QB]))
                E.vinc(v.tensor_scalar(mrow[:, 0:QB], po[64:65, 0:QB], 1.0 / 64,
                                       None, ALU.mult))
                v.wait_ge(E.sem("eV"), E.n("eV"))
                E.vinc(v.scalar_tensor_tensor(vtmp[:, 0:QB], mrow[:, 0:QB], -1.0,
                                              mrow[:, 0:QB], ALU.mult, ALU.mult))
                v.wait_ge(E.sem("eT"), sums_cnt[m])
                v.wait_ge(E.sem("eV"), E.n("eV"))
                E.vinc(v.scalar_tensor_tensor(vtmp[:, 0:QB], ps2, 1.0 / 64,
                                              vtmp[:, 0:QB], ALU.mult, ALU.add))
                v.wait_ge(E.sem("eV"), E.n("eV"))
                E.vinc(v.tensor_scalar_add(vtmp[:, 0:QB], vtmp[:, 0:QB], EPS))
                assert E.n("eV") == eps_cnt[m]
                v.wait_ge(E.sem("eT"), repm_cnt[m])
                E.vinc(v.tensor_sub(rets[:, 0:QB], rets[:, 0:QB], prm))
                v.wait_ge(E.sem("eT"), reps_cnt[m])
                v.wait_ge(E.sem("eV"), E.n("eV"))
                E.vinc(v.tensor_mul(rets[:, 0:QB], rets[:, 0:QB], prs))
                v.wait_ge(E.sem("eV"), E.n("eV"))
                dst = normT[(h % 2) * 64:(h % 2) * 64 + 64,
                            (h // 2) * SOWN + m * QB:(h // 2) * SOWN + (m + 1) * QB]
                E.vinc(v.scalar_tensor_tensor(
                    dst, rets[:, 0:QB], gns[:, h:h + 1],
                    gnb[:, h:h + 1].broadcast_to((64, QB)),
                    ALU.mult, ALU.add))
                assert E.n("eV") == gn_cnt[m]


def emit_stage3_out(E):
    nc = E.nc
    NC, NOWN, SOWN, G, NB, QB, NHC = shapes()
    normT = E.bufs["s3norm"]
    w1 = E.sb("s3w12", [128, 8 * D], BF16)
    w2 = w1
    gsil = [E.sb("s3gsil", [128, QB], BF16), E.sb("s3gsil1", [128, QB], BF16)]
    psg = [E.bufs["ps_a"], E.bufs["ps_b"]]
    zs = E.sb("s3zs", [128, QB], F32)
    pso = E.ps("s3pso", [128, QB], F32)
    for k in ("s3zmm", "s3zev", "s3ztr", "s3zad"):
        E.sem(k)

    @E.block.gpsimd
    def _(g):
        E.qG.issue(g, r3(w1[:], 8), E.dram["w1"].rearrange("(c p) n -> p c n", p=128))
    E.barrier()

    nIT = 8 * NB
    base_T = E.n("eT")
    base_S = E.n("eS")
    base_V = E.n("eV")

    @E.block.tensor
    def _(te):
        for oc in range(8):
            for m in range(NB):
                it = oc * NB + m
                jj = it % 2
                if it >= 2:
                    te.wait_ge(E.sem("eV"), base_V + it - 1)
                for ic in range(8):
                    ins = te.matmul(psg[jj][:, 0:QB],
                                    w1[:, ic * D + oc * 128: ic * D + oc * 128 + 128],
                                    r3(E.bufs["hT"][:], 8)[:, ic, m * QB:(m + 1) * QB],
                                    start=(ic == 0), stop=(ic == 7))
                E.tinc(ins)

    @E.block.scalar
    def _(sc):
        for it in range(nIT):
            jj = it % 2
            sc.wait_ge(E.sem("eT"), base_T + it + 1)
            if it >= 2:
                sc.wait_ge(E.sem("eV"), base_V + it - 1)
            E.sinc(sc.activation(gsil[jj][:, 0:QB], psg[jj][:, 0:QB], AF.Silu))

    @E.block.vector
    def _(v):
        for oc in range(8):
            for m in range(NB):
                it = oc * NB + m
                jj = it % 2
                v.wait_ge(E.sem("eS"), base_S + it + 1)
                dst = normT[:, oc * SOWN + m * QB: oc * SOWN + (m + 1) * QB]
                E.vinc(v.tensor_add(dst, dst, gsil[jj][:, 0:QB]))
    E.barrier()

    @E.block.gpsimd
    def _(g):
        E.qG.issue(g, r3(w2[:], 8), E.dram["w2"].rearrange("(c p) n -> p c n", p=128))
    E.barrier()

    for oc in range(8):
        for m in range(NB):
            it = oc * NB + m
            jj = it % 2

            @E.block.tensor
            def _(te):
                if it == 0:
                    te.wait_ge(E.sem("eV"), E.n("eV"))
                if it >= 2:
                    te.wait_ge(E.sem("s3zev"), it - 1)
                for ic in range(8):
                    ins = te.matmul(psg[jj][:, 0:QB],
                                    w2[:, ic * D + oc * 128: ic * D + oc * 128 + 128],
                                    r3(normT[:], 8)[:, ic, m * QB:(m + 1) * QB],
                                    start=(ic == 0), stop=(ic == 7))
                E.inc(ins, "s3zmm", 1)

            emit_zT_to_resid(E, psg[jj], zs, oc, [G * m + k for k in range(G)], QB,
                             ps_o=pso, pfx="s3z")
    E.barrier()


def build_prog3():
    NC, NOWN, SOWN, G, NB, QB, NHC = shapes()
    specs = [("x2f", (S, D), F32, "ExternalInput"),
             ("rwq", (128, H * 64), F32, "ExternalInput"),
             ("rwk", (128, H * 64), F32, "ExternalInput"),
             ("rwv", (128, H * 64), F32, "ExternalInput"),
             ("w1", (D, D), F32, "ExternalInput"),
             ("w2", (D, D), F32, "ExternalInput"),
             ("gns", (64, H), F32, "ExternalInput"),
             ("gnb", (64, H), F32, "ExternalInput"),
             ("x3", (SOWN, D), F32, "ExternalOutput")]
    specs += const_specs(("idn", "ones", "dmask", "qfac", "kfct"))
    nc, dram = new_prog(specs)
    with ExitStack() as ctx:
        block = ctx.enter_context(nc.Block())
        E = Env(nc, ctx, block)
        E.dram = dram
        emit_load_constants(E, want=("idn", "ones", "dmask"))
        E.sb("hT", [128, 8 * S], BF16)
        E.sb("s3norm", [128, 8 * (S // 2)], BF16)
        E.ps("s3psa", [128, min(512, S)], F32, alias="ps_a")
        E.ps("s3psb", [128, min(512, S)], F32, alias="ps_b")
        with ExitStack() as s3ctx:
            octx = E.ctx
            E.ctx = s3ctx
            with ExitStack() as lnctx:
                E.ctx = lnctx
                alloc_ln_bufs(E)
                E.ctx = s3ctx
                E.barrier()
                emit_ln(E, [(g * 128, None, g * 128) for g in range(NC)],
                        S, src=dram["x2f"])
                E.barrier()
            alloc_stage3_bufs(E)
            E.ctx = octx

            @E.block.gpsimd
            def _(g):
                for nm in ("rwq", "rwk", "rwv"):
                    E.qG.issue(g, E.bufs["s3" + nm][:], E.dram[nm])
                E.qG.issue(g, E.bufs["s3gns"][:], E.dram["gns"])
                E.qG.issue(g, E.bufs["s3gnb"][:], E.dram["gnb"])
            emit_stage3(E)
            E.barrier()
        E.sb("resid", [128, NOWN * D], F32)

        @E.block.sync
        def _(sy):
            sy.wait_ge(E.sem("eS"), E.n("eS"))
            sy.wait_ge(E.sem("eV"), E.n("eV"))
            sy.wait_ge(E.sem("eT"), E.n("eT"))
            for i in range(NOWN):
                E.qS.issue(sy, E.bufs["resid"][:, i * D:(i + 1) * D],
                           E.dram["x2f"][i * 128:(i + 1) * 128, :])
        E.barrier()
        emit_stage3_out(E)
        emit_writeback(E, "x3", waits=("s3zad",))
    return nc


def alloc_stage4_bufs(E):
    NC, NOWN, SOWN, G, NB, QB, NHC = shapes()
    E.sb("s4xf", [128, D], F32)
    E.sb("s4xfT", [128, 8 * 128], F32)
    E.sb("s4wr", [128, 8 * 8], F32)
    E.sb("s4rb", [128, 8], F32)
    E.sb("s4probs", [128, 8], F32)
    E.sb("s4t8", [128, 8], F32)
    E.sb("s4rec", [128, 1], F32)
    E.sb("s4msk", [128, 8], F32)
    E.sb("s4wbuf", [128, NOWN * 8], F32)
    E.sb("s4bns", [128, 12], F32)
    E.sb("s4mv", [128, 2], F32)
    E.sb("s4rstd", [128, 1], F32)
    E.sb("s4nb", [128, 1], F32)
    E.sb("s4wgu", [128, 2 * 2 * 8 * 128], BF16)
    E.sb("s4wd", [128, 2 * NHC * 128], BF16)
    E.sb("s4gs0", [128, QB], BF16)
    E.sb("s4gs1", [128, QB], BF16)
    E.sb("s4zsb", [128, QB], BF16)
    E.sb("s4zsb1", [128, QB], BF16)
    E.sb("s4gu", [128, NHC * (SOWN if NB == 1 else SOWN // 2)], BF16)
    E.ps("s4psa", [128, QB], F32, alias="ps_a")
    E.ps("s4psb", [128, QB], F32, alias="ps_b")
    E.ps("s4pz0", [128, max(QB, 512)], F32)
    E.ps("s4pz1", [128, QB], F32)
    E.ps("s4ptrp", [128, QB], BF16)


def emit_stage4_router(E):
    nc = E.nc
    NC, NOWN, SOWN, G, NB, QB, NHC = shapes()
    resid = E.bufs["resid"]
    xf, xfT = E.bufs["s4xf"], E.bufs["s4xfT"]
    wr, rb = E.bufs["s4wr"], E.bufs["s4rb"]
    probs, t8, rec, msk = E.bufs["s4probs"], E.bufs["s4t8"], E.bufs["s4rec"], E.bufs["s4msk"]
    wbuf = E.bufs["s4wbuf"]
    bns, mv, rstd, nb = E.bufs["s4bns"], E.bufs["s4mv"], E.bufs["s4rstd"], E.bufs["s4nb"]
    plog = E.bufs["s4pz1"][:, 0:8]
    pxf = E.bufs["s4pz0"][:, 0:512]

    rtok = {}

    @E.block.gpsimd
    def _(g):
        rtok[0] = E.qG.issue(g, r3(wr[:], 8),
                             E.dram["wr"].rearrange("(c p) n -> p c n", p=128))

    @E.block.sync
    def _(sy):
        E.qS.issue(sy, rb[0:1, 0:7], E.dram["rbias"])

    @E.block.tensor
    def _(te):
        E.qS.wait_all(te)
        E.tinc(te.matmul(plog[:, 0:7], E.bufs["onesf"][0:1, :], rb[0:1, 0:7],
                         start=True, stop=True))

    @E.block.vector
    def _(v):
        v.wait_ge(E.sem("eT"), E.n("eT"))
        E.vinc(v.tensor_copy(rb[:, 0:7], plog[:, 0:7]))
        E.vinc(v.memset(probs[:, 7:8], -1e30))

    for i in range(NOWN):
        x = resid[:, i * D:(i + 1) * D]

        @E.block.vector
        def _(v):
            if i >= 1:
                v.wait_ge(E.sem("eS"), E.n("eS"))
                v.wait_ge(E.sem("eT"), E.n("eT"))
            v.bn_stats(bns[:, 0:6], x[:, 0:512])
            E.vinc(v.bn_stats(bns[:, 6:12], x[:, 512:1024]))
            v.wait_ge(E.sem("eV"), E.n("eV"))
            E.vinc(v.bn_aggr(mv[:], bns[:]))
            v.wait_ge(E.sem("eV"), E.n("eV"))
            E.vinc(v.tensor_scalar_add(mv[:, 1:2], mv[:, 1:2], EPS))

        @E.block.scalar
        def _(sc):
            sc.wait_ge(E.sem("eV"), E.n("eV"))
            E.sinc(sc.activation(rstd[:], mv[:, 1:2], AF.Ln))
            sc.wait_ge(E.sem("eS"), E.n("eS"))
            E.sinc(sc.activation(rstd[:], rstd[:], AF.Exp, scale=-0.5))

        @E.block.vector
        def _(v):
            v.wait_ge(E.sem("eS"), E.n("eS"))
            E.vinc(v.scalar_tensor_tensor(nb[:], mv[:, 0:1], -1.0, rstd[:],
                                          ALU.mult, ALU.mult))

        @E.block.vector
        def _(v):
            v.wait_ge(E.sem("eV"), E.n("eV"))
            E.vinc(v.scalar_tensor_tensor(xf[:], x, rstd[:],
                                          nb[:, 0:1].broadcast_to((128, D)),
                                          ALU.mult, ALU.add))

        for half in range(2):
            @E.block.tensor
            def _(te):
                te.wait_ge(E.sem("eV"), E.n("eV"))
                for k in range(4):
                    ins = te.transpose(pxf[:, k * 128:(k + 1) * 128],
                                       xf[:, half * 512 + k * 128: half * 512 + (k + 1) * 128],
                                       E.bufs["idnf"][:])
                E.tinc(ins)

            @E.block.vector
            def _(v):
                v.wait_ge(E.sem("eT"), E.n("eT"))
                E.vinc(v.tensor_copy(xfT[:, half * 512:(half + 1) * 512], pxf[:]))

        @E.block.tensor
        def _(te):
            te.wait_ge(E.sem("eV"), E.n("eV"))
            if i == 0:
                E.qG.wait(te, rtok[0])
            for ic in range(8):
                ins = te.matmul(plog[:, 0:7], xfT[:, ic * 128:(ic + 1) * 128],
                                r3(wr[:], 8)[:, ic, 0:7],
                                start=(ic == 0), stop=(ic == 7))
            E.tinc(ins)

        @E.block.vector
        def _(v):
            v.wait_ge(E.sem("eT"), E.n("eT"))
            E.vinc(v.tensor_mul(probs[:, 0:7], plog[:, 0:7], rb[:, 0:7]))

        @E.block.scalar
        def _(sc):
            sc.wait_ge(E.sem("eV"), E.n("eV"))
            E.sinc(sc.activation(probs[:, 0:7], probs[:, 0:7], AF.Exp, scale=-1.0))

        @E.block.vector
        def _(v):
            v.wait_ge(E.sem("eS"), E.n("eS"))
            E.vinc(v.tensor_scalar_add(probs[:, 0:7], probs[:, 0:7], 1.0))
            v.wait_ge(E.sem("eV"), E.n("eV"))
            E.vinc(v.reciprocal(probs[:, 0:7], probs[:, 0:7]))
            v.wait_ge(E.sem("eV"), E.n("eV"))
            E.vinc(v.max(t8[:], probs[:]))
            v.wait_ge(E.sem("eV"), E.n("eV"))
            E.vinc(v.tensor_add(rec[:], t8[:, 0:1], t8[:, 1:2]))
            v.wait_ge(E.sem("eV"), E.n("eV"))
            E.vinc(v.reciprocal(rec[:], rec[:]))
            v.wait_ge(E.sem("eV"), E.n("eV"))
            E.vinc(v.tensor_scalar(msk[:, 0:7], probs[:, 0:7], t8[:, 1:2], None, ALU.is_ge))
            v.wait_ge(E.sem("eV"), E.n("eV"))
            E.vinc(v.tensor_mul(msk[:, 0:7], msk[:, 0:7], probs[:, 0:7]))
            v.wait_ge(E.sem("eV"), E.n("eV"))
            E.vinc(v.tensor_scalar(wbuf[:, i * 8:i * 8 + 7], msk[:, 0:7], rec[:],
                                   None, ALU.mult))


def emit_stage4_experts(E):
    nc = E.nc
    NC, NOWN, SOWN, G, NB, QB, NHC = shapes()
    hT = E.bufs["hT"]
    resid = E.bufs["resid"]
    gu = E.bufs["s4gu"]
    wgu, wd = E.bufs["s4wgu"], E.bufs["s4wd"]
    gs = [E.bufs["s4gs0"], E.bufs["s4gs1"]]
    zsb = [E.bufs["s4zsb"], E.bufs["s4zsb1"]]
    pA = [E.bufs["ps_a"], E.bufs["s4pz0"][:, 0:QB]]
    pB = [E.bufs["ps_b"], E.bufs["s4pz1"]]
    pz = [E.bufs["ps_a"], E.bufs["ps_b"]]
    ptr = [E.bufs["s4ptrp"][:, 0:QB], E.bufs["lnptr0"][:, 0:QB]]
    wg3 = E.dram["wg"].rearrange("e (c p) n -> e p c n", p=128)
    wu3 = E.dram["wu"].rearrange("e (c p) n -> e p c n", p=128)
    wd3 = E.dram["wd"].rearrange("e (c p) n -> e p c n", p=128)
    wbuf = E.bufs["s4wbuf"]
    halves = [list(range(NB))] if NB == 1 else [list(range(NB // 2)),
                                               list(range(NB // 2, NB))]
    gtok = {}
    up_mm_hist = []
    dn_mm_hist = []
    first = True

    for e in range(NEXP + 1):
      for hf, tts in enumerate(halves):
        NT = len(tts)
        base_T = E.n("eT")
        base_S = E.n("eS")
        base_V = E.n("eV")
        nK = NHC * NT
        silu_cnt = lambda k: base_S + k + 1
        gu_cnt = lambda k: base_V + k + 1
        tmm = lambda k: base_T + k + 1

        @E.block.gpsimd
        def _(g):
            for hc in range(NHC):
                sl = hc % 2
                if hc < 2 and up_mm_hist:
                    g.wait_ge(E.sem("eT"), up_mm_hist[-1])
                if hc >= 2:
                    g.wait_ge(E.sem("eT"), base_T + (hc - 2) * NT + NT)
                gtok[(e, hf, hc, 0)] = E.qG.issue(
                    g, r3(wgu[:], 4 * 8)[:, sl * 16:sl * 16 + 8, :],
                    wg3[e][:, :, hc * 128:(hc + 1) * 128])
                gtok[(e, hf, hc, 1)] = E.qG.issue(
                    g, r3(wgu[:], 4 * 8)[:, sl * 16 + 8:sl * 16 + 16, :],
                    wu3[e][:, :, hc * 128:(hc + 1) * 128])

        @E.block.tensor
        def _(te):
            if first:
                te.wait_ge(E.sem("lnev"), E.n("lnev"))
            for hc in range(NHC):
                sl = hc % 2
                for lt, tt in enumerate(tts):
                    k = hc * NT + lt
                    jj = k % 2
                    if lt == 0:
                        E.qG.wait(te, gtok[(e, hf, hc, 0)])
                        E.qG.wait(te, gtok[(e, hf, hc, 1)])
                    if k >= 2:
                        te.wait_ge(E.sem("eS"), silu_cnt(k - 2))
                        te.wait_ge(E.sem("eV"), gu_cnt(k - 2))
                    elif e + hf > 0:
                        te.wait_ge(E.sem("eV"), base_V)
                    for ic in range(8):
                        te.matmul(pA[jj][:, 0:QB],
                                  r3(wgu[:], 4 * 8)[:, sl * 16 + ic, :],
                                  r3(hT[:], 8)[:, ic, tt * QB:(tt + 1) * QB],
                                  start=(ic == 0), stop=(ic == 7))
                    for ic in range(8):
                        ins = te.matmul(pB[jj][:, 0:QB],
                                        r3(wgu[:], 4 * 8)[:, sl * 16 + 8 + ic, :],
                                        r3(hT[:], 8)[:, ic, tt * QB:(tt + 1) * QB],
                                        start=(ic == 0), stop=(ic == 7))
                    E.tinc(ins)

        @E.block.scalar
        def _(sc):
            for hc in range(NHC):
                for lt in range(NT):
                    k = hc * NT + lt
                    jj = k % 2
                    sc.wait_ge(E.sem("eT"), tmm(k))
                    if k >= 2:
                        sc.wait_ge(E.sem("eV"), gu_cnt(k - 2))
                    E.sinc(sc.activation(gs[jj][:, 0:QB], pA[jj][:, 0:QB], AF.Silu))

        @E.block.vector
        def _(v):
            for hc in range(NHC):
                for lt in range(NT):
                    k = hc * NT + lt
                    jj = k % 2
                    v.wait_ge(E.sem("eS"), silu_cnt(k))
                    E.vinc(v.tensor_tensor(
                        r3(gu[:], NHC)[:, hc, lt * QB:(lt + 1) * QB],
                        gs[jj][:, 0:QB], pB[jj][:, 0:QB], ALU.mult))
        up_mm_hist.append(E.n("eT"))
        first = False

        base_T = E.n("eT")
        base_V = E.n("eV")
        gu_final = base_V
        nDK = 8 * NT
        dn_mm = lambda k: base_T + 2 * k + 1 - (1 if k == 0 else 0)
        t_cnt = [0] * (nDK + 1)
        cnt = base_T
        mm_cnt = [0] * nDK
        tr_cnt = [0] * nDK
        for k in range(nDK):
            cnt += 1
            mm_cnt[k] = cnt
            if k >= 1:
                cnt += 1
                tr_cnt[k - 1] = cnt
        cnt += 1
        tr_cnt[nDK - 1] = cnt
        evac_cnt = [0] * nDK
        vcnt = base_V
        for k in range(nDK):
            vcnt += 1
            evac_cnt[k] = vcnt
            if k >= 1:
                vcnt += 1
        add_last = vcnt + 1

        @E.block.gpsimd
        def _(g):
            for oc in range(8):
                slo = oc % 2
                if oc < 2 and dn_mm_hist:
                    g.wait_ge(E.sem("eT"), dn_mm_hist[-1])
                if oc >= 2:
                    g.wait_ge(E.sem("eT"), mm_cnt[(oc - 2) * NT + NT - 1])
                gtok[("d", e, hf, oc)] = E.qG.issue(
                    g, r3(wd[:], 2 * NHC)[:, slo * NHC:slo * NHC + NHC, :],
                    wd3[e][:, :, oc * 128:(oc + 1) * 128])

        @E.block.tensor
        def _(te):
            for oc in range(8):
                slo = oc % 2
                for lt, tt in enumerate(tts):
                    k = oc * NT + lt
                    jj = k % 2
                    if lt == 0:
                        E.qG.wait(te, gtok[("d", e, hf, oc)])
                        if oc == 0:
                            te.wait_ge(E.sem("eV"), gu_final)
                    if k >= 2:
                        te.wait_ge(E.sem("eV"), evac_cnt[k - 2])
                    for hc in range(NHC):
                        ins = te.matmul(pz[jj][:, 0:QB],
                                        r3(wd[:], 2 * NHC)[:, slo * NHC + hc, :],
                                        r3(gu[:], NHC)[:, hc, lt * QB:(lt + 1) * QB],
                                        start=(hc == 0), stop=(hc == NHC - 1))
                    E.tinc(ins)
                    assert E.n("eT") == mm_cnt[k]
                    if k >= 1:
                        j2 = (k - 1) % 2
                        te.wait_ge(E.sem("eV"), evac_cnt[k - 1])
                        for q in range(G):
                            ins = te.transpose(ptr[j2][:, q * 128:(q + 1) * 128],
                                               zsb[j2][:, q * 128:(q + 1) * 128],
                                               E.bufs["idnb"][:])
                        E.tinc(ins)
            j2 = (nDK - 1) % 2
            te.wait_ge(E.sem("eV"), evac_cnt[nDK - 1])
            for q in range(G):
                ins = te.transpose(ptr[j2][:, q * 128:(q + 1) * 128],
                                   zsb[j2][:, q * 128:(q + 1) * 128],
                                   E.bufs["idnb"][:])
            E.tinc(ins)
            assert E.n("eT") == tr_cnt[nDK - 1]

        @E.block.vector
        def _(v):
            def adds(k):
                j2 = k % 2
                oc = k // NT
                tt = tts[k % NT]
                v.wait_ge(E.sem("eT"), tr_cnt[k])
                for q in range(G):
                    ci = G * tt + q
                    dst = resid[:, ci * D + oc * 128: ci * D + oc * 128 + 128]
                    if e == NEXP:
                        ins = v.tensor_add(dst, dst, ptr[j2][:, q * 128:(q + 1) * 128])
                    else:
                        ins = v.scalar_tensor_tensor(
                            dst, ptr[j2][:, q * 128:(q + 1) * 128],
                            wbuf[:, ci * 8 + e:ci * 8 + e + 1],
                            dst, ALU.mult, ALU.add)
                E.vinc(ins)

            for k in range(nDK):
                jj = k % 2
                v.wait_ge(E.sem("eT"), mm_cnt[k])
                E.vinc(v.tensor_copy(zsb[jj][:, 0:QB], pz[jj][:, 0:QB]))
                assert E.n("eV") == evac_cnt[k]
                if k >= 1:
                    adds(k - 1)
            adds(nDK - 1)
            assert E.n("eV") == add_last
        dn_mm_hist.append(E.n("eT"))


def build_prog4():
    NC, NOWN, SOWN, G, NB, QB, NHC = shapes()
    specs = [("xin", (SOWN, D), F32, "ExternalInput"),
             ("wg", (NEXP + 1, D, HID), F32, "ExternalInput"),
             ("wu", (NEXP + 1, D, HID), F32, "ExternalInput"),
             ("wd", (NEXP + 1, HID, D), F32, "ExternalInput"),
             ("wr", (D, 8), F32, "ExternalInput"),
             ("rbias", (1, 7), F32, "ExternalInput"),
             ("x4", (SOWN, D), F32, "ExternalOutput")]
    specs += const_specs(("idn", "ones"))
    nc, dram = new_prog(specs)
    with ExitStack() as ctx:
        block = ctx.enter_context(nc.Block())
        E = Env(nc, ctx, block)
        E.dram = dram
        E.sb("resid", [128, NOWN * D], F32)
        E.sb("hT", [128, 8 * SOWN], BF16)
        alloc_ln_bufs(E)
        alloc_stage4_bufs(E)
        emit_load_constants(E, want=("idn", "ones"))
        E.barrier()
        emit_ln(E, [(i * 128, i, i * 128) for i in range(NOWN)], SOWN,
                src=dram["xin"])
        emit_stage4_router(E)
        E.barrier()
        emit_stage4_experts(E)
        E.barrier()
        emit_writeback(E, "x4", waits=("eV",))
    return nc


def emit_writeback(E, name, waits=()):
    NC, NOWN, SOWN, G, NB, QB, NHC = shapes()
    resid = E.bufs["resid"]

    @E.block.sync
    def _(sy):
        for s in waits:
            sy.wait_ge(E.sem(s), E.n(s))
        for i in range(NOWN):
            E.qS.issue(sy, E.dram[name][i * 128:(i + 1) * 128, :],
                       resid[:, i * D:(i + 1) * D])
        E.qS.wait_all(sy)


def new_prog(dram_specs):
    nc = bass.Bass("TRN2", target_bir_lowering=False)
    dram = {}
    for nm, shp, dt, kind in dram_specs:
        dram[nm] = nc.dram_tensor(nm, list(shp), dt, kind=kind).ap()
    return nc, dram


def const_specs(names):
    cs = host_constants(0)
    return [(k, cs[k].shape, F32, "ExternalInput") for k in names]


def build_prog2():
    NC, NOWN, SOWN, G, NB, QB, NHC = shapes()
    specs = [("xin", (SOWN, D), F32, "ExternalInput"),
             ("w0", (D, D), F32, "ExternalInput"),
             ("x2", (SOWN, D), F32, "ExternalOutput")]
    specs += const_specs(("idn", "ones"))
    nc, dram = new_prog(specs)
    with ExitStack() as ctx:
        block = ctx.enter_context(nc.Block())
        E = Env(nc, ctx, block)
        E.dram = dram
        E.sb("resid", [128, NOWN * D], F32)
        E.sb("hT", [128, 8 * SOWN], BF16)
        alloc_ln_bufs(E)
        emit_load_constants(E, want=("idn", "ones"))
        E.barrier()
        emit_ln(E, [(i * 128, i, i * 128) for i in range(NOWN)], SOWN,
                src=dram["xin"])
        emit_stage2(E)
        emit_writeback(E, "x2", waits=("s2zad",))
    return nc


U32 = mybir.dt.uint32


def gather_index(parity):
    NC, NOWN, SOWN, G, NB, QB, NHC = shapes()
    gi = np.zeros((128, NC), np.uint32)
    qi = np.arange(128, dtype=np.uint32)
    for g in range(NC):
        if g < NOWN:
            gi[:, g] = parity * SOWN + g * 128 + qi
        else:
            gi[:, g] = (1 - parity) * SOWN + (g - NOWN) * 128 + qi
    return gi


def build_prog_full():
    NC, NOWN, SOWN, G, NB, QB, NHC = shapes()
    specs = [("x", (S, D), F32, "ExternalInput"),
             ("wq", (D, D), F32, "ExternalInput"),
             ("wk", (D, D), F32, "ExternalInput"),
             ("wv", (D, D), F32, "ExternalInput"),
             ("wo", (D, D), F32, "ExternalInput"),
             ("w0", (D, D), F32, "ExternalInput"),
             ("rwq", (128, H * 64), F32, "ExternalInput"),
             ("rwk", (128, H * 64), F32, "ExternalInput"),
             ("rwv", (128, H * 64), F32, "ExternalInput"),
             ("w1", (D, D), F32, "ExternalInput"),
             ("w2", (D, D), F32, "ExternalInput"),
             ("gns", (64, H), F32, "ExternalInput"),
             ("gnb", (64, H), F32, "ExternalInput"),
             ("wg", (NEXP + 1, D, HID), F32, "ExternalInput"),
             ("wu", (NEXP + 1, D, HID), F32, "ExternalInput"),
             ("wd", (NEXP + 1, HID, D), F32, "ExternalInput"),
             ("wr", (D, 8), F32, "ExternalInput"),
             ("rbias", (1, 7), F32, "ExternalInput"),
             ("gidx", (128, NC), U32, "ExternalInput"),
             ("y", (SOWN, D), F32, "ExternalOutput")]
    specs += const_specs(("idn", "ones", "perm", "cost", "sint", "dmask",
                          "qfac", "kfct"))
    nc, dram = new_prog(specs)
    cx2 = nc.dram_tensor("cx2", [SOWN, D], F32, kind="Internal").ap()
    gx2 = nc.dram_tensor("gx2", [S, D], F32, kind="Internal").ap()
    with ExitStack() as ctx:
        block = ctx.enter_context(nc.Block())
        E = Env(nc, ctx, block)
        E.dram = dram
        alloc_ln_bufs(E)
        emit_load_constants(E, want=("idn", "ones", "dmask"))
        E.sb("gidxb", [128, NC], U32)

        @E.block.gpsimd
        def _(g):
            E.qG.issue(g, E.bufs["gidxb"][:], dram["gidx"])

        with ExitStack() as epochA:
            with ExitStack() as s1o:
                E.ctx = s1o
                E.sb("attnT", [128, 8 * SOWN], BF16)
                emit_load_constants(E, want=("perm", "cost", "sint"))
                with ExitStack() as s1i:
                    E.ctx = s1i
                    E.sb("hT1", [128, 8 * S], BF16, alias="hT")
                    alloc_stage1_bufs(E)
                    E.ctx = s1o
                    E.barrier()
                    with nc.named_scope("s1ln"):
                        emit_ln(E, [(g * 128, None, g * 128) for g in range(NC)], S,
                                src=dram["x"])
                    with nc.named_scope("s1attn"):
                        emit_stage1(E)
                    E.barrier()
                E.ctx = epochA
                E.sb("residA", [128, NOWN * D], F32, alias="resid", side="right")
                E.ctx = s1o
                with nc.named_scope("s1wo"):
                    emit_stage1_wo(E)
                E.barrier()
                E.ctx = ctx

            with ExitStack() as s2o:
                E.ctx = s2o
                E.sb("hT2", [128, 8 * SOWN], BF16, alias="hT")
                with nc.named_scope("s2"):
                    emit_ln(E, [(None, i, i * 128) for i in range(NOWN)], SOWN,
                            pre_wait=("s1zad",))
                    emit_stage2(E)
                E.ctx = ctx

            @E.block.sync
            def _(sy):
                sy.wait_ge(E.sem("eV"), E.n("eV"))
                sy.wait_ge(E.sem("eS"), E.n("eS"))
                sy.wait_ge(E.sem("eT"), E.n("eT"))
                sy.wait_ge(E.sem("s2zad"), E.n("s2zad"))
                for i in range(NOWN):
                    E.qS.issue(sy, cx2[i * 128:(i + 1) * 128, :],
                               E.bufs["resid"][:, i * D:(i + 1) * D])

            @E.block.gpsimd
            def _(g):
                E.qS.wait_all(g)
                nc.enter_named_scope("xchg") if False else None
                g.collective_compute(
                    "AllGather", ALU.bypass,
                    replica_groups=[[0, 1], [2, 3], [4, 5], [6, 7]],
                    ins=[cx2.opt()], outs=[gx2.opt()]).then_inc(E.sem("cc"))
                E.counts["cc"] += 1
                g.wait_ge(E.sem("cc"), E.counts["cc"])
            E.barrier()

        with ExitStack() as epochB:
            with ExitStack() as s3o:
                E.ctx = s3o
                E.sb("hT3", [128, 8 * S], BF16, alias="hT")
                E.sb("s3norm", [128, 8 * SOWN], BF16)
                E.ps("s3psa", [128, min(512, S)], F32, alias="ps_a")
                E.ps("s3psb", [128, min(512, S)], F32, alias="ps_b")
                with ExitStack() as s3i:
                    E.ctx = s3i
                    alloc_stage3_bufs(E)
                    E.ctx = s3o

                    @E.block.gpsimd
                    def _(g):
                        for nm in ("rwq", "rwk", "rwv"):
                            E.qG.issue(g, E.bufs["s3" + nm][:], E.dram[nm])
                        E.qG.issue(g, E.bufs["s3gns"][:], E.dram["gns"])
                        E.qG.issue(g, E.bufs["s3gnb"][:], E.dram["gnb"])
                    with nc.named_scope("s3ln"):
                        emit_ln(E, [(g * 128, None, g * 128) for g in range(NC)], S,
                                src=("indirect", gx2, E.bufs["gidxb"]))
                    with nc.named_scope("s3ret"):
                        emit_stage3(E)
                    E.barrier()
                E.ctx = epochB
                E.sb("residB", [128, NOWN * D], F32, alias="resid", side="right")
                E.ctx = s3o

                @E.block.sync
                def _(sy):
                    sy.wait_ge(E.sem("eS"), E.n("eS"))
                    sy.wait_ge(E.sem("eV"), E.n("eV"))
                    sy.wait_ge(E.sem("eT"), E.n("eT"))
                    for i in range(NOWN):
                        E.qS.issue(sy, E.bufs["resid"][:, i * D:(i + 1) * D],
                                   cx2[i * 128:(i + 1) * 128, :])
                E.barrier()
                with nc.named_scope("s3out"):
                    emit_stage3_out(E)
                E.barrier()
                E.ctx = ctx

            with ExitStack() as s4o:
                E.ctx = s4o
                E.sb("hT4", [128, 8 * SOWN], BF16, alias="hT")
                alloc_stage4_bufs(E)
                with nc.named_scope("s4rt"):
                    emit_ln(E, [(None, i, i * 128) for i in range(NOWN)], SOWN,
                            pre_wait=("s3zad",))
                    emit_stage4_router(E)
                E.barrier()
                with nc.named_scope("s4moe"):
                    emit_stage4_experts(E)
                E.barrier()
                E.ctx = ctx
            emit_writeback(E, "y", waits=("eV",))
    return nc


def kernel_fused(**inputs):
    NC, NOWN, SOWN, G, NB, QB, NHC = shapes()
    w = _prep_host(inputs)
    nc = build_prog_full()
    maps = []
    for c in range(NCORES):
        p = c & 1
        m = dict(host_constants(p))
        m["gidx"] = gather_index(p)
        m["x"] = np.ascontiguousarray(w["x"][c // 2][packed_positions(p)])
        for k in ("wq", "wk", "wv", "wo", "w0", "rwq", "rwk", "rwv", "w1", "w2",
                  "gns", "gnb", "wg", "wu", "wd", "wr", "rbias"):
            m[k] = w[k]
        maps.append(m)
    res = run_bass_kernel_spmd(nc, maps, core_ids=list(range(NCORES)))
    out = np.empty((B, S, D), np.float32)
    for c in range(NCORES):
        out[c // 2][packed_positions(c & 1)[:SOWN]] = res.results[c]["y"]
    return out


def _prep_host(inputs):
    w = {}
    w["x"] = np.asarray(inputs["L0_attn_x"])
    w["wq"] = np.ascontiguousarray(np.asarray(inputs["L0_attn_Wq"]).transpose(1, 0, 2).reshape(D, D))
    w["wk"] = np.ascontiguousarray(np.asarray(inputs["L0_attn_Wk"]).transpose(1, 0, 2).reshape(D, D))
    w["wv"] = np.ascontiguousarray(np.asarray(inputs["L0_attn_Wv"]).transpose(1, 0, 2).reshape(D, D))
    w["wo"] = np.ascontiguousarray(np.asarray(inputs["L0_attn_Wo"]))
    w["w0"] = np.ascontiguousarray(np.asarray(inputs["L0_ffn_weight"]))
    for nm, key in (("rwq", "L1_attn_wq"), ("rwk", "L1_attn_wk"), ("rwv", "L1_attn_wv")):
        a = np.asarray(inputs[key]).transpose(1, 0, 2).reshape(64, H * 64)
        w[nm] = np.ascontiguousarray(np.tile(a, (2, 1)))
    w["w1"] = np.ascontiguousarray(np.asarray(inputs["L1_attn_w1"]))
    w["w2"] = np.ascontiguousarray(np.asarray(inputs["L1_attn_w2"]))
    w["gns"] = np.ascontiguousarray(np.asarray(inputs["L1_attn_gn_scale"]).reshape(H, 64).T)
    w["gnb"] = np.ascontiguousarray(np.asarray(inputs["L1_attn_gn_bias"]).reshape(H, 64).T)
    w["wg"] = np.ascontiguousarray(np.concatenate(
        [np.asarray(inputs["L1_ffn_routed_gate"]),
         np.asarray(inputs["L1_ffn_shared_gate"])[None]], 0))
    w["wu"] = np.ascontiguousarray(np.concatenate(
        [np.asarray(inputs["L1_ffn_routed_up"]),
         np.asarray(inputs["L1_ffn_shared_up"])[None]], 0))
    w["wd"] = np.ascontiguousarray(np.concatenate(
        [np.asarray(inputs["L1_ffn_routed_down"]),
         np.asarray(inputs["L1_ffn_shared_down"])[None]], 0))
    wr = np.asarray(inputs["L1_ffn_Wr"])
    wrp = np.zeros((D, 8), np.float32)
    wrp[:, 0:7] = wr
    w["wr"] = wrp
    w["rbias"] = np.ascontiguousarray(np.asarray(inputs["L1_ffn_routing_bias"]).reshape(1, 7))
    return w


def kernel(**inputs):
    return kernel_fused(**inputs)


def kernel_multilaunch(**inputs):
    NC, NOWN, SOWN, G, NB, QB, NHC = shapes()
    w = _prep_host(inputs)
    consts = [host_constants(c & 1) for c in range(NCORES)]
    packs = [packed_positions(c & 1) for c in range(NCORES)]

    def cmaps(names, extra):
        maps = []
        for c in range(NCORES):
            m = {k: consts[c][k] for k in names}
            m.update(extra(c))
            maps.append(m)
        return maps

    nc1 = build_prog1()
    maps1 = cmaps(("idn", "ones", "perm", "cost", "sint", "dmask"), lambda c: {
        "x": np.ascontiguousarray(w["x"][c // 2][packs[c]]),
        "wq": w["wq"], "wk": w["wk"], "wv": w["wv"], "wo": w["wo"]})
    res1 = run_bass_kernel_spmd(nc1, maps1, core_ids=list(range(NCORES)))
    x1own = [res1.results[c]["x1"] for c in range(NCORES)]

    nc2 = build_prog2()
    maps2 = cmaps(("idn", "ones"), lambda c: {"xin": x1own[c], "w0": w["w0"]})
    res2 = run_bass_kernel_spmd(nc2, maps2, core_ids=list(range(NCORES)))
    x2own = [res2.results[c]["x2"] for c in range(NCORES)]

    nc3 = build_prog3()
    maps3 = cmaps(("idn", "ones", "dmask", "qfac", "kfct"), lambda c: {
        "x2f": np.ascontiguousarray(np.concatenate([x2own[c], x2own[c ^ 1]], 0)),
        "rwq": w["rwq"], "rwk": w["rwk"], "rwv": w["rwv"],
        "w1": w["w1"], "w2": w["w2"], "gns": w["gns"], "gnb": w["gnb"]})
    res3 = run_bass_kernel_spmd(nc3, maps3, core_ids=list(range(NCORES)))
    x3own = [res3.results[c]["x3"] for c in range(NCORES)]

    nc4 = build_prog4()
    maps4 = cmaps(("idn", "ones"), lambda c: {
        "xin": x3own[c], "wg": w["wg"], "wu": w["wu"], "wd": w["wd"],
        "wr": w["wr"], "rbias": w["rbias"]})
    res4 = run_bass_kernel_spmd(nc4, maps4, core_ids=list(range(NCORES)))

    out = np.empty((B, S, D), np.float32)
    for c in range(NCORES):
        out[c // 2][packs[c][:SOWN]] = res4.results[c]["x4"]
    return out


# version 16
# speedup vs baseline: 1.5688x; 1.0851x over previous
import numpy as np
import ml_dtypes
import concourse.bass as bass
import concourse.mybir as mybir
from concourse.bass_utils import run_bass_kernel_spmd
from contextlib import ExitStack

F32 = mybir.dt.float32
BF16 = mybir.dt.bfloat16
AF = mybir.ActivationFunctionType
ALU = mybir.AluOpType
AX = mybir.AxisListType
BF = ml_dtypes.bfloat16

B = 4
S = 4096
D = 1024
H = 16
DK = 64
HID = 2816
NEXP = 7
NCORES = 8
EPS = 1e-5


def shapes():
    NC = S // 128
    NOWN = NC // 2
    SOWN = NOWN * 128
    G = min(4, NOWN)
    NB = NOWN // G
    QB = G * 128
    NHC = HID // 128
    return NC, NOWN, SOWN, G, NB, QB, NHC


def gammas_np():
    lin = np.linspace(np.log(1.0 / 512.0), np.log(1.0 / 32.0), H)
    return 1.0 - np.exp(lin)


def packed_positions(parity):
    NC, NOWN, SOWN, G, NB, QB, NHC = shapes()
    own = np.concatenate([np.arange((parity + 2 * i) * 128, (parity + 2 * i) * 128 + 128)
                          for i in range(NOWN)])
    sib = np.concatenate([np.arange((1 - parity + 2 * i) * 128,
                                    (1 - parity + 2 * i) * 128 + 128)
                          for i in range(NOWN)])
    return np.concatenate([own, sib])


def host_constants(parity):
    NC, NOWN, SOWN, G, NB, QB, NHC = shapes()
    c = {}
    pos = np.arange(S, dtype=np.float64)
    inv = 10000.0 ** (-(np.arange(0, DK, 2, dtype=np.float64) / DK))
    ang = pos[:, None] * inv[None, :]
    cosr = np.repeat(np.cos(ang), 2, axis=-1).T
    sinr = np.repeat(np.sin(ang), 2, axis=-1).T
    pk = packed_positions(parity)
    c["cost"] = np.ascontiguousarray(np.tile(cosr, (2, 1))[:, pk]).astype(np.float32)
    c["sint"] = np.ascontiguousarray(np.tile(sinr, (2, 1))[:, pk]).astype(np.float32)
    P = np.zeros((128, 128), np.float32)
    for i in range(64):
        P[2 * i, 2 * i + 1] = -1.0
        P[2 * i + 1, 2 * i] = 1.0
    c["perm"] = np.ascontiguousarray(P.T)
    c["idn"] = np.eye(128, dtype=np.float32)
    c["ones"] = np.ones((128, 128), np.float32)
    dm = np.zeros((128, (2 * G) * QB), np.float32)
    ki = np.arange(128)[:, None]
    qi = np.arange(128)[None, :]
    for part in range(2):
        for r in range(G):
            for i in range(G):
                blk = dm[:, (part * G + r) * QB + i * 128:
                         (part * G + r) * QB + (i + 1) * 128]
                if part == 0:
                    if r < i:
                        blk[:] = 1.0
                    elif r == i:
                        blk[:] = (qi >= ki).astype(np.float32)
                else:
                    keep = (r < i) if parity == 0 else (r <= i)
                    if keep:
                        blk[:] = 1.0
    c["dmask"] = dm
    g = gammas_np()
    j = np.arange(QB)
    qoff = (256 * (j // 128) + (j % 128)).astype(np.float64)
    c["qfac"] = np.exp(np.log(g)[:, None] * qoff[None, :]).reshape(1, H * QB).astype(np.float32)
    KB = min(512, S)
    kio = np.arange(128, dtype=np.float64)
    own_e = np.tile(-kio, KB // 128)
    sib_e = np.tile(-kio + (128.0 * (2 * parity - 1)), KB // 128)
    c["kfct"] = np.exp(np.log(g)[:, None]
                       * np.concatenate([own_e, sib_e])[None, :]
                       ).reshape(1, H * 2 * KB).astype(np.float32)
    return c


def decay_imm(h, m, c, G):
    g = gammas_np()[h]
    v = np.exp(np.log(g) * 256.0 * (G * m - c))
    v32 = np.float32(v)
    if not np.isfinite(v32):
        v32 = np.float32(0.0)
    return float(v32)


class DQ:

    def __init__(self, E, name, n=4):
        self.E = E
        self.name = name
        self.n = n
        self.counts = [0] * n
        self.idx = 0
        for i in range(n):
            E.sem(f"{name}{i}")

    def issue(self, eng, out, in_):
        k = self.idx % self.n
        self.idx += 1
        sem = self.E.sem(f"{self.name}{k}")
        if self.counts[k] > 0:
            eng.wait_ge(sem, 16 * self.counts[k])
        eng.dma_start(out=out, in_=in_).then_inc(sem, 16)
        self.counts[k] += 1
        return (k, self.counts[k])

    def issue_indirect(self, eng, out, in_, in_offset):
        k = self.idx % self.n
        self.idx += 1
        sem = self.E.sem(f"{self.name}{k}")
        if self.counts[k] > 0:
            eng.wait_ge(sem, 16 * self.counts[k])
        eng.indirect_dma_start(out=out, out_offset=None, in_=in_,
                               in_offset=in_offset).then_inc(sem, 16)
        self.counts[k] += 1
        return (k, self.counts[k])

    def wait(self, eng, token):
        k, c = token
        eng.wait_ge(self.E.sem(f"{self.name}{k}"), 16 * c)

    def wait_all(self, eng):
        for k in range(self.n):
            if self.counts[k]:
                eng.wait_ge(self.E.sem(f"{self.name}{k}"), 16 * self.counts[k])


class Env:
    def __init__(self, nc, ctx, block):
        self.nc = nc
        self.ctx = ctx
        self.block = block
        self.sems = {}
        self.counts = {}
        self.barn = 0
        self.bufs = {}
        self._qS = None
        self._qG = None

    @property
    def qS(self):
        if self._qS is None:
            self._qS = DQ(self, "qS", 4)
        return self._qS

    @property
    def qG(self):
        if self._qG is None:
            self._qG = DQ(self, "qG", 4)
        return self._qG

    def sem(self, name):
        if name not in self.sems:
            self.sems[name] = self.ctx.enter_context(self.nc.semaphore(name))
            self.counts[name] = 0
        return self.sems[name]

    def inc(self, instr, name, k=1):
        instr.then_inc(self.sem(name), k)
        self.counts[name] += k
        return self.counts[name]

    def n(self, name):
        self.sem(name)
        return self.counts[name]

    def sb(self, name, shape, dtype, alias=None, side=None):
        if side is None:
            t = self.ctx.enter_context(self.nc.sbuf_tensor(name, shape, dtype))
        else:
            t = self.ctx.enter_context(self.nc.sbuf_tensor(name, shape, dtype, side=side))
        self.bufs[alias or name] = t
        return t

    def ps(self, name, shape, dtype=F32, alias=None):
        t = self.ctx.enter_context(self.nc.psum_tensor(name, shape, dtype))
        self.bufs[alias or name] = t
        return t

    def tinc(self, ins):
        return self.inc(ins, "eT", 1)

    def vinc(self, ins):
        return self.inc(ins, "eV", 1)

    def sinc(self, ins):
        return self.inc(ins, "eS", 1)

    def ginc(self, ins):
        return self.inc(ins, "eG", 1)

    def barrier(self):
        self.barn += 1
        bar = self.sem("bar")

        @self.block.sync
        def _(e):
            if self._qS is not None:
                self._qS.wait_all(e)
            e.nop().then_inc(bar, 1)

        @self.block.gpsimd
        def _(e):
            if self._qG is not None:
                self._qG.wait_all(e)
            e.nop().then_inc(bar, 1)

        for engf in (self.block.tensor, self.block.vector, self.block.scalar):
            @engf
            def _(e):
                e.nop().then_inc(bar, 1)
        tgt = 5 * self.barn
        self.counts["bar"] = tgt
        for engf in (self.block.tensor, self.block.vector, self.block.scalar,
                     self.block.gpsimd, self.block.sync):
            @engf
            def _(e):
                e.wait_ge(bar, tgt)


def r3(ap, c):
    return ap.rearrange("p (c w) -> p c w", c=c)


def emit_load_constants(E, want=("idn", "ones")):
    nc = E.nc
    g = None

    @E.block.gpsimd
    def _(gp):
        for nm in want:
            src = E.dram[nm]
            sh = list(src.shape)
            if nm == "idn":
                E.qG.issue(gp, E.sb("idnf", [128, 128], F32)[:], src)
                E.qG.issue(gp, E.sb("idnb", [128, 128], BF16)[:], src)
            elif nm == "ones":
                E.qG.issue(gp, E.sb("onesf", [128, 128], F32)[:], src)
                E.qG.issue(gp, E.sb("onesb", [128, 128], BF16)[:], src)
            elif nm == "perm":
                E.qG.issue(gp, E.sb("permb", [128, 128], BF16)[:], src)
            elif nm in ("cost", "sint"):
                E.qG.issue(gp, E.sb(nm + "b", sh, BF16)[:], src)
            elif nm == "dmask":
                E.qG.issue(gp, E.sb("dmaskb", sh, BF16)[:], src)
            elif nm in ("qfac", "kfct"):
                E.qG.issue(gp, E.sb(nm + "f", sh, F32)[:], src)


def alloc_ln_bufs(E, fp32_keep_n=0):
    E.sb("lnx0", [128, D], F32)
    E.sb("lnx1", [128, D], F32)
    E.sb("lnbns", [128, 24], F32)
    E.sb("lnmv", [128, 4], F32)
    E.sb("lnrstd", [128, 2], F32)
    E.sb("lnnbias", [128, 2], F32)
    E.sb("lnxn", [128, 2 * D], BF16)
    E.ps("lnptr0", [128, 1024], BF16)
    E.ps("lnptr1", [128, 1024], BF16)
    if fp32_keep_n:
        E.sb("xnf32", [128, fp32_keep_n * D], F32)


def emit_ln(E, chunks, hT_W, src=None, fp32_keep=False, pre_wait=()):
    assert not fp32_keep
    """LN chunks into feature-major bf16 hT (and optionally fp32 residual /
    fp32 normalized copies). chunks: list of (src_row|None, resid_slot|None,
    hT_col)."""
    nc = E.nc
    hT = E.bufs["hT"]
    resid = E.bufs.get("resid")
    xt = [E.bufs["lnx0"], E.bufs["lnx1"]]
    bns, mv = E.bufs["lnbns"], E.bufs["lnmv"]
    rstd, nbias, xn = E.bufs["lnrstd"], E.bufs["lnnbias"], E.bufs["lnxn"]
    ptr = [E.bufs["lnptr0"], E.bufs["lnptr1"]]
    ld, sv, ss, tt, ev = "lnld", "lnsv", "lnss", "lntt", "lnev"
    base = {k: E.n(k) for k in (ld, sv, ss, tt, ev)}

    indirect = isinstance(src, tuple) and src and src[0] == "indirect"

    def xsrc_of(i):
        row, slot, _hc = chunks[i]
        if slot is not None and not indirect:
            return resid[:, slot * D:(slot + 1) * D]
        if row is None and slot is not None:
            return resid[:, slot * D:(slot + 1) * D]
        return xt[i % 2][:]

    tokens = {}
    if src is not None and isinstance(src, tuple) and src[0] == "indirect":
        _, gsrc, gidx_sb = src

        @E.block.gpsimd
        def _(g):
            import bass_rust
            for i, (row, slot, _hc) in enumerate(chunks):
                if slot is None and i >= 2:
                    g.wait_ge(E.sem(sv), base[sv] + 5 * (i - 1))
                tokens[i] = E.qG.issue_indirect(
                    g, xsrc_of(i), gsrc,
                    bass_rust.IndirectOffsetOnAxis(gidx_sb[:, i:i + 1], 0))
        qwait = E.qG
    elif src is not None:
        @E.block.sync
        def _(sy):
            for i, (row, slot, _hc) in enumerate(chunks):
                if row is None:
                    continue
                if slot is None and i >= 2:
                    sy.wait_ge(E.sem(sv), base[sv] + 5 * (i - 1))
                tokens[i] = E.qS.issue(sy, xsrc_of(i), src[row:row + 128, :])
        qwait = E.qS
    else:
        qwait = E.qS

    n = len(chunks)

    @E.block.vector
    def _(v):
        for s_ in pre_wait:
            v.wait_ge(E.sem(s_), E.n(s_))

        def emit_evac(i):
            j = i % 2
            hc = chunks[i][2]
            v.wait_ge(E.sem(tt), base[tt] + i + 1)
            E.counts[ev] += 1
            v.tensor_copy(r3(hT[:], 8)[:, :, hc:hc + 128],
                          r3(ptr[j][:], 8)).then_inc(E.sem(ev), 1)

        for i, (row, slot, _hc) in enumerate(chunks):
            j = i % 2
            if i in tokens:
                qwait.wait(v, tokens[i])
            x = xsrc_of(i)
            v.bn_stats(bns[:, j * 12:j * 12 + 6], x[:, 0:512])
            E.counts[sv] += 1
            v.bn_stats(bns[:, j * 12 + 6:j * 12 + 12], x[:, 512:1024]).then_inc(E.sem(sv), 1)
            v.wait_ge(E.sem(sv), E.counts[sv])
            E.counts[sv] += 1
            v.bn_aggr(mv[:, j * 2:j * 2 + 2], bns[:, j * 12:j * 12 + 12]).then_inc(E.sem(sv), 1)
            v.wait_ge(E.sem(sv), E.counts[sv])
            E.counts[sv] += 1
            v.tensor_scalar_add(mv[:, j * 2 + 1:j * 2 + 2],
                                mv[:, j * 2 + 1:j * 2 + 2], EPS).then_inc(E.sem(sv), 1)
            v.wait_ge(E.sem(ss), base[ss] + 2 * i + 2)
            E.counts[sv] += 1
            v.scalar_tensor_tensor(nbias[:, j:j + 1], mv[:, j * 2:j * 2 + 1], -1.0,
                                   rstd[:, j:j + 1], ALU.mult, ALU.mult).then_inc(E.sem(sv), 1)
            v.wait_ge(E.sem(sv), E.counts[sv])
            if i >= 2:
                v.wait_ge(E.sem(tt), base[tt] + (i - 2) + 1)
            E.counts[sv] += 1
            v.scalar_tensor_tensor(xn[:, j * D:(j + 1) * D], x, rstd[:, j:j + 1],
                                   nbias[:, j:j + 1].broadcast_to((128, D)),
                                   ALU.mult, ALU.add).then_inc(E.sem(sv), 1)
            if i >= 1:
                emit_evac(i - 1)
        emit_evac(n - 1)

    @E.block.scalar
    def _(sc):
        for i, (row, slot, _hc) in enumerate(chunks):
            j = i % 2
            sc.wait_ge(E.sem(sv), base[sv] + 5 * i + 3)
            E.counts[ss] += 1
            sc.activation(rstd[:, j:j + 1], mv[:, j * 2 + 1:j * 2 + 2],
                          AF.Ln).then_inc(E.sem(ss), 1)
            sc.wait_ge(E.sem(ss), E.counts[ss])
            E.counts[ss] += 1
            sc.activation(rstd[:, j:j + 1], rstd[:, j:j + 1], AF.Exp,
                          scale=-0.5).then_inc(E.sem(ss), 1)

    @E.block.tensor
    def _(te):
        for i in range(n):
            j = i % 2
            te.wait_ge(E.sem(sv), base[sv] + 5 * i + 5)
            if i >= 2:
                te.wait_ge(E.sem(ev), base[ev] + (i - 2) + 1)
            E.counts[tt] += 1
            for k in range(8):
                ins = te.transpose(ptr[j][:, k * 128:(k + 1) * 128],
                                   xn[:, j * D + k * 128: j * D + (k + 1) * 128],
                                   E.bufs["idnb"][:])
                if k == 7:
                    ins.then_inc(E.sem(tt), 1)


def emit_zT_to_resid(E, zsrc_psum, zs_buf, oc, chunk_idxs, mcols_w, scale_ap=None,
                     ps_o=None, pfx="z", w_col=None):
    resid = E.bufs["resid"]
    n = len(chunk_idxs)
    smm, sev, str_, sad = pfx + "mm", pfx + "ev", pfx + "tr", pfx + "ad"
    it = E.n(sev)

    @E.block.vector
    def _(v):
        v.wait_ge(E.sem(smm), it + 1)
        if it >= 1:
            v.wait_ge(E.sem(str_), it)
        if scale_ap is not None:
            ins = v.tensor_scalar(zs_buf[:, 0:mcols_w], zsrc_psum[:, 0:mcols_w],
                                  scale_ap, None, ALU.mult)
        else:
            ins = v.tensor_copy(zs_buf[:, 0:mcols_w], zsrc_psum[:, 0:mcols_w])
        E.counts[sev] += 1
        ins.then_inc(E.sem(sev), 1)

    @E.block.tensor
    def _(te):
        te.wait_ge(E.sem(sev), it + 1)
        if it >= 1:
            te.wait_ge(E.sem(sad), it)
        for k in range(n):
            ins = te.transpose(ps_o[:, k * 128:(k + 1) * 128],
                               zs_buf[:, k * 128:(k + 1) * 128], E.bufs["idnf"][:])
        E.counts[str_] += 1
        ins.then_inc(E.sem(str_), 1)

    @E.block.vector
    def _(v):
        v.wait_ge(E.sem(str_), it + 1)
        for k, ci in enumerate(chunk_idxs):
            dst = resid[:, ci * D + oc * 128: ci * D + oc * 128 + 128]
            if w_col is None:
                ins = v.tensor_add(dst, dst, ps_o[:, k * 128:(k + 1) * 128])
            else:
                ins = v.scalar_tensor_tensor(dst, ps_o[:, k * 128:(k + 1) * 128],
                                             w_col[k], dst, ALU.mult, ALU.add)
        E.counts[sad] += 1
        ins.then_inc(E.sem(sad), 1)


def emit_stage2(E):
    nc = E.nc
    NC, NOWN, SOWN, G, NB, QB, NHC = shapes()
    wf = E.sb("s2wf", [128, 8 * D], F32)
    wqT = E.sb("s2wqT", [128, 8 * D], BF16)
    wqc = E.sb("s2wqc", [128, D], BF16)
    wtm = E.sb("s2wtm", [128, D], BF16)
    asum = E.sb("s2asum", [128, 8], F32)
    tot = E.sb("s2tot", [1, 2], F32)
    sca = E.sb("s2sca", [128, 3], F32)
    zs = E.sb("s2zs", [128, QB], F32)
    pst = E.ps("s2pst", [128, 1024], BF16)
    psz = [E.ps("s2psz0", [128, QB], F32), E.ps("s2psz1", [128, QB], F32)]
    pso = E.ps("s2pso", [128, QB], F32)
    pss = psz[0][0:1, 0:8]
    psr = pso[:, 0:1]
    sv, mm, ss = "s2sv", "s2mm", "s2ss"
    for k in (sv, mm, ss, "s2zmm", "s2zev", "s2ztr", "s2zad"):
        E.sem(k)

    tok_wf = [None]

    @E.block.sync
    def _(sy):
        tok_wf[0] = E.qS.issue(sy, r3(wf[:], 8),
                               E.dram["w0"].rearrange("(c p) n -> p c n", p=128))

    @E.block.vector
    def _(v):
        E.qS.wait(v, tok_wf[0])
        for c in range(8):
            ins = v.reduce_sum(asum[:, c:c + 1], wf[:, c * D:(c + 1) * D], AX.X,
                               apply_absolute_value=True)
        E.inc(ins, sv, 1)

    @E.block.tensor
    def _(te):
        te.wait_ge(E.sem(sv), E.n(sv))
        E.inc(te.matmul(pss[:], E.bufs["onesf"][:, 0:1], asum[:], start=True, stop=True),
              mm, 1)

    @E.block.vector
    def _(v):
        v.wait_ge(E.sem(mm), E.n(mm))
        E.inc(v.reduce_sum(tot[:, 0:1], pss[:], AX.X), sv, 1)
        v.wait_ge(E.sem(sv), E.n(sv))
        E.inc(v.tensor_scalar_mul(tot[:, 0:1], tot[:, 0:1], 1.0 / (D * D)), sv, 1)
        v.wait_ge(E.sem(sv), E.n(sv))
        E.inc(v.tensor_scalar_max(tot[:, 0:1], tot[:, 0:1], EPS), sv, 1)

    @E.block.tensor
    def _(te):
        te.wait_ge(E.sem(sv), E.n(sv))
        E.inc(te.matmul(psr[:], E.bufs["onesf"][0:1, :], tot[:, 0:1], start=True, stop=True),
              mm, 1)

    @E.block.vector
    def _(v):
        v.wait_ge(E.sem(mm), E.n(mm))
        E.inc(v.tensor_copy(sca[:, 0:1], psr[:]), sv, 1)
        v.wait_ge(E.sem(sv), E.n(sv))
        v.tensor_scalar_mul(sca[:, 1:2], sca[:, 0:1], 0.5)
        E.inc(v.tensor_scalar_mul(sca[:, 2:3], sca[:, 0:1], -0.5), sv, 1)

    for c in range(8):
        @E.block.vector
        def _(v, c=c):
            v.wait_ge(E.sem(sv), E.n(sv))
            if c >= 1:
                v.wait_ge(E.sem(mm), E.n(mm))
            v.tensor_scalar(wqc[:], wf[:, c * D:(c + 1) * D], sca[:, 1:2], None, ALU.is_gt)
            E.inc(v.tensor_scalar(wtm[:], wf[:, c * D:(c + 1) * D], sca[:, 2:3], None, ALU.is_lt), sv, 1)
            v.wait_ge(E.sem(sv), E.n(sv))
            E.inc(v.tensor_sub(wqc[:], wqc[:], wtm[:]), sv, 1)

        @E.block.tensor
        def _(te, c=c):
            te.wait_ge(E.sem(sv), E.n(sv))
            if c >= 1:
                te.wait_ge(E.sem(ss), E.n(ss))
            for k in range(8):
                ins = te.transpose(pst[:, k * 128:(k + 1) * 128],
                                   wqc[:, k * 128:(k + 1) * 128], E.bufs["idnb"][:])
            E.inc(ins, mm, 1)

        @E.block.vector
        def _(v, c=c):
            v.wait_ge(E.sem(mm), E.n(mm))
            E.inc(v.tensor_copy(r3(wqT[:], 8)[:, :, c * 128:(c + 1) * 128],
                                r3(pst[:], 8)), ss, 1)

    E.barrier()

    nlnev = E.n("lnev")
    for oc in range(8):
        for m in range(NB):
            it = oc * NB + m
            jj = it % 2
            @E.block.tensor
            def _(te):
                if it == 0 and nlnev:
                    te.wait_ge(E.sem("lnev"), nlnev)
                if it >= 2:
                    te.wait_ge(E.sem("s2zev"), it - 1)
                for ic in range(8):
                    ins = te.matmul(psz[jj][:],
                                    wqT[:, ic * D + oc * 128: ic * D + oc * 128 + 128],
                                    r3(E.bufs["hT"][:], 8)[:, ic, m * QB:(m + 1) * QB],
                                    start=(ic == 0), stop=(ic == 7))
                E.inc(ins, "s2zmm", 1)

            emit_zT_to_resid(E, psz[jj], zs, oc, [G * m + k for k in range(G)], QB,
                             scale_ap=sca[:, 0:1], ps_o=pso, pfx="s2z")
    E.barrier()


def emit_stage1(E):
    nc = E.nc
    NC, NOWN, SOWN, G, NB, QB, NHC = shapes()
    KB = min(512, S)
    NKB = S // KB
    hT = E.bufs["hT"]
    attnT = E.bufs["attnT"]
    kT = [E.bufs["s1kT0"], E.bufs["s1kT1"]]
    qT = [E.bufs["s1qT0"], E.bufs["s1qT1"]]
    vv = [E.bufs["s1vv0"], E.bufs["s1vv1"]]
    ws = E.bufs["s1ws"]
    qa = [E.bufs["s1qa0"], E.bufs["s1qa1"]]
    t1, t2 = E.bufs["s1t1"], E.bufs["s1t2"]
    sqb = E.bufs["s1sqb"]
    k2row = E.bufs["s1k2row"]
    mxb = E.bufs["s1mxb"]
    ptile = [E.bufs["s1pt0"], E.bufs["s1pt1"]]
    atmp = E.bufs["s1atmp"]
    sinv = E.bufs["s1sinv"]
    psc = [E.bufs["ps_A"], E.bufs["ps_B"]]
    pj = [psc[0][:, 0:512], psc[1][:, 0:512]]
    pr = [psc[0][:, 512:1024], psc[1][:, 512:1024]]
    ptv = E.bufs["ps_e"]
    po = E.bufs["ps_f"]
    prep = psc[0][:, 512:1024]
    pq2 = psc[0][0:1, 512:512 + QB]
    pk2 = psc[1][0:1, 512:512 + KB]
    projfree = {}
    cosb, sinb = E.bufs["costb"], E.bufs["sintb"]
    cosqb, sinqb = cosb, sinb
    dmaskb = E.bufs["dmaskb"]
    SC = 0.35355339059327373

    @E.block.vector
    def _(v):
        v.memset(kT[0][64:65, :], -1.0)
        v.memset(kT[1][64:65, :], -1.0)
        v.memset(r3(vv[0][:], NC)[:, :, 64:65], 1.0)
        E.vinc(v.memset(r3(vv[1][:], NC)[:, :, 64:65], 1.0))

    wslot = 3 * 8 * 128
    wtok = {}
    projdone = {}
    mq2free = {}

    def w_ap(hp, kind):
        return r3(ws[:], 2 * 3 * 8)[:, (hp % 2) * 24 + kind * 8:(hp % 2) * 24 + kind * 8 + 8, :]

    for hp in range(8):
        @E.block.gpsimd
        def _(g):
            if hp >= 2:
                g.wait_ge(E.sem("eT"), projdone[hp - 2])
            for kind, nm in enumerate(("wq", "wk", "wv")):
                wtok[(hp, kind)] = E.qG.issue(
                    g, w_ap(hp, kind),
                    E.dram[nm].rearrange("(c p) n -> p c n", p=128)[:, :, hp * 128:(hp + 1) * 128])

        jobs = ([("k", gb) for gb in range(NKB)] + [("v", gb) for gb in range(NKB)]
                + [("q", m) for m in range(NB)])
        nJ = len(jobs)
        base_T = E.n("eT")
        base_V = E.n("eV")
        evac_cnt = [0] * nJ
        tail_cnt = [0] * nJ
        vc = base_V
        for bi, (kind, blk) in enumerate(jobs):
            vc += 1
            evac_cnt[bi] = vc
            if bi >= 1:
                vc += 2 if jobs[bi - 1][0] in ("k", "q") else 1
                tail_cnt[bi - 1] = vc
        vc += 2 if jobs[nJ - 1][0] in ("k", "q") else 1
        tail_cnt[nJ - 1] = vc
        proj_cnt = [0] * nJ
        rot_cnt = [0] * nJ
        tc = base_T
        for bi in range(nJ):
            tc += 1
            proj_cnt[bi] = tc
            if bi >= 1:
                tc += 1
                rot_cnt[bi - 1] = tc
        tc += 1
        rot_cnt[nJ - 1] = tc

        @E.block.tensor
        def _(te):
            def emit_tailT(bi):
                kind, blk = jobs[bi]
                j = bi % 2
                w = QB if kind == "q" else KB
                te.wait_ge(E.sem("eV"), evac_cnt[bi])
                if kind == "v":
                    if bi >= 1:
                        te.wait_ge(E.sem("eV"), tail_cnt[bi - 1])
                    for kq in range(w // 128):
                        ins = te.transpose(ptv[:, kq * 128:(kq + 1) * 128],
                                           qa[j][:, kq * 128:(kq + 1) * 128],
                                           E.bufs["idnb"][:])
                else:
                    if bi >= 2:
                        te.wait_ge(E.sem("eV"), tail_cnt[bi - 2])
                    ins = te.matmul(pr[j][:, 0:w], E.bufs["permb"][:], qa[j][:, 0:w],
                                    start=True, stop=True)
                E.tinc(ins)

            for bi, (kind, blk) in enumerate(jobs):
                j = bi % 2
                kindi = {"q": 0, "k": 1, "v": 2}[kind]
                w = QB if kind == "q" else KB
                if bi == 0:
                    if hp == 0:
                        te.wait_ge(E.sem("lnev"), E.n("lnev"))
                        te.wait_ge(E.sem("eV"), E.n("eV"))
                    for kk in range(3):
                        E.qG.wait(te, wtok[(hp, kk)])
                if bi >= 2:
                    te.wait_ge(E.sem("eV"), evac_cnt[bi - 2])
                elif hp > 0:
                    te.wait_ge(E.sem("eV"), base_V)
                for ic in range(8):
                    cols = blk * KB if kind != "q" else blk * QB
                    ins = te.matmul(pj[j][:, 0:w], w_ap(hp, kindi)[:, ic, :],
                                    r3(hT[:], 8)[:, ic, cols:cols + w],
                                    start=(ic == 0), stop=(ic == 7))
                E.tinc(ins)
                assert E.n("eT") == proj_cnt[bi]
                if bi >= 1:
                    emit_tailT(bi - 1)
            emit_tailT(nJ - 1)
            assert E.n("eT") == rot_cnt[nJ - 1]
        projdone[hp] = E.n("eT")

        base_G = E.n("eG")
        t2_cnt = {}
        gcv = base_G
        for bi, (kind, blk) in enumerate(jobs):
            if kind in ("q", "k"):
                gcv += 1
                t2_cnt[bi] = gcv

        @E.block.gpsimd
        def _(g):
            for bi, (kind, blk) in enumerate(jobs):
                if kind not in ("q", "k"):
                    continue
                j = bi % 2
                w = QB if kind == "q" else KB
                cs = cosqb if kind == "q" else cosb
                cols = blk * (QB if kind == "q" else KB)
                g.wait_ge(E.sem("eV"), evac_cnt[bi])
                if bi >= 2:
                    g.wait_ge(E.sem("eV"), tail_cnt[bi - 2])
                E.ginc(g.tensor_tensor(t2[:, j * KB:j * KB + w], qa[j][:, 0:w],
                                       cs[:, cols:cols + w], ALU.mult))

        @E.block.vector
        def _(v):
            def emit_tailV(bi):
                kind, blk = jobs[bi]
                j = bi % 2
                w = QB if kind == "q" else KB
                v.wait_ge(E.sem("eT"), rot_cnt[bi])
                if kind == "v":
                    g0 = blk * (w // 128)
                    nch = w // 128
                    v.tensor_copy(r3(vv[0][:], NC)[:, g0:g0 + nch, 0:64],
                                  r3(ptv[:, 0:w], nch)[:, :, 0:64])
                    ins = v.tensor_copy(r3(vv[1][:], NC)[:, g0:g0 + nch, 0:64],
                                        r3(ptv[:, 0:w], nch)[:, :, 64:128])
                    E.vinc(ins)
                else:
                    if kind == "q":
                        sn = sinqb
                        cols = blk * QB
                        dst0, dst1 = qT[0], qT[1]
                    else:
                        sn = sinb
                        cols = blk * KB
                        dst0, dst1 = kT[0], kT[1]
                    E.vinc(v.tensor_tensor(t1[:, 0:w], pr[j][:, 0:w],
                                           sn[:, cols:cols + w], ALU.mult))
                    v.wait_ge(E.sem("eV"), E.n("eV"))
                    v.wait_ge(E.sem("eG"), t2_cnt[bi])
                    v.tensor_tensor(dst0[0:64, cols:cols + w],
                                    t2[0:64, j * KB:j * KB + w],
                                    t1[0:64, 0:w], ALU.add)
                    E.vinc(v.tensor_tensor(dst1[0:64, cols:cols + w],
                                           t2[64:128, j * KB:j * KB + w],
                                           t1[64:128, 0:w], ALU.add))
                assert E.n("eV") == tail_cnt[bi]

            for bi, (kind, blk) in enumerate(jobs):
                j = bi % 2
                w = QB if kind == "q" else KB
                v.wait_ge(E.sem("eT"), proj_cnt[bi])
                if kind in ("q", "k"):
                    ins = v.tensor_scalar(qa[j][:, 0:w], pj[j][:, 0:w], SC, None,
                                          ALU.mult)
                else:
                    ins = v.tensor_copy(qa[j][:, 0:w], pj[j][:, 0:w])
                E.vinc(ins)
                assert E.n("eV") == evac_cnt[bi]
                if bi >= 1:
                    emit_tailV(bi - 1)
            emit_tailV(nJ - 1)
        projfree[hp] = E.n("eV")

        for x in range(2):
            kTx, qTx, vvx = kT[x], qT[x], vv[x]
            base_T = E.n("eT")
            base_V = E.n("eV")
            @E.block.vector
            def _(v):
                for gb in range(NKB):
                    if gb == 0:
                        v.wait_ge(E.sem("eV"), E.n("eV"))
                    else:
                        v.wait_ge(E.sem("eT"), base_T + gb)
                    E.vinc(v.tensor_mul(sqb[:, 0:KB], kTx[0:64, gb * KB:(gb + 1) * KB],
                                        kTx[0:64, gb * KB:(gb + 1) * KB]))
                    v.wait_ge(E.sem("eT"), base_T + gb + 1)
                    with nc.allow_low_precision(reason="|k|^2 row feeds a loose softmax bound"):
                        E.vinc(v.tensor_copy(k2row[0:1, gb * KB:(gb + 1) * KB], pk2))
                v.wait_ge(E.sem("eV"), E.n("eV"))
                E.vinc(v.reduce_max(mxb[:], k2row[:], AX.X))
                nmax = E.n("eV")
                for m in range(NB):
                    v.wait_ge(E.sem("eT"), base_T + NKB + m)
                    E.vinc(v.tensor_mul(sqb[:, 0:QB], qTx[0:64, m * QB:(m + 1) * QB],
                                        qTx[0:64, m * QB:(m + 1) * QB]))
                    v.wait_ge(E.sem("eT"), base_T + NKB + m + 1)
                    v.wait_ge(E.sem("eV"), nmax)
                    E.vinc(v.tensor_scalar(qTx[64:65, m * QB:(m + 1) * QB], pq2,
                                           mxb[0:1, 0:1], 0.5, ALU.add, ALU.mult))

            @E.block.tensor
            def _(te):
                for gb in range(NKB):
                    te.wait_ge(E.sem("eV"), base_V + 2 * gb + 1)
                    E.tinc(te.matmul(pk2, E.bufs["onesb"][0:64, 0:1], sqb[:, 0:KB],
                                     start=True, stop=True))
                for m in range(NB):
                    te.wait_ge(E.sem("eV"), base_V + 2 * NKB + 1 + 2 * m + 1)
                    E.tinc(te.matmul(pq2, E.bufs["onesb"][0:64, 0:1], sqb[:, 0:QB],
                                     start=True, stop=True))

            if x == 1:
                mq2free[hp] = E.n("eV")

            base_T = E.n("eT")
            base_S = E.n("eS")
            base_V = E.n("eV")
            seqs = []
            for m in range(NB):
                seq = [(part, c) for part in range(2) for c in range(G * m + G)]
                seqs.append(seq)
            t_ops = []
            for m in range(NB):
                n = len(seqs[m])
                for k in range(n // 2):
                    t_ops.append(("score", m, 2 * k))
                    t_ops.append(("score", m, 2 * k + 1))
                    if k >= 1:
                        t_ops.append(("acc", m, 2 * k - 2))
                        t_ops.append(("acc", m, 2 * k - 1))
                t_ops.append(("acc", m, n - 2))
                t_ops.append(("acc", m, n - 1))
                t_ops.append(("rep", m, 0))
            score_cnt = {}
            acc_cnt = {}
            rep_cnt = {}
            tc = base_T
            for op, m, si in t_ops:
                tc += 1
                if op == "score":
                    score_cnt[(m, si)] = tc
                elif op == "acc":
                    acc_cnt[(m, si)] = tc
                else:
                    rep_cnt[m] = tc
            expb_cnt = {}
            mask_cnt = {}
            sig_cnt = {}
            atmp_cnt = {}
            store_cnt = {}
            scv = base_S
            vcv = base_V
            for m in range(NB):
                for k in range(len(seqs[m]) // 2):
                    scv += 1
                    expb_cnt[(m, k)] = scv
                for si, (part, c) in enumerate(seqs[m]):
                    if c >= G * m:
                        vcv += 1
                        mask_cnt[(m, si)] = vcv
                vcv += 1
                sig_cnt[m] = vcv
                vcv += 1
                atmp_cnt[m] = vcv
                vcv += 1
                store_cnt[m] = vcv

            def slot(si):
                return psc[(si // 2) % 2][:, (si % 2) * QB:(si % 2) * QB + QB]

            def pslot(si):
                return ptile[(si // 2) % 2][:, (si % 2) * QB:(si % 2) * QB + QB]

            @E.block.tensor
            def _(te):
                for op, m, si in t_ops:
                    seq = seqs[m]
                    if op == "score":
                        part, c = seq[si]
                        jp = part * NOWN + c
                        k = si // 2
                        if si == 0:
                            te.wait_ge(E.sem("eV"),
                                       store_cnt[m - 1] if m > 0 else base_V)
                            if m == 0:
                                te.wait_ge(E.sem("eS"), base_S)
                        if si % 2 == 0 and k >= 2:
                            te.wait_ge(E.sem("eS"), expb_cnt[(m, k - 2)])
                        E.tinc(te.matmul(slot(si),
                                         kTx[:, jp * 128:(jp + 1) * 128],
                                         qTx[:, m * QB:(m + 1) * QB],
                                         start=True, stop=True))
                        assert E.n("eT") == score_cnt[(m, si)]
                    elif op == "acc":
                        part, c = seq[si]
                        jp = part * NOWN + c
                        r = (part * G + (c - G * m)) if c >= G * m else -1
                        if r >= 0:
                            te.wait_ge(E.sem("eV"), mask_cnt[(m, si)])
                        else:
                            te.wait_ge(E.sem("eS"), expb_cnt[(m, si // 2)])
                        E.tinc(te.matmul(po[:, 0:QB], r3(vvx[:], NC)[:, jp, :],
                                         pslot(si), start=(si == 0),
                                         stop=(si == len(seq) - 1)))
                        assert E.n("eT") == acc_cnt[(m, si)]
                    else:
                        te.wait_ge(E.sem("eV"), sig_cnt[m])
                        E.tinc(te.matmul(prep[:, 0:QB], E.bufs["onesf"][0:1, :],
                                         sinv[:, 0:QB], start=True, stop=True))
                        assert E.n("eT") == rep_cnt[m]

            @E.block.scalar
            def _(sc):
                for m in range(NB):
                    for k in range(len(seqs[m]) // 2):
                        sc.wait_ge(E.sem("eT"), score_cnt[(m, 2 * k + 1)])
                        if k >= 2:
                            sc.wait_ge(E.sem("eT"), acc_cnt[(m, 2 * k - 3)])
                        pi = k % 2
                        E.sinc(sc.activation(ptile[pi][:, 0:2 * QB],
                                             psc[pi][:, 0:2 * QB], AF.Exp))

            @E.block.vector
            def _(v):
                for m in range(NB):
                    seq = seqs[m]
                    for si, (part, c) in enumerate(seq):
                        r = (part * G + (c - G * m)) if c >= G * m else -1
                        if r >= 0:
                            v.wait_ge(E.sem("eS"), expb_cnt[(m, si // 2)])
                            E.vinc(v.tensor_mul(pslot(si), pslot(si),
                                                dmaskb[:, r * QB:(r + 1) * QB]))
                    v.wait_ge(E.sem("eT"), acc_cnt[(m, len(seq) - 1)])
                    E.vinc(v.reciprocal(sinv[:, 0:QB], po[64:65, 0:QB]))
                    assert E.n("eV") == sig_cnt[m]
                    E.vinc(v.tensor_copy(atmp[:, 0:QB], po[0:64, 0:QB]))
                    v.wait_ge(E.sem("eV"), E.n("eV"))
                    v.wait_ge(E.sem("eT"), rep_cnt[m])
                    h = 2 * hp + x
                    dst = attnT[(h % 2) * 64:(h % 2) * 64 + 64,
                                (h // 2) * SOWN + m * QB:(h // 2) * SOWN + (m + 1) * QB]
                    E.vinc(v.tensor_tensor(dst, atmp[:, 0:QB], prep[0:64, 0:QB],
                                           ALU.mult))
                    assert E.n("eV") == store_cnt[m]


def emit_stage1_wo(E):
    nc = E.nc
    NC, NOWN, SOWN, G, NB, QB, NHC = shapes()
    attnT = E.bufs["attnT"]
    wo = E.sb("s1wo", [128, 8 * D], BF16)
    zs = E.sb("s1zs", [128, QB], F32)
    psz = [E.ps("s1psz0", [128, QB], F32), E.ps("s1psz1", [128, QB], F32)]
    pso = E.ps("s1pso", [128, QB], F32)
    for k in ("s1zmm", "s1zev", "s1ztr", "s1zad"):
        E.sem(k)

    @E.block.gpsimd
    def _(g):
        E.qG.issue(g, r3(wo[:], 8), E.dram["wo"].rearrange("(c p) n -> p c n", p=128))

    @E.block.sync
    def _(sy):
        for i in range(NOWN):
            E.qS.issue(sy, E.bufs["resid"][:, i * D:(i + 1) * D],
                       E.dram["x"][i * 128:(i + 1) * 128, :])
    E.barrier()

    for oc in range(8):
        for m in range(NB):
            it = oc * NB + m
            jj = it % 2

            @E.block.tensor
            def _(te):
                if it == 0:
                    te.wait_ge(E.sem("eV"), E.n("eV"))
                if it >= 2:
                    te.wait_ge(E.sem("s1zev"), it - 1)
                for ic in range(8):
                    ins = te.matmul(psz[jj][:],
                                    wo[:, ic * D + oc * 128: ic * D + oc * 128 + 128],
                                    r3(attnT[:], 8)[:, ic, m * QB:(m + 1) * QB],
                                    start=(ic == 0), stop=(ic == 7))
                E.inc(ins, "s1zmm", 1)

            emit_zT_to_resid(E, psz[jj], zs, oc, [G * m + k for k in range(G)], QB,
                             ps_o=pso, pfx="s1z")
    E.barrier()


def alloc_stage1_bufs(E):
    NC, NOWN, SOWN, G, NB, QB, NHC = shapes()
    KB = min(512, S)
    E.sb("s1kT0", [65, S], BF16)
    E.sb("s1kT1", [65, S], BF16)
    E.sb("s1qT0", [65, SOWN], BF16)
    E.sb("s1qT1", [65, SOWN], BF16)
    E.sb("s1vv0", [128, NC * 65], BF16)
    E.sb("s1vv1", [128, NC * 65], BF16)
    E.sb("s1ws", [128, 2 * 3 * 8 * 128], BF16)
    E.sb("s1qa0", [128, KB], BF16)
    E.sb("s1qa1", [128, KB], BF16)
    E.sb("s1t1", [128, KB], BF16)
    E.sb("s1t2", [128, 2 * KB], BF16)
    E.sb("s1sqb", [64, KB], BF16)
    E.sb("s1k2row", [1, S], BF16)
    E.sb("s1mxb", [1, 1], F32)
    E.sb("s1q2s", [1, QB], F32)
    E.sb("s1pt0", [128, 2 * QB], BF16)
    E.sb("s1pt1", [128, 2 * QB], BF16)
    E.sb("s1atmp", [64, QB], BF16)
    E.sb("s1sinv", [1, QB], F32)
    E.ps("s1psA", [128, 1024], F32, alias="ps_A")
    E.ps("s1psB", [128, 1024], F32, alias="ps_B")
    E.ps("s1pse", [128, KB], BF16, alias="ps_e")
    E.ps("s1psf", [65, QB], F32, alias="ps_f")


def build_prog1():
    NC, NOWN, SOWN, G, NB, QB, NHC = shapes()
    specs = [("x", (S, D), F32, "ExternalInput"),
             ("wq", (D, D), F32, "ExternalInput"),
             ("wk", (D, D), F32, "ExternalInput"),
             ("wv", (D, D), F32, "ExternalInput"),
             ("wo", (D, D), F32, "ExternalInput"),
             ("x1", (SOWN, D), F32, "ExternalOutput")]
    specs += const_specs(("idn", "ones", "perm", "cost", "sint", "dmask"))
    nc, dram = new_prog(specs)
    with ExitStack() as ctx:
        block = ctx.enter_context(nc.Block())
        E = Env(nc, ctx, block)
        E.dram = dram
        E.sb("attnT", [128, 8 * SOWN], BF16)
        alloc_ln_bufs(E)
        emit_load_constants(E, want=("idn", "ones", "perm", "cost", "sint", "dmask"))
        with ExitStack() as s1ctx:
            Einner = E
            old_ctx = E.ctx
            E.ctx = s1ctx
            E.sb("hT", [128, 8 * S], BF16)
            alloc_stage1_bufs(E)
            E.ctx = old_ctx
            E.barrier()
            emit_ln(E, [(g * 128, None, g * 128) for g in range(NC)], S,
                    src=dram["x"])
            emit_stage1(E)
            E.barrier()
        E.sb("resid", [128, NOWN * D], F32)
        emit_stage1_wo(E)
        emit_writeback(E, "x1", waits=("s1zad",))
    return nc


def alloc_stage3_bufs(E):
    NC, NOWN, SOWN, G, NB, QB, NHC = shapes()
    KB = min(512, S)
    E.sb("s3rwq", [128, H * 64], BF16)
    E.sb("s3rwk", [128, H * 64], BF16)
    E.sb("s3rwv", [128, H * 64], BF16)
    E.sb("s3gns", [64, H], F32)
    E.sb("s3gnb", [64, H], F32)
    E.sb("s3kT", [64, S], BF16)
    E.sb("s3qT", [64, SOWN], BF16)
    E.sb("s3vv", [128, NC * 65], BF16)
    E.sb("s3repq", [64, QB], BF16)
    E.sb("s3repk0", [64, KB], BF16)
    E.sb("s3repk1", [64, KB], BF16)
    E.sb("s3qfac", [1, 2 * QB], BF16)
    E.sb("s3kfct", [1, 2 * 2 * KB], BF16)
    E.sb("s3pt4", [128, 4 * QB], BF16)
    E.sb("s3rets", [64, QB], F32)
    E.sb("s3sq", [64, QB], F32)
    E.sb("s3mrow", [1, QB], F32)
    E.sb("s3vtmp", [1, QB], F32)
    E.sb("s3rstd", [1, QB], F32)
    E.ps("s3psf", [65, QB], F32, alias="ps_f")
    E.ps("s3ppa", [128, min(512, S)], F32, alias="ps_pa")
    E.ps("s3ppb", [128, min(512, S)], F32, alias="ps_pb")


def emit_stage3(E):
    nc = E.nc
    NC, NOWN, SOWN, G, NB, QB, NHC = shapes()
    KB = min(512, S)
    NKB = S // KB
    hT = E.bufs["hT"]
    kT, qT, vv = E.bufs["s3kT"], E.bufs["s3qT"], E.bufs["s3vv"]
    repq, repk = E.bufs["s3repq"], [E.bufs["s3repk0"], E.bufs["s3repk1"]]
    pt4 = E.bufs["s3pt4"]
    rets, sq = E.bufs["s3rets"], E.bufs["s3sq"]
    mrow, vtmp, rstd = E.bufs["s3mrow"], E.bufs["s3vtmp"], E.bufs["s3rstd"]
    normT = E.bufs["s3norm"]
    pA, pB, po = E.bufs["ps_a"], E.bufs["ps_b"], E.bufs["ps_f"]
    pPa, pPb = E.bufs["ps_pa"], E.bufs["ps_pb"]
    pb4 = [pA, pB, pPa, pPb]
    ps2 = pB[0:1, 0:QB]
    prm = pA[0:64, 0:QB]
    prs = pB[64:128, 0:QB]
    projfree = {}
    qfac, kfct = E.bufs["s3qfac"], E.bufs["s3kfct"]
    gns, gnb = E.bufs["s3gns"], E.bufs["s3gnb"]
    ftok = {}
    retfree = {}

    def hs(h, cols, w):
        return r3(hT[:], 8)[(h % 2) * 64:(h % 2) * 64 + 64, h // 2, cols:cols + w]

    for h in range(H):
        sl = h % 2
        if h == 0:
            @E.block.gpsimd
            def _(g):
                ftok[(0, "q")] = E.qG.issue(g, qfac[0:1, 0:QB],
                                            E.dram["qfac"][0:1, 0:QB])
                ftok[(0, "k")] = E.qG.issue(g, kfct[0:1, 0:2 * KB],
                                            E.dram["kfct"][0:1, 0:2 * KB])

        jobs = ([("r", i) for i in range(3)] + [("k", gb) for gb in range(NKB)]
                + [("q", m) for m in range(NB)] + [("v", g) for g in range(NC)])
        nJ = len(jobs)
        base_T = E.n("eT")
        base_V = E.n("eV")
        mm_cnt = [base_T + 1 + i for i in range(nJ)]
        ev_cnt = [0] * nJ
        vc = base_V
        for bi, (kind, blk) in enumerate(jobs):
            vc += 2 if kind == "v" else 1
            ev_cnt[bi] = vc

        if h + 1 < H:
            @E.block.gpsimd
            def _(g):
                if h >= 1:
                    g.wait_ge(E.sem("eT"), mm_cnt[2])
                nsl = (h + 1) % 2
                ftok[(h + 1, "q")] = E.qG.issue(
                    g, qfac[0:1, nsl * QB:(nsl + 1) * QB],
                    E.dram["qfac"][0:1, (h + 1) * QB:(h + 2) * QB])
                ftok[(h + 1, "k")] = E.qG.issue(
                    g, kfct[0:1, nsl * 2 * KB:(nsl + 1) * 2 * KB],
                    E.dram["kfct"][0:1, (h + 1) * 2 * KB:(h + 2) * 2 * KB])

        @E.block.tensor
        def _(te):
            for bi, (kind, blk) in enumerate(jobs):
                jj = bi % 2
                pX = pPa if jj == 0 else pPb
                if bi == 0:
                    E.qG.wait(te, ftok[(h, "q")])
                    E.qG.wait(te, ftok[(h, "k")])
                    if h == 0:
                        te.wait_ge(E.sem("lnev"), E.n("lnev"))
                        E.qG.wait_all(te)
                if bi <= 1 and h > 0:
                    vv_, ss_ = retfree[h - 1]
                    te.wait_ge(E.sem("eV"), vv_)
                    te.wait_ge(E.sem("eS"), ss_)
                if bi >= 2:
                    te.wait_ge(E.sem("eV"), ev_cnt[bi - 2])
                b0 = (h % 2) * 64
                if kind == "r":
                    srcap = (qfac[0:1, sl * QB:sl * QB + QB] if blk == 0 else
                             kfct[0:1, sl * 2 * KB + (blk - 1) * KB:
                                  sl * 2 * KB + blk * KB])
                    w = QB if blk == 0 else KB
                    ins = te.matmul(pX[0:64, 0:w], E.bufs["onesb"][0:1, 0:64], srcap,
                                    start=True, stop=True)
                elif kind == "k":
                    ins = te.matmul(pX[0:64, 0:KB],
                                    E.bufs["s3rwk"][b0:b0 + 64, h * 64:(h + 1) * 64],
                                    hs(h, blk * KB, KB), start=True, stop=True)
                elif kind == "q":
                    ins = te.matmul(pX[0:64, 0:QB],
                                    E.bufs["s3rwq"][b0:b0 + 64, h * 64:(h + 1) * 64],
                                    hs(h, blk * QB, QB), start=True, stop=True)
                else:
                    ins = te.matmul(pX[0:128, 0:64], hs(h, blk * 128, 128),
                                    E.bufs["s3rwv"][b0:b0 + 64, h * 64:(h + 1) * 64],
                                    start=True, stop=True)
                E.tinc(ins)
                assert E.n("eT") == mm_cnt[bi]

        @E.block.vector
        def _(v):
            for bi, (kind, blk) in enumerate(jobs):
                jj = bi % 2
                pX = pPa if jj == 0 else pPb
                v.wait_ge(E.sem("eT"), mm_cnt[bi])
                if kind == "r":
                    dst = repq if blk == 0 else repk[blk - 1]
                    w = QB if blk == 0 else KB
                    ins = v.tensor_copy(dst[:, 0:w], pX[0:64, 0:w])
                elif kind == "k":
                    part = 0 if blk * KB < SOWN else 1
                    ins = v.tensor_tensor(kT[:, blk * KB:(blk + 1) * KB], pX[0:64, 0:KB],
                                          repk[part][:, 0:KB], ALU.mult)
                elif kind == "q":
                    ins = v.tensor_tensor(qT[:, blk * QB:(blk + 1) * QB], pX[0:64, 0:QB],
                                          repq[:, 0:QB], ALU.mult)
                else:
                    E.vinc(v.tensor_copy(r3(vv[:], NC)[:, blk, 0:64], pX[0:128, 0:64]))
                    with nc.allow_low_precision(reason="v_aug rowsum feeds bf16 matmul"):
                        ins = v.reduce_sum(r3(vv[:], NC)[:, blk, 64:65], pX[0:128, 0:64],
                                           AX.X)
                E.vinc(ins)
                assert E.n("eV") == ev_cnt[bi]
        projfree[h] = E.n("eV")

        base_T = E.n("eT")
        base_S = E.n("eS")
        base_V = E.n("eV")
        seqs = [[(part, c) for part in range(2) for c in range(G * m + G)]
                for m in range(NB)]
        sT_cnt = {}
        acc_cnt = {}
        sums_cnt = {}
        repm_cnt = {}
        reps_cnt = {}
        evp_cnt = {}
        mask_cnt = {}
        retc_cnt = {}
        sqm_cnt = {}
        mrow_cnt = {}
        var1_cnt = {}
        var2_cnt = {}
        eps_cnt = {}
        sub_cnt = {}
        mul_cnt = {}
        gn_cnt = {}
        ln_cnt = {}
        exp_cnt = {}
        tc, scv, vcv = base_T, base_S, base_V
        for m in range(NB):
            n = len(seqs[m])
            for k in range(n // 2):
                tc += 1
                sT_cnt[(m, 2 * k)] = tc
                tc += 1
                sT_cnt[(m, 2 * k + 1)] = tc
                if k >= 1:
                    tc += 1
                    acc_cnt[(m, 2 * k - 2)] = tc
                    tc += 1
                    acc_cnt[(m, 2 * k - 1)] = tc
            tc += 1
            acc_cnt[(m, n - 2)] = tc
            tc += 1
            acc_cnt[(m, n - 1)] = tc
            tc += 1
            sums_cnt[m] = tc
            tc += 1
            repm_cnt[m] = tc
            tc += 1
            reps_cnt[m] = tc
            for si, (part, c) in enumerate(seqs[m]):
                if si % 2 == 0:
                    vcv += 1
                    evp_cnt[(m, si)] = ("eV", vcv)
                else:
                    scv += 1
                    evp_cnt[(m, si)] = ("eS", scv)
                if c >= G * m:
                    vcv += 1
                    mask_cnt[(m, si)] = vcv
            vcv += 1
            retc_cnt[m] = vcv
            vcv += 1
            sqm_cnt[m] = vcv
            vcv += 1
            mrow_cnt[m] = vcv
            vcv += 1
            var1_cnt[m] = vcv
            vcv += 1
            var2_cnt[m] = vcv
            vcv += 1
            eps_cnt[m] = vcv
            scv += 1
            ln_cnt[m] = scv
            scv += 1
            exp_cnt[m] = scv
            vcv += 1
            sub_cnt[m] = vcv
            vcv += 1
            mul_cnt[m] = vcv
            vcv += 1
            gn_cnt[m] = vcv

        t_ops = []
        for m in range(NB):
            n = len(seqs[m])
            for k in range(n // 2):
                t_ops.append(("sT", m, 2 * k))
                t_ops.append(("sT", m, 2 * k + 1))
                if k >= 1:
                    t_ops.append(("acc", m, 2 * k - 2))
                    t_ops.append(("acc", m, 2 * k - 1))
            t_ops.append(("acc", m, n - 2))
            t_ops.append(("acc", m, n - 1))
            t_ops.append(("sum", m, 0))
            t_ops.append(("repm", m, 0))
            t_ops.append(("reps", m, 0))

        @E.block.tensor
        def _(te):
            for op, m, si in t_ops:
                seq = seqs[m]
                if op == "sT":
                    part, c = seq[si]
                    jp = part * NOWN + c
                    sl = si % 4
                    pS = pb4[sl]
                    if si <= 3:
                        te.wait_ge(E.sem("eV"), gn_cnt[m - 1] if m > 0 else base_V)
                    if si >= 4:
                        es, ec = evp_cnt[(m, si - 4)]
                        te.wait_ge(E.sem(es), ec)
                    E.tinc(te.matmul(pS[:, 0:QB], kT[:, jp * 128:(jp + 1) * 128],
                                     qT[:, m * QB:(m + 1) * QB], start=True, stop=True))
                    assert E.n("eT") == sT_cnt[(m, si)]
                elif op == "acc":
                    part, c = seq[si]
                    jp = part * NOWN + c
                    sl = si % 4
                    r = (part * G + (c - G * m)) if c >= G * m else -1
                    if r >= 0:
                        te.wait_ge(E.sem("eV"), mask_cnt[(m, si)])
                    else:
                        es, ec = evp_cnt[(m, si)]
                        te.wait_ge(E.sem(es), ec)
                    E.tinc(te.matmul(po[:, 0:QB], r3(vv[:], NC)[:, jp, :],
                                     pt4[:, sl * QB:(sl + 1) * QB], start=(si == 0),
                                     stop=(si == len(seq) - 1)))
                    assert E.n("eT") == acc_cnt[(m, si)]
                elif op == "sum":
                    te.wait_ge(E.sem("eV"), sqm_cnt[m])
                    E.tinc(te.matmul(ps2, E.bufs["onesf"][0:64, 0:1], sq[:, 0:QB],
                                     start=True, stop=True))
                elif op == "repm":
                    te.wait_ge(E.sem("eV"), mrow_cnt[m])
                    E.tinc(te.matmul(prm, E.bufs["onesf"][0:1, 0:64], mrow[:, 0:QB],
                                     start=True, stop=True))
                else:
                    te.wait_ge(E.sem("eS"), exp_cnt[m])
                    E.tinc(te.matmul(prs, E.bufs["onesf"][0:1, 0:64], rstd[:, 0:QB],
                                     start=True, stop=True))
                    assert E.n("eT") == reps_cnt[m]

        @E.block.scalar
        def _(sc):
            for m in range(NB):
                for si, (part, c) in enumerate(seqs[m]):
                    if si % 2 == 0:
                        continue
                    sl = si % 4
                    sc.wait_ge(E.sem("eT"), sT_cnt[(m, si)])
                    if si >= 4:
                        sc.wait_ge(E.sem("eT"), acc_cnt[(m, si - 4)])
                    E.sinc(sc.activation(pt4[:, sl * QB:(sl + 1) * QB],
                                         pb4[sl][:, 0:QB], AF.Identity,
                                         scale=decay_imm(h, m, c, G)))
                sc.wait_ge(E.sem("eV"), eps_cnt[m])
                E.sinc(sc.activation(rstd[:, 0:QB], vtmp[:, 0:QB], AF.Ln))
                sc.wait_ge(E.sem("eS"), E.n("eS"))
                E.sinc(sc.activation(rstd[:, 0:QB], rstd[:, 0:QB], AF.Exp, scale=-0.5))

        @E.block.vector
        def _(v):
            for m in range(NB):
                seq = seqs[m]
                for si, (part, c) in enumerate(seq):
                    sl = si % 4
                    r = (part * G + (c - G * m)) if c >= G * m else -1
                    if si % 2 == 0:
                        v.wait_ge(E.sem("eT"), sT_cnt[(m, si)])
                        if si >= 4:
                            v.wait_ge(E.sem("eT"), acc_cnt[(m, si - 4)])
                        E.vinc(v.tensor_scalar(pt4[:, sl * QB:(sl + 1) * QB],
                                               pb4[sl][:, 0:QB],
                                               decay_imm(h, m, c, G), None, ALU.mult))
                    if r >= 0:
                        es, ec = evp_cnt[(m, si)]
                        v.wait_ge(E.sem(es), ec)
                        E.vinc(v.tensor_mul(pt4[:, sl * QB:(sl + 1) * QB],
                                            pt4[:, sl * QB:(sl + 1) * QB],
                                            E.bufs["dmaskb"][:, r * QB:(r + 1) * QB]))
                v.wait_ge(E.sem("eT"), acc_cnt[(m, len(seq) - 1)])
                E.vinc(v.tensor_copy(rets[:, 0:QB], po[0:64, 0:QB]))
                v.wait_ge(E.sem("eV"), E.n("eV"))
                E.vinc(v.tensor_mul(sq[:, 0:QB], rets[:, 0:QB], rets[:, 0:QB]))
                E.vinc(v.tensor_scalar(mrow[:, 0:QB], po[64:65, 0:QB], 1.0 / 64,
                                       None, ALU.mult))
                v.wait_ge(E.sem("eV"), E.n("eV"))
                E.vinc(v.scalar_tensor_tensor(vtmp[:, 0:QB], mrow[:, 0:QB], -1.0,
                                              mrow[:, 0:QB], ALU.mult, ALU.mult))
                v.wait_ge(E.sem("eT"), sums_cnt[m])
                v.wait_ge(E.sem("eV"), E.n("eV"))
                E.vinc(v.scalar_tensor_tensor(vtmp[:, 0:QB], ps2, 1.0 / 64,
                                              vtmp[:, 0:QB], ALU.mult, ALU.add))
                v.wait_ge(E.sem("eV"), E.n("eV"))
                E.vinc(v.tensor_scalar_add(vtmp[:, 0:QB], vtmp[:, 0:QB], EPS))
                assert E.n("eV") == eps_cnt[m]
                v.wait_ge(E.sem("eT"), repm_cnt[m])
                E.vinc(v.tensor_sub(rets[:, 0:QB], rets[:, 0:QB], prm))
                v.wait_ge(E.sem("eT"), reps_cnt[m])
                v.wait_ge(E.sem("eV"), E.n("eV"))
                E.vinc(v.tensor_mul(rets[:, 0:QB], rets[:, 0:QB], prs))
                v.wait_ge(E.sem("eV"), E.n("eV"))
                dst = normT[(h % 2) * 64:(h % 2) * 64 + 64,
                            (h // 2) * SOWN + m * QB:(h // 2) * SOWN + (m + 1) * QB]
                E.vinc(v.scalar_tensor_tensor(
                    dst, rets[:, 0:QB], gns[:, h:h + 1],
                    gnb[:, h:h + 1].broadcast_to((64, QB)),
                    ALU.mult, ALU.add))
                assert E.n("eV") == gn_cnt[m]
        retfree[h] = (E.n("eV"), E.n("eS"))


def emit_stage3_out(E):
    nc = E.nc
    NC, NOWN, SOWN, G, NB, QB, NHC = shapes()
    normT = E.bufs["s3norm"]
    w1 = E.sb("s3w12", [128, 8 * D], BF16)
    w2 = w1
    gsil = [E.sb("s3gsil", [128, QB], BF16), E.sb("s3gsil1", [128, QB], BF16)]
    psg = [E.bufs["ps_a"], E.bufs["ps_b"]]
    zs = E.sb("s3zs", [128, QB], F32)
    pso = E.ps("s3pso", [128, QB], F32)
    for k in ("s3zmm", "s3zev", "s3ztr", "s3zad"):
        E.sem(k)

    @E.block.gpsimd
    def _(g):
        E.qG.issue(g, r3(w1[:], 8), E.dram["w1"].rearrange("(c p) n -> p c n", p=128))
    E.barrier()

    nIT = 8 * NB
    base_T = E.n("eT")
    base_S = E.n("eS")
    base_V = E.n("eV")

    @E.block.tensor
    def _(te):
        for oc in range(8):
            for m in range(NB):
                it = oc * NB + m
                jj = it % 2
                if it >= 2:
                    te.wait_ge(E.sem("eV"), base_V + it - 1)
                for ic in range(8):
                    ins = te.matmul(psg[jj][:, 0:QB],
                                    w1[:, ic * D + oc * 128: ic * D + oc * 128 + 128],
                                    r3(E.bufs["hT"][:], 8)[:, ic, m * QB:(m + 1) * QB],
                                    start=(ic == 0), stop=(ic == 7))
                E.tinc(ins)

    @E.block.scalar
    def _(sc):
        for it in range(nIT):
            jj = it % 2
            sc.wait_ge(E.sem("eT"), base_T + it + 1)
            if it >= 2:
                sc.wait_ge(E.sem("eV"), base_V + it - 1)
            E.sinc(sc.activation(gsil[jj][:, 0:QB], psg[jj][:, 0:QB], AF.Silu))

    @E.block.vector
    def _(v):
        for oc in range(8):
            for m in range(NB):
                it = oc * NB + m
                jj = it % 2
                v.wait_ge(E.sem("eS"), base_S + it + 1)
                dst = normT[:, oc * SOWN + m * QB: oc * SOWN + (m + 1) * QB]
                E.vinc(v.tensor_add(dst, dst, gsil[jj][:, 0:QB]))
    E.barrier()

    @E.block.gpsimd
    def _(g):
        E.qG.issue(g, r3(w2[:], 8), E.dram["w2"].rearrange("(c p) n -> p c n", p=128))
    E.barrier()

    for oc in range(8):
        for m in range(NB):
            it = oc * NB + m
            jj = it % 2

            @E.block.tensor
            def _(te):
                if it == 0:
                    te.wait_ge(E.sem("eV"), E.n("eV"))
                if it >= 2:
                    te.wait_ge(E.sem("s3zev"), it - 1)
                for ic in range(8):
                    ins = te.matmul(psg[jj][:, 0:QB],
                                    w2[:, ic * D + oc * 128: ic * D + oc * 128 + 128],
                                    r3(normT[:], 8)[:, ic, m * QB:(m + 1) * QB],
                                    start=(ic == 0), stop=(ic == 7))
                E.inc(ins, "s3zmm", 1)

            emit_zT_to_resid(E, psg[jj], zs, oc, [G * m + k for k in range(G)], QB,
                             ps_o=pso, pfx="s3z")
    E.barrier()


def build_prog3():
    NC, NOWN, SOWN, G, NB, QB, NHC = shapes()
    specs = [("x2f", (S, D), F32, "ExternalInput"),
             ("rwq", (128, H * 64), F32, "ExternalInput"),
             ("rwk", (128, H * 64), F32, "ExternalInput"),
             ("rwv", (128, H * 64), F32, "ExternalInput"),
             ("w1", (D, D), F32, "ExternalInput"),
             ("w2", (D, D), F32, "ExternalInput"),
             ("gns", (64, H), F32, "ExternalInput"),
             ("gnb", (64, H), F32, "ExternalInput"),
             ("x3", (SOWN, D), F32, "ExternalOutput")]
    specs += const_specs(("idn", "ones", "dmask", "qfac", "kfct"))
    nc, dram = new_prog(specs)
    with ExitStack() as ctx:
        block = ctx.enter_context(nc.Block())
        E = Env(nc, ctx, block)
        E.dram = dram
        emit_load_constants(E, want=("idn", "ones", "dmask"))
        E.sb("hT", [128, 8 * S], BF16)
        E.sb("s3norm", [128, 8 * (S // 2)], BF16)
        E.ps("s3psa", [128, min(512, S)], F32, alias="ps_a")
        E.ps("s3psb", [128, min(512, S)], F32, alias="ps_b")
        with ExitStack() as s3ctx:
            octx = E.ctx
            E.ctx = s3ctx
            with ExitStack() as lnctx:
                E.ctx = lnctx
                alloc_ln_bufs(E)
                E.ctx = s3ctx
                E.barrier()
                emit_ln(E, [(g * 128, None, g * 128) for g in range(NC)],
                        S, src=dram["x2f"])
                E.barrier()
            alloc_stage3_bufs(E)
            E.ctx = octx

            @E.block.gpsimd
            def _(g):
                for nm in ("rwq", "rwk", "rwv"):
                    E.qG.issue(g, E.bufs["s3" + nm][:], E.dram[nm])
                E.qG.issue(g, E.bufs["s3gns"][:], E.dram["gns"])
                E.qG.issue(g, E.bufs["s3gnb"][:], E.dram["gnb"])
            emit_stage3(E)
            E.barrier()
        E.sb("resid", [128, NOWN * D], F32)

        @E.block.sync
        def _(sy):
            sy.wait_ge(E.sem("eS"), E.n("eS"))
            sy.wait_ge(E.sem("eV"), E.n("eV"))
            sy.wait_ge(E.sem("eT"), E.n("eT"))
            for i in range(NOWN):
                E.qS.issue(sy, E.bufs["resid"][:, i * D:(i + 1) * D],
                           E.dram["x2f"][i * 128:(i + 1) * 128, :])
        E.barrier()
        emit_stage3_out(E)
        emit_writeback(E, "x3", waits=("s3zad",))
    return nc


def alloc_stage4_bufs(E):
    NC, NOWN, SOWN, G, NB, QB, NHC = shapes()
    E.sb("s4xf", [128, D], F32)
    E.sb("s4xfT", [128, 8 * 128], F32)
    E.sb("s4wr", [128, 8 * 8], F32)
    E.sb("s4rb", [128, 8], F32)
    E.sb("s4probs", [128, 8], F32)
    E.sb("s4t8", [128, 8], F32)
    E.sb("s4rec", [128, 1], F32)
    E.sb("s4msk", [128, 8], F32)
    E.sb("s4wbuf", [128, NOWN * 8], F32)
    E.sb("s4bns", [128, 12], F32)
    E.sb("s4mv", [128, 2], F32)
    E.sb("s4rstd", [128, 1], F32)
    E.sb("s4nb", [128, 1], F32)
    E.sb("s4wgu", [128, 2 * 2 * 8 * 128], BF16)
    E.sb("s4wd", [128, 2 * NHC * 128], BF16)
    E.sb("s4gs0", [128, QB], BF16)
    E.sb("s4gs1", [128, QB], BF16)
    E.sb("s4zsb", [128, QB], BF16)
    E.sb("s4zsb1", [128, QB], BF16)
    E.sb("s4gu", [128, NHC * (SOWN if NB == 1 else SOWN // 2)], BF16)
    E.ps("s4psa", [128, QB], F32, alias="ps_a")
    E.ps("s4psb", [128, QB], F32, alias="ps_b")
    E.ps("s4pz0", [128, max(QB, 512)], F32)
    E.ps("s4pz1", [128, QB], F32)
    E.ps("s4ptrp", [128, QB], BF16)


def emit_stage4_router(E):
    nc = E.nc
    NC, NOWN, SOWN, G, NB, QB, NHC = shapes()
    resid = E.bufs["resid"]
    xf, xfT = E.bufs["s4xf"], E.bufs["s4xfT"]
    wr, rb = E.bufs["s4wr"], E.bufs["s4rb"]
    probs, t8, rec, msk = E.bufs["s4probs"], E.bufs["s4t8"], E.bufs["s4rec"], E.bufs["s4msk"]
    wbuf = E.bufs["s4wbuf"]
    bns, mv, rstd, nb = E.bufs["s4bns"], E.bufs["s4mv"], E.bufs["s4rstd"], E.bufs["s4nb"]
    plog = E.bufs["s4pz1"][:, 0:8]
    pxf = E.bufs["s4pz0"][:, 0:512]

    rtok = {}

    @E.block.gpsimd
    def _(g):
        rtok[0] = E.qG.issue(g, r3(wr[:], 8),
                             E.dram["wr"].rearrange("(c p) n -> p c n", p=128))

    @E.block.sync
    def _(sy):
        E.qS.issue(sy, rb[0:1, 0:7], E.dram["rbias"])

    @E.block.tensor
    def _(te):
        E.qS.wait_all(te)
        E.tinc(te.matmul(plog[:, 0:7], E.bufs["onesf"][0:1, :], rb[0:1, 0:7],
                         start=True, stop=True))

    @E.block.vector
    def _(v):
        v.wait_ge(E.sem("eT"), E.n("eT"))
        E.vinc(v.tensor_copy(rb[:, 0:7], plog[:, 0:7]))
        E.vinc(v.memset(probs[:, 7:8], -1e30))

    for i in range(NOWN):
        x = resid[:, i * D:(i + 1) * D]

        @E.block.vector
        def _(v):
            if i >= 1:
                v.wait_ge(E.sem("eS"), E.n("eS"))
                v.wait_ge(E.sem("eT"), E.n("eT"))
            v.bn_stats(bns[:, 0:6], x[:, 0:512])
            E.vinc(v.bn_stats(bns[:, 6:12], x[:, 512:1024]))
            v.wait_ge(E.sem("eV"), E.n("eV"))
            E.vinc(v.bn_aggr(mv[:], bns[:]))
            v.wait_ge(E.sem("eV"), E.n("eV"))
            E.vinc(v.tensor_scalar_add(mv[:, 1:2], mv[:, 1:2], EPS))

        @E.block.scalar
        def _(sc):
            sc.wait_ge(E.sem("eV"), E.n("eV"))
            E.sinc(sc.activation(rstd[:], mv[:, 1:2], AF.Ln))
            sc.wait_ge(E.sem("eS"), E.n("eS"))
            E.sinc(sc.activation(rstd[:], rstd[:], AF.Exp, scale=-0.5))

        @E.block.vector
        def _(v):
            v.wait_ge(E.sem("eS"), E.n("eS"))
            E.vinc(v.scalar_tensor_tensor(nb[:], mv[:, 0:1], -1.0, rstd[:],
                                          ALU.mult, ALU.mult))

        @E.block.vector
        def _(v):
            v.wait_ge(E.sem("eV"), E.n("eV"))
            E.vinc(v.scalar_tensor_tensor(xf[:], x, rstd[:],
                                          nb[:, 0:1].broadcast_to((128, D)),
                                          ALU.mult, ALU.add))

        for half in range(2):
            @E.block.tensor
            def _(te):
                te.wait_ge(E.sem("eV"), E.n("eV"))
                for k in range(4):
                    ins = te.transpose(pxf[:, k * 128:(k + 1) * 128],
                                       xf[:, half * 512 + k * 128: half * 512 + (k + 1) * 128],
                                       E.bufs["idnf"][:])
                E.tinc(ins)

            @E.block.vector
            def _(v):
                v.wait_ge(E.sem("eT"), E.n("eT"))
                E.vinc(v.tensor_copy(xfT[:, half * 512:(half + 1) * 512], pxf[:]))

        @E.block.tensor
        def _(te):
            te.wait_ge(E.sem("eV"), E.n("eV"))
            if i == 0:
                E.qG.wait(te, rtok[0])
            for ic in range(8):
                ins = te.matmul(plog[:, 0:7], xfT[:, ic * 128:(ic + 1) * 128],
                                r3(wr[:], 8)[:, ic, 0:7],
                                start=(ic == 0), stop=(ic == 7))
            E.tinc(ins)

        @E.block.vector
        def _(v):
            v.wait_ge(E.sem("eT"), E.n("eT"))
            E.vinc(v.tensor_mul(probs[:, 0:7], plog[:, 0:7], rb[:, 0:7]))

        @E.block.scalar
        def _(sc):
            sc.wait_ge(E.sem("eV"), E.n("eV"))
            E.sinc(sc.activation(probs[:, 0:7], probs[:, 0:7], AF.Exp, scale=-1.0))

        @E.block.vector
        def _(v):
            v.wait_ge(E.sem("eS"), E.n("eS"))
            E.vinc(v.tensor_scalar_add(probs[:, 0:7], probs[:, 0:7], 1.0))
            v.wait_ge(E.sem("eV"), E.n("eV"))
            E.vinc(v.reciprocal(probs[:, 0:7], probs[:, 0:7]))
            v.wait_ge(E.sem("eV"), E.n("eV"))
            E.vinc(v.max(t8[:], probs[:]))
            v.wait_ge(E.sem("eV"), E.n("eV"))
            E.vinc(v.tensor_add(rec[:], t8[:, 0:1], t8[:, 1:2]))
            v.wait_ge(E.sem("eV"), E.n("eV"))
            E.vinc(v.reciprocal(rec[:], rec[:]))
            v.wait_ge(E.sem("eV"), E.n("eV"))
            E.vinc(v.tensor_scalar(msk[:, 0:7], probs[:, 0:7], t8[:, 1:2], None, ALU.is_ge))
            v.wait_ge(E.sem("eV"), E.n("eV"))
            E.vinc(v.tensor_mul(msk[:, 0:7], msk[:, 0:7], probs[:, 0:7]))
            v.wait_ge(E.sem("eV"), E.n("eV"))
            E.vinc(v.tensor_scalar(wbuf[:, i * 8:i * 8 + 7], msk[:, 0:7], rec[:],
                                   None, ALU.mult))


def emit_stage4_experts(E):
    nc = E.nc
    NC, NOWN, SOWN, G, NB, QB, NHC = shapes()
    hT = E.bufs["hT"]
    resid = E.bufs["resid"]
    gu = E.bufs["s4gu"]
    wgu, wd = E.bufs["s4wgu"], E.bufs["s4wd"]
    gs = [E.bufs["s4gs0"], E.bufs["s4gs1"]]
    zsb = [E.bufs["s4zsb"], E.bufs["s4zsb1"]]
    pA = [E.bufs["ps_a"], E.bufs["s4pz0"][:, 0:QB]]
    pB = [E.bufs["ps_b"], E.bufs["s4pz1"]]
    pz = [E.bufs["ps_a"], E.bufs["ps_b"]]
    ptr = [E.bufs["s4ptrp"][:, 0:QB], E.bufs["lnptr0"][:, 0:QB]]
    wg3 = E.dram["wg"].rearrange("e (c p) n -> e p c n", p=128)
    wu3 = E.dram["wu"].rearrange("e (c p) n -> e p c n", p=128)
    wd3 = E.dram["wd"].rearrange("e (c p) n -> e p c n", p=128)
    wbuf = E.bufs["s4wbuf"]
    halves = [list(range(NB))] if NB == 1 else [list(range(NB // 2)),
                                               list(range(NB // 2, NB))]
    gtok = {}
    up_mm_hist = []
    dn_mm_hist = []
    first = True

    for e in range(NEXP + 1):
      for hf, tts in enumerate(halves):
        NT = len(tts)
        base_T = E.n("eT")
        base_S = E.n("eS")
        base_V = E.n("eV")
        nK = NHC * NT
        silu_cnt = lambda k: base_S + k + 1
        gu_cnt = lambda k: base_V + k + 1
        tmm = lambda k: base_T + k + 1

        @E.block.gpsimd
        def _(g):
            for hc in range(NHC):
                sl = hc % 2
                if hc < 2 and up_mm_hist:
                    g.wait_ge(E.sem("eT"), up_mm_hist[-1])
                if hc >= 2:
                    g.wait_ge(E.sem("eT"), base_T + (hc - 2) * NT + NT)
                gtok[(e, hf, hc, 0)] = E.qG.issue(
                    g, r3(wgu[:], 4 * 8)[:, sl * 16:sl * 16 + 8, :],
                    wg3[e][:, :, hc * 128:(hc + 1) * 128])
                gtok[(e, hf, hc, 1)] = E.qG.issue(
                    g, r3(wgu[:], 4 * 8)[:, sl * 16 + 8:sl * 16 + 16, :],
                    wu3[e][:, :, hc * 128:(hc + 1) * 128])

        @E.block.tensor
        def _(te):
            if first:
                te.wait_ge(E.sem("lnev"), E.n("lnev"))
            for hc in range(NHC):
                sl = hc % 2
                for lt, tt in enumerate(tts):
                    k = hc * NT + lt
                    jj = k % 2
                    if lt == 0:
                        E.qG.wait(te, gtok[(e, hf, hc, 0)])
                        E.qG.wait(te, gtok[(e, hf, hc, 1)])
                    if k >= 2:
                        te.wait_ge(E.sem("eS"), silu_cnt(k - 2))
                        te.wait_ge(E.sem("eV"), gu_cnt(k - 2))
                    elif e + hf > 0:
                        te.wait_ge(E.sem("eV"), base_V)
                    for ic in range(8):
                        te.matmul(pA[jj][:, 0:QB],
                                  r3(wgu[:], 4 * 8)[:, sl * 16 + ic, :],
                                  r3(hT[:], 8)[:, ic, tt * QB:(tt + 1) * QB],
                                  start=(ic == 0), stop=(ic == 7))
                    for ic in range(8):
                        ins = te.matmul(pB[jj][:, 0:QB],
                                        r3(wgu[:], 4 * 8)[:, sl * 16 + 8 + ic, :],
                                        r3(hT[:], 8)[:, ic, tt * QB:(tt + 1) * QB],
                                        start=(ic == 0), stop=(ic == 7))
                    E.tinc(ins)

        @E.block.scalar
        def _(sc):
            for hc in range(NHC):
                for lt in range(NT):
                    k = hc * NT + lt
                    jj = k % 2
                    sc.wait_ge(E.sem("eT"), tmm(k))
                    if k >= 2:
                        sc.wait_ge(E.sem("eV"), gu_cnt(k - 2))
                    E.sinc(sc.activation(gs[jj][:, 0:QB], pA[jj][:, 0:QB], AF.Silu))

        @E.block.vector
        def _(v):
            for hc in range(NHC):
                for lt in range(NT):
                    k = hc * NT + lt
                    jj = k % 2
                    v.wait_ge(E.sem("eS"), silu_cnt(k))
                    E.vinc(v.tensor_tensor(
                        r3(gu[:], NHC)[:, hc, lt * QB:(lt + 1) * QB],
                        gs[jj][:, 0:QB], pB[jj][:, 0:QB], ALU.mult))
        up_mm_hist.append(E.n("eT"))
        first = False

        base_T = E.n("eT")
        base_V = E.n("eV")
        gu_final = base_V
        nDK = 8 * NT
        dn_mm = lambda k: base_T + 2 * k + 1 - (1 if k == 0 else 0)
        t_cnt = [0] * (nDK + 1)
        cnt = base_T
        mm_cnt = [0] * nDK
        tr_cnt = [0] * nDK
        for k in range(nDK):
            cnt += 1
            mm_cnt[k] = cnt
            if k >= 1:
                cnt += 1
                tr_cnt[k - 1] = cnt
        cnt += 1
        tr_cnt[nDK - 1] = cnt
        evac_cnt = [0] * nDK
        vcnt = base_V
        for k in range(nDK):
            vcnt += 1
            evac_cnt[k] = vcnt
            if k >= 1:
                vcnt += 1
        add_last = vcnt + 1

        @E.block.gpsimd
        def _(g):
            for oc in range(8):
                slo = oc % 2
                if oc < 2 and dn_mm_hist:
                    g.wait_ge(E.sem("eT"), dn_mm_hist[-1])
                if oc >= 2:
                    g.wait_ge(E.sem("eT"), mm_cnt[(oc - 2) * NT + NT - 1])
                gtok[("d", e, hf, oc)] = E.qG.issue(
                    g, r3(wd[:], 2 * NHC)[:, slo * NHC:slo * NHC + NHC, :],
                    wd3[e][:, :, oc * 128:(oc + 1) * 128])

        @E.block.tensor
        def _(te):
            for oc in range(8):
                slo = oc % 2
                for lt, tt in enumerate(tts):
                    k = oc * NT + lt
                    jj = k % 2
                    if lt == 0:
                        E.qG.wait(te, gtok[("d", e, hf, oc)])
                        if oc == 0:
                            te.wait_ge(E.sem("eV"), gu_final)
                    if k >= 2:
                        te.wait_ge(E.sem("eV"), evac_cnt[k - 2])
                    for hc in range(NHC):
                        ins = te.matmul(pz[jj][:, 0:QB],
                                        r3(wd[:], 2 * NHC)[:, slo * NHC + hc, :],
                                        r3(gu[:], NHC)[:, hc, lt * QB:(lt + 1) * QB],
                                        start=(hc == 0), stop=(hc == NHC - 1))
                    E.tinc(ins)
                    assert E.n("eT") == mm_cnt[k]
                    if k >= 1:
                        j2 = (k - 1) % 2
                        te.wait_ge(E.sem("eV"), evac_cnt[k - 1])
                        for q in range(G):
                            ins = te.transpose(ptr[j2][:, q * 128:(q + 1) * 128],
                                               zsb[j2][:, q * 128:(q + 1) * 128],
                                               E.bufs["idnb"][:])
                        E.tinc(ins)
            j2 = (nDK - 1) % 2
            te.wait_ge(E.sem("eV"), evac_cnt[nDK - 1])
            for q in range(G):
                ins = te.transpose(ptr[j2][:, q * 128:(q + 1) * 128],
                                   zsb[j2][:, q * 128:(q + 1) * 128],
                                   E.bufs["idnb"][:])
            E.tinc(ins)
            assert E.n("eT") == tr_cnt[nDK - 1]

        @E.block.vector
        def _(v):
            def adds(k):
                j2 = k % 2
                oc = k // NT
                tt = tts[k % NT]
                v.wait_ge(E.sem("eT"), tr_cnt[k])
                for q in range(G):
                    ci = G * tt + q
                    dst = resid[:, ci * D + oc * 128: ci * D + oc * 128 + 128]
                    if e == NEXP:
                        ins = v.tensor_add(dst, dst, ptr[j2][:, q * 128:(q + 1) * 128])
                    else:
                        ins = v.scalar_tensor_tensor(
                            dst, ptr[j2][:, q * 128:(q + 1) * 128],
                            wbuf[:, ci * 8 + e:ci * 8 + e + 1],
                            dst, ALU.mult, ALU.add)
                E.vinc(ins)

            for k in range(nDK):
                jj = k % 2
                v.wait_ge(E.sem("eT"), mm_cnt[k])
                E.vinc(v.tensor_copy(zsb[jj][:, 0:QB], pz[jj][:, 0:QB]))
                assert E.n("eV") == evac_cnt[k]
                if k >= 1:
                    adds(k - 1)
            adds(nDK - 1)
            assert E.n("eV") == add_last
        dn_mm_hist.append(E.n("eT"))


def build_prog4():
    NC, NOWN, SOWN, G, NB, QB, NHC = shapes()
    specs = [("xin", (SOWN, D), F32, "ExternalInput"),
             ("wg", (NEXP + 1, D, HID), F32, "ExternalInput"),
             ("wu", (NEXP + 1, D, HID), F32, "ExternalInput"),
             ("wd", (NEXP + 1, HID, D), F32, "ExternalInput"),
             ("wr", (D, 8), F32, "ExternalInput"),
             ("rbias", (1, 7), F32, "ExternalInput"),
             ("x4", (SOWN, D), F32, "ExternalOutput")]
    specs += const_specs(("idn", "ones"))
    nc, dram = new_prog(specs)
    with ExitStack() as ctx:
        block = ctx.enter_context(nc.Block())
        E = Env(nc, ctx, block)
        E.dram = dram
        E.sb("resid", [128, NOWN * D], F32)
        E.sb("hT", [128, 8 * SOWN], BF16)
        alloc_ln_bufs(E)
        alloc_stage4_bufs(E)
        emit_load_constants(E, want=("idn", "ones"))
        E.barrier()
        emit_ln(E, [(i * 128, i, i * 128) for i in range(NOWN)], SOWN,
                src=dram["xin"])
        emit_stage4_router(E)
        E.barrier()
        emit_stage4_experts(E)
        E.barrier()
        emit_writeback(E, "x4", waits=("eV",))
    return nc


def emit_writeback(E, name, waits=()):
    NC, NOWN, SOWN, G, NB, QB, NHC = shapes()
    resid = E.bufs["resid"]

    @E.block.sync
    def _(sy):
        for s in waits:
            sy.wait_ge(E.sem(s), E.n(s))
        for i in range(NOWN):
            E.qS.issue(sy, E.dram[name][i * 128:(i + 1) * 128, :],
                       resid[:, i * D:(i + 1) * D])
        E.qS.wait_all(sy)


def new_prog(dram_specs):
    nc = bass.Bass("TRN2", target_bir_lowering=False)
    dram = {}
    for nm, shp, dt, kind in dram_specs:
        dram[nm] = nc.dram_tensor(nm, list(shp), dt, kind=kind).ap()
    return nc, dram


def const_specs(names):
    cs = host_constants(0)
    return [(k, cs[k].shape, F32, "ExternalInput") for k in names]


def build_prog2():
    NC, NOWN, SOWN, G, NB, QB, NHC = shapes()
    specs = [("xin", (SOWN, D), F32, "ExternalInput"),
             ("w0", (D, D), F32, "ExternalInput"),
             ("x2", (SOWN, D), F32, "ExternalOutput")]
    specs += const_specs(("idn", "ones"))
    nc, dram = new_prog(specs)
    with ExitStack() as ctx:
        block = ctx.enter_context(nc.Block())
        E = Env(nc, ctx, block)
        E.dram = dram
        E.sb("resid", [128, NOWN * D], F32)
        E.sb("hT", [128, 8 * SOWN], BF16)
        alloc_ln_bufs(E)
        emit_load_constants(E, want=("idn", "ones"))
        E.barrier()
        emit_ln(E, [(i * 128, i, i * 128) for i in range(NOWN)], SOWN,
                src=dram["xin"])
        emit_stage2(E)
        emit_writeback(E, "x2", waits=("s2zad",))
    return nc


U32 = mybir.dt.uint32


def gather_index(parity):
    NC, NOWN, SOWN, G, NB, QB, NHC = shapes()
    gi = np.zeros((128, NC), np.uint32)
    qi = np.arange(128, dtype=np.uint32)
    for g in range(NC):
        if g < NOWN:
            gi[:, g] = parity * SOWN + g * 128 + qi
        else:
            gi[:, g] = (1 - parity) * SOWN + (g - NOWN) * 128 + qi
    return gi


def build_prog_full():
    NC, NOWN, SOWN, G, NB, QB, NHC = shapes()
    specs = [("x", (S, D), F32, "ExternalInput"),
             ("wq", (D, D), F32, "ExternalInput"),
             ("wk", (D, D), F32, "ExternalInput"),
             ("wv", (D, D), F32, "ExternalInput"),
             ("wo", (D, D), F32, "ExternalInput"),
             ("w0", (D, D), F32, "ExternalInput"),
             ("rwq", (128, H * 64), F32, "ExternalInput"),
             ("rwk", (128, H * 64), F32, "ExternalInput"),
             ("rwv", (128, H * 64), F32, "ExternalInput"),
             ("w1", (D, D), F32, "ExternalInput"),
             ("w2", (D, D), F32, "ExternalInput"),
             ("gns", (64, H), F32, "ExternalInput"),
             ("gnb", (64, H), F32, "ExternalInput"),
             ("wg", (NEXP + 1, D, HID), F32, "ExternalInput"),
             ("wu", (NEXP + 1, D, HID), F32, "ExternalInput"),
             ("wd", (NEXP + 1, HID, D), F32, "ExternalInput"),
             ("wr", (D, 8), F32, "ExternalInput"),
             ("rbias", (1, 7), F32, "ExternalInput"),
             ("gidx", (128, NC), U32, "ExternalInput"),
             ("y", (SOWN, D), F32, "ExternalOutput")]
    specs += const_specs(("idn", "ones", "perm", "cost", "sint", "dmask",
                          "qfac", "kfct"))
    nc, dram = new_prog(specs)
    cx2 = nc.dram_tensor("cx2", [SOWN, D], F32, kind="Internal").ap()
    gx2 = nc.dram_tensor("gx2", [S, D], F32, kind="Internal").ap()
    with ExitStack() as ctx:
        block = ctx.enter_context(nc.Block())
        E = Env(nc, ctx, block)
        E.dram = dram
        alloc_ln_bufs(E)
        emit_load_constants(E, want=("idn", "ones", "dmask"))
        E.sb("gidxb", [128, NC], U32)

        @E.block.gpsimd
        def _(g):
            E.qG.issue(g, E.bufs["gidxb"][:], dram["gidx"])

        with ExitStack() as epochA:
            with ExitStack() as s1o:
                E.ctx = s1o
                E.sb("attnT", [128, 8 * SOWN], BF16)
                emit_load_constants(E, want=("perm", "cost", "sint"))
                with ExitStack() as s1i:
                    E.ctx = s1i
                    E.sb("hT1", [128, 8 * S], BF16, alias="hT")
                    alloc_stage1_bufs(E)
                    E.ctx = s1o
                    E.barrier()
                    with nc.named_scope("s1ln"):
                        emit_ln(E, [(g * 128, None, g * 128) for g in range(NC)], S,
                                src=dram["x"])
                    with nc.named_scope("s1attn"):
                        emit_stage1(E)
                    E.barrier()
                E.ctx = epochA
                E.sb("residA", [128, NOWN * D], F32, alias="resid", side="right")
                E.ctx = s1o
                with nc.named_scope("s1wo"):
                    emit_stage1_wo(E)
                E.barrier()
                E.ctx = ctx

            with ExitStack() as s2o:
                E.ctx = s2o
                E.sb("hT2", [128, 8 * SOWN], BF16, alias="hT")
                with nc.named_scope("s2"):
                    emit_ln(E, [(None, i, i * 128) for i in range(NOWN)], SOWN,
                            pre_wait=("s1zad",))
                    emit_stage2(E)
                E.ctx = ctx

            _xscope = nc.enter_named_scope("xchg", False)

            @E.block.sync
            def _(sy):
                sy.wait_ge(E.sem("eV"), E.n("eV"))
                sy.wait_ge(E.sem("eS"), E.n("eS"))
                sy.wait_ge(E.sem("eT"), E.n("eT"))
                sy.wait_ge(E.sem("s2zad"), E.n("s2zad"))
                for i in range(NOWN):
                    E.qS.issue(sy, cx2[i * 128:(i + 1) * 128, :],
                               E.bufs["resid"][:, i * D:(i + 1) * D])

            @E.block.gpsimd
            def _(g):
                E.qS.wait_all(g)
                nc.enter_named_scope("xchg") if False else None
                g.collective_compute(
                    "AllGather", ALU.bypass,
                    replica_groups=[[0, 1], [2, 3], [4, 5], [6, 7]],
                    ins=[cx2.opt()], outs=[gx2.opt()]).then_inc(E.sem("cc"))
                E.counts["cc"] += 1
                g.wait_ge(E.sem("cc"), E.counts["cc"])
            E.barrier()
            nc.leave_named_scope("xchg", _xscope[0], False)

        with ExitStack() as epochB:
            with ExitStack() as s3o:
                E.ctx = s3o
                E.sb("hT3", [128, 8 * S], BF16, alias="hT")
                E.sb("s3norm", [128, 8 * SOWN], BF16)
                E.ps("s3psa", [128, min(512, S)], F32, alias="ps_a")
                E.ps("s3psb", [128, min(512, S)], F32, alias="ps_b")
                with ExitStack() as s3i:
                    E.ctx = s3i
                    alloc_stage3_bufs(E)
                    E.ctx = s3o

                    @E.block.gpsimd
                    def _(g):
                        for nm in ("rwq", "rwk", "rwv"):
                            E.qG.issue(g, E.bufs["s3" + nm][:], E.dram[nm])
                        E.qG.issue(g, E.bufs["s3gns"][:], E.dram["gns"])
                        E.qG.issue(g, E.bufs["s3gnb"][:], E.dram["gnb"])
                    with nc.named_scope("s3ln"):
                        emit_ln(E, [(g * 128, None, g * 128) for g in range(NC)], S,
                                src=("indirect", gx2, E.bufs["gidxb"]))
                    with nc.named_scope("s3ret"):
                        emit_stage3(E)
                    E.barrier()
                E.ctx = epochB
                E.sb("residB", [128, NOWN * D], F32, alias="resid", side="right")
                E.ctx = s3o

                @E.block.sync
                def _(sy):
                    sy.wait_ge(E.sem("eS"), E.n("eS"))
                    sy.wait_ge(E.sem("eV"), E.n("eV"))
                    sy.wait_ge(E.sem("eT"), E.n("eT"))
                    for i in range(NOWN):
                        E.qS.issue(sy, E.bufs["resid"][:, i * D:(i + 1) * D],
                                   cx2[i * 128:(i + 1) * 128, :])
                E.barrier()
                with nc.named_scope("s3out"):
                    emit_stage3_out(E)
                E.barrier()
                E.ctx = ctx

            with ExitStack() as s4o:
                E.ctx = s4o
                E.sb("hT4", [128, 8 * SOWN], BF16, alias="hT")
                alloc_stage4_bufs(E)
                with nc.named_scope("s4rt"):
                    emit_ln(E, [(None, i, i * 128) for i in range(NOWN)], SOWN,
                            pre_wait=("s3zad",))
                    emit_stage4_router(E)
                E.barrier()
                with nc.named_scope("s4moe"):
                    emit_stage4_experts(E)
                E.barrier()
                E.ctx = ctx
            emit_writeback(E, "y", waits=("eV",))
    return nc


def kernel_fused(**inputs):
    NC, NOWN, SOWN, G, NB, QB, NHC = shapes()
    w = _prep_host(inputs)
    nc = build_prog_full()
    maps = []
    for c in range(NCORES):
        p = c & 1
        m = dict(host_constants(p))
        m["gidx"] = gather_index(p)
        m["x"] = np.ascontiguousarray(w["x"][c // 2][packed_positions(p)])
        for k in ("wq", "wk", "wv", "wo", "w0", "rwq", "rwk", "rwv", "w1", "w2",
                  "gns", "gnb", "wg", "wu", "wd", "wr", "rbias"):
            m[k] = w[k]
        maps.append(m)
    res = run_bass_kernel_spmd(nc, maps, core_ids=list(range(NCORES)))
    out = np.empty((B, S, D), np.float32)
    for c in range(NCORES):
        out[c // 2][packed_positions(c & 1)[:SOWN]] = res.results[c]["y"]
    return out


def _prep_host(inputs):
    w = {}
    w["x"] = np.asarray(inputs["L0_attn_x"])
    w["wq"] = np.ascontiguousarray(np.asarray(inputs["L0_attn_Wq"]).transpose(1, 0, 2).reshape(D, D))
    w["wk"] = np.ascontiguousarray(np.asarray(inputs["L0_attn_Wk"]).transpose(1, 0, 2).reshape(D, D))
    w["wv"] = np.ascontiguousarray(np.asarray(inputs["L0_attn_Wv"]).transpose(1, 0, 2).reshape(D, D))
    w["wo"] = np.ascontiguousarray(np.asarray(inputs["L0_attn_Wo"]))
    w["w0"] = np.ascontiguousarray(np.asarray(inputs["L0_ffn_weight"]))
    for nm, key in (("rwq", "L1_attn_wq"), ("rwk", "L1_attn_wk"), ("rwv", "L1_attn_wv")):
        a = np.asarray(inputs[key]).transpose(1, 0, 2).reshape(64, H * 64)
        w[nm] = np.ascontiguousarray(np.tile(a, (2, 1)))
    w["w1"] = np.ascontiguousarray(np.asarray(inputs["L1_attn_w1"]))
    w["w2"] = np.ascontiguousarray(np.asarray(inputs["L1_attn_w2"]))
    w["gns"] = np.ascontiguousarray(np.asarray(inputs["L1_attn_gn_scale"]).reshape(H, 64).T)
    w["gnb"] = np.ascontiguousarray(np.asarray(inputs["L1_attn_gn_bias"]).reshape(H, 64).T)
    w["wg"] = np.ascontiguousarray(np.concatenate(
        [np.asarray(inputs["L1_ffn_routed_gate"]),
         np.asarray(inputs["L1_ffn_shared_gate"])[None]], 0))
    w["wu"] = np.ascontiguousarray(np.concatenate(
        [np.asarray(inputs["L1_ffn_routed_up"]),
         np.asarray(inputs["L1_ffn_shared_up"])[None]], 0))
    w["wd"] = np.ascontiguousarray(np.concatenate(
        [np.asarray(inputs["L1_ffn_routed_down"]),
         np.asarray(inputs["L1_ffn_shared_down"])[None]], 0))
    wr = np.asarray(inputs["L1_ffn_Wr"])
    wrp = np.zeros((D, 8), np.float32)
    wrp[:, 0:7] = wr
    w["wr"] = wrp
    w["rbias"] = np.ascontiguousarray(np.asarray(inputs["L1_ffn_routing_bias"]).reshape(1, 7))
    return w


def kernel(**inputs):
    return kernel_fused(**inputs)


def kernel_multilaunch(**inputs):
    NC, NOWN, SOWN, G, NB, QB, NHC = shapes()
    w = _prep_host(inputs)
    consts = [host_constants(c & 1) for c in range(NCORES)]
    packs = [packed_positions(c & 1) for c in range(NCORES)]

    def cmaps(names, extra):
        maps = []
        for c in range(NCORES):
            m = {k: consts[c][k] for k in names}
            m.update(extra(c))
            maps.append(m)
        return maps

    nc1 = build_prog1()
    maps1 = cmaps(("idn", "ones", "perm", "cost", "sint", "dmask"), lambda c: {
        "x": np.ascontiguousarray(w["x"][c // 2][packs[c]]),
        "wq": w["wq"], "wk": w["wk"], "wv": w["wv"], "wo": w["wo"]})
    res1 = run_bass_kernel_spmd(nc1, maps1, core_ids=list(range(NCORES)))
    x1own = [res1.results[c]["x1"] for c in range(NCORES)]

    nc2 = build_prog2()
    maps2 = cmaps(("idn", "ones"), lambda c: {"xin": x1own[c], "w0": w["w0"]})
    res2 = run_bass_kernel_spmd(nc2, maps2, core_ids=list(range(NCORES)))
    x2own = [res2.results[c]["x2"] for c in range(NCORES)]

    nc3 = build_prog3()
    maps3 = cmaps(("idn", "ones", "dmask", "qfac", "kfct"), lambda c: {
        "x2f": np.ascontiguousarray(np.concatenate([x2own[c], x2own[c ^ 1]], 0)),
        "rwq": w["rwq"], "rwk": w["rwk"], "rwv": w["rwv"],
        "w1": w["w1"], "w2": w["w2"], "gns": w["gns"], "gnb": w["gnb"]})
    res3 = run_bass_kernel_spmd(nc3, maps3, core_ids=list(range(NCORES)))
    x3own = [res3.results[c]["x3"] for c in range(NCORES)]

    nc4 = build_prog4()
    maps4 = cmaps(("idn", "ones"), lambda c: {
        "xin": x3own[c], "wg": w["wg"], "wu": w["wu"], "wd": w["wd"],
        "wr": w["wr"], "rbias": w["rbias"]})
    res4 = run_bass_kernel_spmd(nc4, maps4, core_ids=list(range(NCORES)))

    out = np.empty((B, S, D), np.float32)
    for c in range(NCORES):
        out[c // 2][packs[c][:SOWN]] = res4.results[c]["x4"]
    return out
